# version 21
# baseline (speedup 1.0000x reference)
import numpy as np
import concourse.bass as bass
import concourse.mybir as mybir
from concourse.bass_utils import run_bass_kernel_spmd
ds = bass.ds

F32 = mybir.dt.float32
BF16 = mybir.dt.bfloat16
Act = mybir.ActivationFunctionType
Alu = mybir.AluOpType
AXX = mybir.AxisListType.X

NCHUNK = 16
Q = 256
D = 1024
H = 16
HN = 64
EPS_LN = 1e-5
EPS_GN = 1e-5 * 64

WANT_TAPS = {}
LIMITS = None
STAGES = None

ENGS = ("pe", "dve", "act", "pool", "sp")
ENG_SEM = {"pe": "s_pe", "dve": "s_dv", "act": "s_ac", "pool": "s_po", "sp": "s_sp"}


def raw_matmul(t, nc, out, lhsT, rhs, start, stop, tile_position=None, ldw=True):
    if ldw:
        return t.matmul(out, lhsT, rhs, start=start, stop=stop,
                        tile_position=tile_position)
    m = mybir.InstMatmult(
        name=f"I-{nc.next_id()}",
        ins=[t.lower_ap(rhs), t.lower_ap(lhsT, for_matmul_weights=True)],
        outs=[t.lower_ap(out)],
        replication_resolution=0, replication_shift_amnt=0, replication_num_rows=0,
        start_tensor_calc=start, stop_tensor_calc=stop,
        tile_size=(lhsT.shape[0], lhsT.shape[-1]),
        tile_position=tile_position if tile_position is not None else (0, 0),
        ldweights=False,
    )
    return t.add_instruction(m)


class Sched:

    NDMA_LANES = 4

    def __init__(self, nc):
        self.nc = nc
        self.ops = {e: [] for e in ENGS}
        self.sems = {}
        self.counts = {}
        self.bufs = {}
        self.dma_rr = {e: 0 for e in ENGS}

    def sem(self, name):
        if name not in self.sems:
            self.sems[name] = self.nc.alloc_semaphore(name)
            self.counts[name] = 0
        return self.sems[name]

    def _buf(self, name):
        if name not in self.bufs:
            self.bufs[name] = {"w": None, "r": []}
        return self.bufs[name]

    def emit(self, eng, fn, reads=(), writes=(), extra_waits=(), dma=False, tag=""):
        if dma:
            lane = self.dma_rr[eng] % self.NDMA_LANES
            self.dma_rr[eng] += 1
            inc_sem = f"s_dma_{eng}{lane}"
        else:
            inc_sem = ENG_SEM[eng]
        self.sem(inc_sem)
        waits = list(extra_waits)
        if dma and self.counts[inc_sem] > 0:
            waits.append((inc_sem, self.counts[inc_sem]))
        def need(dep_eng, tok):
            if dep_eng == "pe" and eng == "pe" and not self._is_dma_tok(tok):
                return False
            return True
        PSUM_NAMES = ("pp0", "pp1", "yps", "sp0", "sp1", "su0", "su1")
        for bn in reads:
            b = self._buf(bn)
            if b["w"] is not None:
                weng, wtok = b["w"]
                if need(weng, wtok):
                    waits.append(wtok)
            if bn.split("_acc")[0] in PSUM_NAMES and eng in ("dve", "act"):
                other = "act" if eng == "dve" else "dve"
                for reng, rtok in b["r"]:
                    if reng == other:
                        waits.append(rtok)
        for bn in writes:
            b = self._buf(bn)
            if b["w"] is not None:
                weng, wtok = b["w"]
                if need(weng, wtok):
                    waits.append(wtok)
            for reng, rtok in b["r"]:
                if need(reng, rtok):
                    waits.append(rtok)
        wmap = {}
        for (sname, v) in waits:
            assert v <= self.counts[sname], f"wait {sname}>={v} not yet emitted (tag={tag})"
            wmap[sname] = max(wmap.get(sname, 0), v)
        amt = 16 if dma else 1
        self.counts[inc_sem] += amt
        tok = (inc_sem, self.counts[inc_sem])
        self.ops[eng].append((fn, tuple(sorted(wmap.items())), inc_sem, amt, tag))
        for bn in reads:
            self._buf(bn)["r"].append((eng, tok))
        for bn in writes:
            b = self._buf(bn)
            b["w"] = (eng, tok)
            b["r"] = []
        return tok

    @staticmethod
    def _is_dma_tok(tok):
        return tok[0].startswith("s_dma_")

    def barrier(self):
        return [(s, c) for s, c in self.counts.items() if c > 0]

    def build(self):
        nc = self.nc
        with nc.Block() as block:
            def mk(eng_key):
                def body(eng):
                    last = {}
                    for (fn, waits, inc, amt, tag) in self.ops[eng_key]:
                        for (s, v) in waits:
                            if last.get(s, -1) >= v:
                                continue
                            eng.wait_ge(self.sems[s], v)
                            last[s] = v
                        ins = fn(eng)
                        ins.then_inc(self.sems[inc], amt)
                return body
            block.tensor(mk("pe"))
            block.vector(mk("dve"))
            block.scalar(mk("act"))
            block.gpsimd(mk("pool"))
            block.sync(mk("sp"))


def build_nc(debug_taps=None):
    taps = dict(WANT_TAPS)
    if debug_taps:
        taps.update(debug_taps)
    nc = bass.Bass("TRN2", target_bir_lowering=False)
    s = Sched(nc)

    xo = nc.dram_tensor("xo", [4096, D], F32, kind="ExternalInput").ap()
    hmask_in = nc.dram_tensor("hmask", [128, 1], F32, kind="ExternalInput").ap()
    xprow_in = nc.dram_tensor("xprow", [1, D], F32, kind="ExternalInput").ap()
    xpcol_in = nc.dram_tensor("xpcol", [128, 8], F32, kind="ExternalInput").ap()
    k_dram = nc.dram_tensor("k_spill", [NCHUNK, 128, 8, Q], BF16, kind="Internal").ap()
    v_dram = nc.dram_tensor("v_spill", [NCHUNK, 128, 2, D], BF16, kind="Internal").ap()
    kw_dram = nc.dram_tensor("kw_spill", [NCHUNK, 128, 8, 2, 128], BF16, kind="Internal").ap()
    l1_dram = nc.dram_tensor("l1_spill", [NCHUNK, 128, 8, Q], BF16, kind="Internal").ap()
    my1_dram = nc.dram_tensor("my1", [128, 512], F32, kind="Internal")
    g1_dram = nc.dram_tensor("g1", [8, 128, 512], F32, kind="Internal")
    my2_dram = nc.dram_tensor("my2", [128, 8], F32, kind="Internal")
    g2_dram = nc.dram_tensor("g2", [8, 128, 8], F32, kind="Internal")
    w_names = ["wrT", "wkT", "wvT", "wgT", "woT", "fwrT", "fwkT", "fwvT"]
    w_dram = {n: nc.dram_tensor(n, [D, D], F32, kind="ExternalInput").ap() for n in w_names}
    maa_in = nc.dram_tensor("maa", [128, 8, 4], F32, kind="ExternalInput").ap()
    lnw_in = nc.dram_tensor("lnw", [128, 8, 2], F32, kind="ExternalInput").ap()
    mus_in = nc.dram_tensor("mus", [128, 8, 2], F32, kind="ExternalInput").ap()
    dec_rep_in = nc.dram_tensor("dec_rep", [128, 8], F32, kind="ExternalInput").ap()
    dec_full_in = nc.dram_tensor("dec_full", [128, 16], F32, kind="ExternalInput").ap()
    u_full_in = nc.dram_tensor("u_full", [128, 16], F32, kind="ExternalInput").ap()
    out_d = nc.dram_tensor("out", [4096, D], F32, kind="ExternalOutput").ap()
    x1_dram = nc.dram_tensor("x1_scratch", [4096, D], F32, kind="Internal").ap()

    jj = np.arange(128)[:, None]
    ii = np.arange(128)[None, :]
    EA_np = np.where(jj < ii, ii - jj - 1, -1.0).astype(np.float32)
    MK_np = (jj < ii).astype(np.float32)
    EB_np = (ii + 128 - jj - 1).astype(np.float32)
    EYE_np = np.eye(128, dtype=np.float32)
    IOTA_np = np.tile(np.arange(Q, dtype=np.float32), (128, 1))
    ONES1_np = np.ones((1, 128), dtype=np.float32)
    EA_t = nc.inline_tensor(EA_np, name="EA").ap()
    MKm_t = nc.inline_tensor(MK_np, name="MKm").ap()
    EB_t = nc.inline_tensor(EB_np, name="EB").ap()
    EYE_t = nc.inline_tensor(EYE_np, name="EYE").ap()
    IOTA_t = nc.inline_tensor(IOTA_np, name="IOTA").ap()
    ONES1_t = nc.inline_tensor(ONES1_np, name="ONES1").ap()

    A = nc.alloc_sbuf_tensor
    warena = A("warena", [128, 5, 8, D], BF16)
    SLOT = {"wkT": 0, "wvT": 1, "wrT": 2, "wgT": 3, "woT": 4,
            "fwrT": 2, "fwkT": 3, "fwvT": 4}
    WBUF = {"wkT": "wsl0", "wvT": "wsl1", "wrT": "wsl2", "wgT": "wsl3", "woT": "wsl4",
            "fwrT": "wsl2", "fwkT": "wsl3", "fwvT": "wsl4"}

    y_sb2 = None
    colA = A("colA", [128, 8, 4], F32)
    colB = A("colB", [128, 8, 2], F32)
    colC = A("colC", [128, 8, 2], F32)
    dcr = A("dcr", [128, 8], F32)
    dcf = A("dcf", [128, 16], F32)
    ufl = A("ufl", [128, 16], F32)
    logw_r = A("logw_r", [128, 8], F32)
    logw_f = A("logw_f", [128, 16], F32)
    ws_rep = A("ws_rep", [128, 8], F32)
    tsc = A("tsc", [128, Q], F32)
    tsc2 = A("tsc2", [128, 128], F32)
    wmat = A("wmat", [128, H, Q], BF16)
    m_fm = A("m_fm", [128, 8, Q], BF16)
    mk_fm = A("mk_fm", [128, 8, Q], BF16)

    hmask = A("hmask_s", [128, 1], F32)
    bnd_hist = A("bnd_hist", [128, 8, NCHUNK], BF16)
    state_f = A("state_f", [128, 8, HN], F32)
    state_b = A("state_b", [128, 8, HN], BF16)

    xch_b = A("xch_b", [128, 2, 2, D], F32)
    ln1x_tm = A("ln1x_tm", [128, 2, D], BF16)
    ln1x_fm = A("ln1x_fm", [128, 8, Q], BF16)
    bnd1 = A("bnd1", [128, 8, 1], BF16)
    bnd3 = A("bnd3", [128, 8, 1], BF16)
    dfm = A("dfm", [128, 8, Q], BF16)
    mix0 = A("mix0", [128, 8, Q], BF16)
    mix1 = A("mix1", [128, 8, Q], BF16)
    MIXB = [(mix0, "mix0"), (mix1, "mix1")]
    y_sb = A("y_sb", [128, D], F32)
    consts = y_sb.rearrange("p (a b) -> p a b", a=8)[:, 0:6, :]
    iota = y_sb[:, 640:896]
    _sl0 = warena[:, 0]
    _sl1 = warena[:, 1]
    ln1x_fmB = _sl0[:, :, 0:256]
    kwkTB = _sl0[:, :, 256:512].rearrange("p a (b c) -> p a b c", b=2)
    dfmB = _sl0[:, :, 512:768]
    k_fmB = _sl0[:, :, 768:1024]
    v_tmB = _sl1[:, 0:2, :]
    mixB2 = _sl1[:, 2:4, :].rearrange("p a (b c) -> p (a b) c", c=256)
    mixB3 = _sl1[:, 4:6, :].rearrange("p a (b c) -> p (a b) c", c=256)
    r_fmB = _sl1[:, 6:8, :].rearrange("p a (b c) -> p (a b) c", c=256)
    rp_tmB = None
    r_fm = A("r_fm", [128, 8, Q], BF16)
    k_fm = A("k_fm", [128, 8, Q], BF16)
    g_fm = A("g_fm", [128, 8, Q], BF16)
    v_tm = A("v_tm", [128, 2, D], BF16)
    kwk_fm = A("kwk_fm", [128, 8, Q], BF16)
    kwkT = A("kwkT", [128, 8, 2, 128], BF16)
    r_wb = A("r_wb", [128, 8, Q], BF16)
    sbar = A("sbar", [128, H, 384], BF16)
    sqs = A("sqs", [128, D], F32)
    stF = A("stF", [128, 2, 2], F32)
    stFb = A("stFb", [128, 4, 6], F32)
    stFr = A("stFr", [128, 2], F32)
    stFn = A("stFn", [128, 2], F32)
    st_sum = A("st_sum", [128, 16], F32)
    st_sq = A("st_sq", [128, 16], F32)
    st_mu = A("st_mu", [128, 16], F32)
    st_var = A("st_var", [128, 16], F32)
    st_rstd = A("st_rstd", [128, 16], F32)
    st_nmr = A("st_nmr", [128, 16], F32)
    stO6 = A("stO6", [128, 4, 6], F32)
    stOw = A("stOw", [128, 2, 2], F32)
    stOr = A("stOr", [128, 2], F32)
    stOn = A("stOn", [128, 2], F32)
    stFb2 = A("stFb2", [128, 4, 6], F32)
    stF2 = A("stF2", [128, 2, 2], F32)
    stFr2 = A("stFr2", [128, 2], F32)
    stFn2 = A("stFn2", [128, 2], F32)
    stO6b = A("stO6b", [128, 4, 6], F32)
    stOwb = A("stOwb", [128, 2, 2], F32)
    stOrb = A("stOrb", [128, 2], F32)
    stOnb = A("stOnb", [128, 2], F32)
    stP0 = A("stP0", [128, 2, 2], F32)
    yn_tm = A("yn_tm", [128, 2, D], BF16)
    yn_fm = A("yn_fm", [128, 8, Q], BF16)
    rp_tmB_v = yn_fm.rearrange("p a b -> p (a b)").rearrange("p (a b) -> p a b", a=2)
    yg_fm = A("yg_fm", [128, 8, Q], BF16)
    x1s = A("x1s", [128, 2, D], F32)
    ln3x_tm = ln1x_tm
    ln3x_fm = ln1x_fm
    rp_tm = yn_tm
    kact_fm = k_fm
    och = sbar.bitcast(F32).rearrange("p a b -> p (a b)")[:, 0:2048].rearrange(
        "p (a b) -> p a b", a=2)
    ln3x_tmB = kwk_fm.rearrange("p a b -> p (a b)").rearrange(
        "p (t d) -> p t d", t=2)
    yn_tmB = mk_fm.rearrange("p a b -> p (a b)").rearrange(
        "p (t d) -> p t d", t=2)
    kwkTB_P1 = r_wb.rearrange("p a (b c) -> p a b c", b=2)
    v_tmB_P1 = sbar.rearrange("p h q -> p (h q)")[:, 0:2048].rearrange(
        "p (t d) -> p t d", t=2)
    _x1bf = x1s.bitcast(BF16).rearrange("p a b -> p (a b)")
    x1mA = _x1bf[:, 0:2048].rearrange("p (a b) -> p a b", a=8)
    x1mB = _x1bf[:, 2048:4096].rearrange("p (a b) -> p a b", a=8)
    MIXB.append((x1mA, "x1mA"))
    MIXB.append((x1mB, "x1mB"))
    gv_x1 = g_fm.bitcast(F32).rearrange("p a b -> p (a b)")
    ygv_x1 = yg_fm.bitcast(F32).rearrange("p a b -> p (a b)")
    rwv_x1 = r_wb.bitcast(F32).rearrange("p a b -> p (a b)")
    mfv_x1 = m_fm.bitcast(F32).rearrange("p a b -> p (a b)")
    mkv_x1 = mk_fm.bitcast(F32).rearrange("p a b -> p (a b)")
    rfv_x1 = r_fm.bitcast(F32).rearrange("p a b -> p (a b)")
    vtv_x1 = v_tm.bitcast(F32).rearrange("p a b -> p (a b)")
    kTv_x1 = kwkT.bitcast(F32).rearrange("p a b c -> p (a b c)")

    pp0 = nc.alloc_psum_tensor("pp0", [128, 512], F32)
    pp1 = nc.alloc_psum_tensor("pp1", [128, 512], F32)
    yps = nc.alloc_psum_tensor("yps", [128, 1024], F32)
    sp0 = nc.alloc_psum_tensor("sp0", [128, 512], F32)
    sp1 = nc.alloc_psum_tensor("sp1", [128, 512], F32)
    su0 = nc.alloc_psum_tensor("su0", [128, 512], F32)
    su1 = nc.alloc_psum_tensor("su1", [128, 512], F32)
    PP = [pp0, pp1]
    SU = [su0, su1]

    def tap(name, src_ap, shape, reads):
        if name not in taps:
            return
        d = nc.dram_tensor("tap_" + name, list(shape), F32, kind="ExternalOutput").ap()
        s.emit("pool", lambda g, a=src_ap, d=d: g.dma_start(out=d, in_=a),
               reads=list(reads), writes=[], dma=True, tag="tapdma_" + name)

    def dma(eng, out_ap, in_ap, reads, writes, tag, slow=False):
        return s.emit(eng, lambda e, o=out_ap, i=in_ap: e.dma_start(
            out=o, in_=i, allow_slow_non_contiguous=slow),
            reads=reads, writes=writes, dma=True, tag=tag)

    for idx, nm in enumerate(["wkT", "wvT", "wrT", "wgT", "woT"]):
        sl = SLOT[nm]
        dma("pool", warena[:, sl, :, :],
            w_dram[nm].rearrange("(cc p) d -> p cc d", p=128),
            [], [WBUF[nm]], f"ldw_{nm}")
    dma("sp", colA[:], maa_in, [], ["colA"], "ld_maa")
    dma("sp", hmask[:], hmask_in, [], ["hmask"], "ld_hm")
    dma("sp", colB[:], lnw_in, [], ["colB"], "ld_lnw")
    dma("sp", colC[:], mus_in, [], ["colC_raw"], "ld_mus")
    dma("sp", dcr[:], dec_rep_in, [], ["dcr"], "ld_dcr")
    dma("sp", dcf[:], dec_full_in, [], ["dcf"], "ld_dcf")
    dma("sp", ufl[:], u_full_in, [], ["ufl"], "ld_ufl")
    dma("sp", consts[:, 0, :], EA_t, [], ["cEA"], "ld_EA")
    dma("sp", consts[:, 1, :], MKm_t, [], ["cMK"], "ld_MK")
    dma("sp", consts[:, 2, :], EB_t, [], ["cEB"], "ld_EB")
    dma("sp", consts[:, 3, :], EYE_t, [], ["cEYE"], "ld_EYE")
    dma("sp", iota[:], IOTA_t, [], ["iota"], "ld_IOTA")

    s.emit("act", lambda a: a.activation(logw_r[:], dcr[:], Act.Exp),
           reads=["dcr"], writes=["logw_r"], tag="exp_dcr")
    s.emit("dve", lambda v: v.tensor_scalar(logw_r[:], logw_r[:], -1.0, None, Alu.mult),
           reads=["logw_r"], writes=["logw_r"], tag="neg_dcr")
    s.emit("act", lambda a: a.activation(logw_f[:], dcf[:], Act.Exp),
           reads=["dcf"], writes=["logw_f"], tag="exp_dcf")
    s.emit("dve", lambda v: v.tensor_scalar(logw_f[:], logw_f[:], -1.0, None, Alu.mult),
           reads=["logw_f"], writes=["logw_f"], tag="neg_dcf")
    s.emit("act", lambda a: a.activation(ws_rep[:], logw_r[:], Act.Exp, scale=float(Q)),
           reads=["logw_r"], writes=["ws_rep"], tag="ws_rep")
    s.emit("dve", lambda v: v.tensor_scalar(colC[:], colC[:], -1.0, 1.0, Alu.mult, Alu.add),
           reads=["colC_raw"], writes=["colC"], tag="colC")

    for cc in range(8):
        s.emit("dve", lambda v, cc=cc: v.tensor_scalar(
            tsc[:], iota[:], logw_r[:, cc:cc + 1], None, Alu.mult),
            reads=["iota", "logw_r"], writes=["tsc"], tag=f"m_mul{cc}")
        s.emit("act", lambda a, cc=cc: a.activation(m_fm[:, cc, :], tsc[:], Act.Exp),
               reads=["tsc"], writes=["m_fm"], tag=f"m_exp{cc}")
        s.emit("dve", lambda v, cc=cc: v.tensor_scalar(
            tsc[:], iota[:], -1.0, float(Q - 1), Alu.mult, Alu.add),
            reads=["iota"], writes=["tsc"], tag=f"mk_rev{cc}")
        s.emit("dve", lambda v, cc=cc: v.tensor_scalar(
            tsc[:], tsc[:], logw_r[:, cc:cc + 1], None, Alu.mult),
            reads=["tsc", "logw_r"], writes=["tsc"], tag=f"mk_mul{cc}")
        s.emit("act", lambda a, cc=cc: a.activation(mk_fm[:, cc, :], tsc[:], Act.Exp),
               reads=["tsc"], writes=["mk_fm"], tag=f"mk_exp{cc}")

    for h in range(H):
        s.emit("dve", lambda v, h=h: v.tensor_scalar(
            tsc2[:], consts[:, 0, :], logw_f[:, h:h + 1], None, Alu.mult),
            reads=["cEA", "logw_f"], writes=["tsc2"], tag=f"wa1_{h}")
        s.emit("act", lambda a, h=h: a.activation(tsc2[:], tsc2[:], Act.Exp),
               reads=["tsc2"], writes=["tsc2"], tag=f"wa2_{h}")
        s.emit("dve", lambda v, h=h: v.tensor_tensor(
            tsc2[:], tsc2[:], consts[:, 1, :], Alu.mult),
            reads=["tsc2", "cMK"], writes=["tsc2"], tag=f"wa3_{h}")
        s.emit("dve", lambda v, h=h: v.tensor_scalar(
            consts[:, 4, :], consts[:, 3, :], ufl[:, h:h + 1], None, Alu.mult),
            reads=["cEYE", "ufl"], writes=["cU"], tag=f"wa4_{h}")
        s.emit("dve", lambda v, h=h: v.tensor_tensor(
            wmat[:, h, 0:128], tsc2[:], consts[:, 4, :], Alu.add),
            reads=["tsc2", "cU"], writes=["wmat"], tag=f"wa5_{h}")
        s.emit("dve", lambda v, h=h: v.tensor_scalar(
            tsc2[:], consts[:, 2, :], logw_f[:, h:h + 1], None, Alu.mult),
            reads=["cEB", "logw_f"], writes=["tsc2"], tag=f"wb1_{h}")
        s.emit("act", lambda a, h=h: a.activation(wmat[:, h, 128:256], tsc2[:], Act.Exp),
               reads=["tsc2"], writes=["wmat"], tag=f"wb2_{h}")

    s.emit("dve", lambda v: v.memset(state_f[:], 0.0), writes=["state_f"], tag="z_sf")
    s.emit("dve", lambda v: v.memset(state_b[:], 0.0), writes=["state_b"], tag="z_sb")
    s.emit("dve", lambda v: v.memset(bnd1[:], 0.0), writes=["bnd1"], tag="z_b1")
    s.emit("dve", lambda v: v.memset(bnd3[:], 0.0), writes=["bnd3"], tag="z_b3")

    rowbuf = och
    dma("sp", rowbuf[0:1, 0, :], xprow_in, [], ["och"], "br_ldrow")
    dma("sp", stO6.rearrange("p a b -> p (a b)")[:, 0:8], xpcol_in, [], ["stO6"], "br_ldcol")
    dma("sp", rowbuf[0:1, 1, 0:128], ONES1_t, [], ["ones1"], "br_ldone")
    s.emit("dve", lambda v: v.tensor_reduce(
        stP0[0:1, 0, 0:1], rowbuf[0:1, 0, :], AXX, Alu.add),
        reads=["och"], writes=["stP0"], tag="br_sum")
    s.emit("act", lambda a: a.activation(
        rowbuf[0:1, 1, 128:128 + D - 128], rowbuf[0:1, 0, 0:D - 128], Act.Square,
        accum_out=stP0[0:1, 0, 1:2]),
        reads=["och"], writes=["och2", "stP0"], tag="br_sq1")
    s.emit("act", lambda a: a.activation(
        rowbuf[0:1, 0, 0:128], rowbuf[0:1, 0, D - 128:D], Act.Square,
        accum_out=stP0[0:1, 1, 0:1]),
        reads=["och"], writes=["och3", "stP0"], tag="br_sq2")
    s.emit("dve", lambda v: v.tensor_tensor(
        stP0[0:1, 0, 1:2], stP0[0:1, 0, 1:2], stP0[0:1, 1, 0:1], Alu.add),
        reads=["stP0", "och2", "och3"], writes=["stP0"], tag="br_sqs")
    s.emit("dve", lambda v: v.tensor_scalar(
        stP0[0:1, 0, 0:1], stP0[0:1, 0, 0:1], 1.0 / D, None, Alu.mult),
        reads=["stP0"], writes=["stP0"], tag="br_mu")
    s.emit("dve", lambda v: v.tensor_tensor(
        stP0[0:1, 1, 1:2], stP0[0:1, 0, 0:1], stP0[0:1, 0, 0:1], Alu.mult),
        reads=["stP0"], writes=["stP0"], tag="br_mu2")
    s.emit("dve", lambda v: v.tensor_scalar(
        stP0[0:1, 0, 1:2], stP0[0:1, 0, 1:2], 1.0 / D, None, Alu.mult),
        reads=["stP0"], writes=["stP0"], tag="br_v1")
    s.emit("dve", lambda v: v.tensor_tensor(
        stP0[0:1, 0, 1:2], stP0[0:1, 0, 1:2], stP0[0:1, 1, 1:2], Alu.subtract),
        reads=["stP0"], writes=["stP0"], tag="br_v2")
    s.emit("dve", lambda v: v.tensor_scalar(
        stP0[0:1, 0, 1:2], stP0[0:1, 0, 1:2], EPS_LN, None, Alu.add),
        reads=["stP0"], writes=["stP0"], tag="br_v3")
    s.emit("act", lambda a: a.activation(
        stP0[0:1, 1, 0:1], stP0[0:1, 0, 1:2], Act.Sqrt),
        reads=["stP0"], writes=["stP0"], tag="br_sd")
    s.emit("dve", lambda v: v.reciprocal(stP0[0:1, 1, 0:1], stP0[0:1, 1, 0:1]),
           reads=["stP0"], writes=["stP0"], tag="br_rs")
    s.emit("dve", lambda v: v.tensor_scalar(
        stP0[0:1, 1, 1:2], stP0[0:1, 0, 0:1], stP0[0:1, 1, 0:1], -1.0, Alu.mult, Alu.mult),
        reads=["stP0"], writes=["stP0"], tag="br_nm")
    s.emit("pe", lambda t: t.matmul(
        sp0[:, 0:2], rowbuf[0:1, 1, 0:128], stP0[0:1, 1, 0:2],
        start=True, stop=True),
        reads=["ones1", "stP0", "sp0"], writes=["sp0"], tag="br_bc")
    s.emit("dve", lambda v: v.tensor_scalar(
        bnd1.rearrange("p a b -> p (a b)"),
        stO6.rearrange("p a b -> p (a b)")[:, 0:8],
        sp0[:, 0:1], sp0[:, 1:2], Alu.mult, Alu.add),
        reads=["stO6", "sp0", "hmask"], writes=["bnd1"], tag="br_ap")

    def ln_stats_tt(get_src, src_names, G, tt, tag):
        b6, b6n, a2, a2n, w, wn, rr, rrn, nn_, nnn = G
        for g in range(2):
            s.emit("dve", lambda v, tt=tt, g=g: v.bn_stats(
                b6[:, tt * 2 + g, :], get_src(tt, g)),
                reads=src_names, writes=[f"{b6n}.{tt}"], tag=f"{tag}_bs{tt}.{g}")
        s.emit("dve", lambda v, tt=tt: v.bn_aggr(
            w[:, tt, 0:2],
            b6[:, tt * 2:(tt + 1) * 2, :].rearrange("p a b -> p (a b)")),
            reads=[f"{b6n}.{tt}"], writes=[f"{wn}.{tt}"], tag=f"{tag}_ag{tt}")

    def ln_fin(G, tt, tag):
        b6, b6n, a2, a2n, w, wn, rr, rrn, nn_, nnn = G
        s.emit("dve", lambda v, tt=tt: v.tensor_scalar(
            w[:, tt, 1:2], w[:, tt, 1:2], EPS_LN, None, Alu.add),
            reads=[f"{wn}.{tt}"], writes=[f"{wn}.{tt}"], tag=f"{tag}_ep{tt}")
        s.emit("act", lambda a, tt=tt: a.activation(
            rr[:, tt:tt + 1], w[:, tt, 1:2], Act.Sqrt),
            reads=[f"{wn}.{tt}"], writes=[f"{rrn}.{tt}"], tag=f"{tag}_sd{tt}")
        s.emit("dve", lambda v, tt=tt: v.reciprocal(rr[:, tt:tt + 1], rr[:, tt:tt + 1]),
               reads=[f"{rrn}.{tt}"], writes=[f"{rrn}.{tt}"], tag=f"{tag}_rs{tt}")
        s.emit("dve", lambda v, tt=tt: v.tensor_scalar(
            nn_[:, tt:tt + 1], w[:, tt, 0:1], rr[:, tt:tt + 1], -1.0,
            Alu.mult, Alu.mult),
            reads=[f"{wn}.{tt}", f"{rrn}.{tt}"], writes=[f"{nnn}.{tt}"],
            tag=f"{tag}_nm{tt}")

    GF = (stFb, "stFb", None, "", stF, "stF", stFr, "stFr", stFn, "stFn")
    GO = (stO6, "stO6", None, "", stOw, "stOw", stOr, "stOr", stOn, "stOn")
    GFb = (stFb2, "stFb2", None, "", stF2, "stF2", stFr2, "stFr2", stFn2, "stFn2")
    GOb = (stO6b, "stO6b", None, "", stOwb, "stOwb", stOrb, "stOrb", stOnb, "stOnb")

    def ln_stats(buf_f32_ap, src_names, nfeat, tag, G=GF):
        for tt in range(2):
            ln_stats_tt(lambda tt, g: buf_f32_ap[:, tt, g * 512:(g + 1) * 512],
                        src_names, G, tt, tag)
            ln_fin(G, tt, tag)

    def ln_apply(dst_ap, dst_name, src_ap, src_names, tag, G=GF):
        b6, b6n, a2, a2n, w, wn, rr, rrn, nn_, nnn = G
        for tt in range(2):
            s.emit("act", lambda a, tt=tt: a.activation(
                dst_ap[:, tt, :], src_ap[:, tt, :], Act.Identity,
                bias=nn_[:, tt:tt + 1], scale=rr[:, tt:tt + 1]),
                reads=list(src_names) + [f"{rrn}.{tt}", f"{nnn}.{tt}"],
                writes=[dst_name], tag=f"{tag}_ap{tt}")

    def transpose_tm_to_fm(dst_store, dst_name, src_ap, src_names, tag):
        for tt in range(2):
            s.emit("sp", lambda a, tt=tt: a.dma_start(
                out=dst_store[:, :, tt * 128:(tt + 1) * 128],
                in_=src_ap[:, tt, :], transpose=True),
                reads=src_names, writes=[dst_name], dma=True, tag=f"{tag}_tr{tt}")

    def mk_shift_delta(d_name, d_ap, src_ap, src_name, bnd_ap, bnd_name, tag):
        s.emit("pool", lambda v: v.tensor_tensor(
            d_ap[:, :, 1:Q], src_ap[:, :, 0:Q - 1], src_ap[:, :, 1:Q], Alu.subtract),
            reads=[src_name], writes=[d_name], tag=f"{tag}_d")
        s.emit("pool", lambda v: v.tensor_tensor(
            d_ap[:, :, 0:1], bnd_ap[:], src_ap[:, :, 0:1], Alu.subtract),
            reads=[src_name, bnd_name], writes=[d_name], tag=f"{tag}_d0")

    mix_rr = [0]
    mix_n = [4]
    def mk_mix(coef_ap, coef_names, base_ap, base_names, d_ap, d_names, tag):
        mt, mn = MIXB[mix_rr[0] % mix_n[0]]
        mix_rr[0] += 1
        for cc in range(8):
            s.emit("dve", lambda v, cc=cc, mt=mt: v.scalar_tensor_tensor(
                mt[:, cc, :], d_ap[:, cc, :], coef_ap[:, cc:cc + 1],
                base_ap[:, cc, :], Alu.mult, Alu.add),
                reads=list(d_names) + list(base_names) + list(coef_names),
                writes=[mn], tag=f"{tag}_{cc}")
        return mt, mn

    def proj_fm(mt, mn, wslot, wname, evac, tag, pps=None):
        if pps is None:
            pps = [(pp0[:, 0:512], "pp0"), (pp1[:, 0:512], "pp1")]
        for dc in range(8):
            p, pname = pps[dc % 2]
            for kc in range(8):
                s.emit("pe", lambda t, dc=dc, kc=kc, p=p, mt=mt: raw_matmul(
                    t, nc, p[:, 0:Q],
                    warena[:, wslot, kc, dc * 128:(dc + 1) * 128],
                    mt[:, kc, :], start=(kc == 0), stop=(kc == 7)),
                    reads=[mn, wname] + ([pname] if kc == 0 else []),
                    writes=[pname] if kc == 0 else [pname + "_acc"],
                    tag=f"{tag}_mm{dc}.{kc}")
            evac(dc, p, pname)

    def proj_tm2(stat_ap, stat_names, wslot, wname, evac, tag, pps=None):
        if pps is None:
            pps = [(pp0[:, 0:512], "pp0"), (pp1[:, 0:512], "pp1")]
        for tt in range(2):
            for kc in range(8):
                for dh in range(2):
                    p, pname = pps[dh]
                    s.emit("pe", lambda t, tt=tt, dh=dh, kc=kc, p=p: raw_matmul(
                        t, nc, p[:, 0:512],
                        stat_ap[:, kc, tt * 128:(tt + 1) * 128],
                        warena[:, wslot, kc, dh * 512:(dh + 1) * 512],
                        start=(kc == 0), stop=(kc == 7), ldw=(dh == 0)),
                        reads=list(stat_names) + [wname] + ([pname] if kc == 0 else []),
                        writes=[pname] if kc == 0 else [pname + "_acc"],
                        tag=f"{tag}_mm{tt}.{dh}.{kc}")
            evac(tt)

    P2_PE_GUARD = [0]
    FCTX = {}
    TCTX = {}
    def attn_chunk(src_dram, c, mode, tag, phase=None):
        full = (mode == "full")
        tg = f"{tag}{c}"
        xch = xch_b[:, c % 2]
        xchn = f"xch{c % 2}"
        def st(name):
            return STAGES is None or name in STAGES
        if full and phase == "body":
            C = FCTX.pop(c)
            xch, xchn = C["xch"], C["xchn"]
            ln1x_fm_c, ln1x_fm_cn = C["l1"], C["l1n"]
            dfm_c, dfm_cn = C["d"], C["dn"]
            k_fm_c, k_fm_cn = C["k"], C["kn"]
            v_tm_c, v_tm_cn = C["v"], C["vn"]
            kwkT_c, kwkT_cn = C["kw"], C["kwn"]
            r_c, r_cn = C["r"], C["rn"]
        elif full:
            if c % 2 == 0:
                f_l1, f_l1n = ln1x_fm[:], "ln1x_fm"
                f_d, f_dn = dfm[:], "dfm"
                f_k, f_kn = k_fm[:], "k_fm"
                f_v, f_vn = v_tm[:], "v_tm"
                f_kw, f_kwn = kwkT[:], "kwkT"
            else:
                f_l1, f_l1n = ln1x_fmB, "ln1x_fmB"
                f_d, f_dn = dfmB, "dfmB"
                f_k, f_kn = k_fmB, "k_fmB"
                f_v, f_vn = v_tmB, "v_tmB"
                f_kw, f_kwn = kwkTB, "kwkTB"
            if phase == "ldx":
                dma("sp", xch, src_dram[c * Q:(c + 1) * Q, :].rearrange("(tt p) d -> p tt d", p=128),
                    [], [xchn], f"{tg}_ldx")
                return
            if phase == "load":
                gw = [("s_pe", P2_PE_GUARD[0])] if (c % 2 == 1 and c <= 2) else []
                s.emit("sp", lambda e, c=c, f_l1=f_l1: e.dma_start(out=f_l1, in_=l1_dram[c]),
                       reads=[], writes=[f_l1n], extra_waits=gw, dma=True, tag=f"{tg}_ldl1")
                s.emit("pool", lambda e, c=c, f_k=f_k: e.dma_start(out=f_k, in_=k_dram[c]),
                       reads=[], writes=[f_kn], extra_waits=gw, dma=True, tag=f"{tg}_ldk")
                s.emit("pool", lambda e, c=c, f_v=f_v: e.dma_start(out=f_v, in_=v_dram[c]),
                       reads=[], writes=[f_vn], extra_waits=gw, dma=True, tag=f"{tg}_ldv")
                s.emit("pool", lambda e, c=c, f_kw=f_kw: e.dma_start(out=f_kw, in_=kw_dram[c]),
                       reads=[], writes=[f_kwn], extra_waits=gw, dma=True, tag=f"{tg}_ldkw")
                return
            bnd_ap = bnd1[:] if c == 0 else bnd_hist[:, :, c - 1:c]
            bnd_nm = "bnd1" if c == 0 else "bnd_hist"
            mk_shift_delta(f_dn, f_d, f_l1, f_l1n, bnd_ap, bnd_nm, tg + "_sh")
            ln1x_fm_c, ln1x_fm_cn = f_l1, f_l1n
            dfm_c, dfm_cn = f_d, f_dn
            k_fm_c, k_fm_cn = f_k, f_kn
            v_tm_c, v_tm_cn = f_v, f_vn
            kwkT_c, kwkT_cn = f_kw, f_kwn
            if c % 2 == 0:
                r_c, r_cn = r_fm[:], "r_fm"
            else:
                r_c, r_cn = r_fmB, "r_fmB"
            mr, mrn = mk_mix(colA[:, :, 2], ["colA"], ln1x_fm_c, [ln1x_fm_cn], dfm_c, [dfm_cn], tg + "_mxr")
            def evac_r(dc, p, pname):
                s.emit("act", lambda a, dc=dc, p=p: a.activation(
                    r_c[:, dc, :], p[:, 0:Q], Act.Identity),
                    reads=[pname, pname + "_acc"], writes=[r_cn], tag=f"{tg}_evr{dc}")
            proj_fm(mr, mrn, SLOT["wrT"], WBUF["wrT"], evac_r, tg + "_pr")
            if phase == "front":
                FCTX[c] = dict(xch=xch, xchn=xchn, l1=ln1x_fm_c, l1n=ln1x_fm_cn,
                               d=dfm_c, dn=dfm_cn, k=k_fm_c, kn=k_fm_cn,
                               v=v_tm_c, vn=v_tm_cn, kw=kwkT_c, kwn=kwkT_cn,
                               r=r_c, rn=r_cn)
                return
        else:
            dma("sp", xch, src_dram[c * Q:(c + 1) * Q, :].rearrange("(tt p) d -> p tt d", p=128),
                [], [xchn], f"{tg}_ldx")
            ln_stats(xch, [xchn], D, tg + "_ln1")
            ln_apply(ln1x_tm, "ln1x_tm", xch, [xchn], tg + "_ln1")
            if not st("tr"):
                return
            transpose_tm_to_fm(ln1x_fm, "ln1x_fm", ln1x_tm, ["ln1x_tm"], tg + "_l1t")
            mk_shift_delta("dfm", dfm, ln1x_fm, "ln1x_fm",
                           bnd1[:] if c == 0 else bnd_hist[:, :, c - 1:c],
                           "bnd1" if c == 0 else "bnd_hist", tg + "_sh")
            s.emit("dve", lambda v, c=c: v.tensor_copy(
                bnd_hist[:, :, c:c + 1], ln1x_fm[:, :, Q - 1:Q]),
                reads=["ln1x_fm"], writes=["bnd_hist"], tag=f"{tg}_bnd1")
            dma("pool", l1_dram[c], ln1x_fm[:], ["ln1x_fm"], ["l1_dram"], f"{tg}_spl1")
            ln1x_fm_c, ln1x_fm_cn = ln1x_fm[:], "ln1x_fm"
            dfm_c, dfm_cn = dfm[:], "dfm"
            k_fm_c, k_fm_cn = k_fm[:], "k_fm"
            v_tm_c, v_tm_cn = v_tm[:], "v_tm"
            kwkT_c, kwkT_cn = kwkT[:], "kwkT"

        x1ch = xch
        x1n = xchn
        if not full:
            mk, mkn = mk_mix(colA[:, :, 0], ["colA"], ln1x_fm, ["ln1x_fm"], dfm, ["dfm"], tg + "_mxk")
            def evac_k(dc, p, pname):
                s.emit("act", lambda a, dc=dc, p=p: a.activation(
                    k_fm[:, dc, :], p[:, 0:Q], Act.Identity),
                    reads=[pname, pname + "_acc"], writes=["k_fm"], tag=f"{tg}_evk{dc}")
                s.emit("dve", lambda v, dc=dc, p=p: v.tensor_tensor(
                    kwk_fm[:, dc, :], p[:, 0:Q], mk_fm[:, dc, :], Alu.mult),
                    reads=[pname, pname + "_acc", "mk_fm"], writes=["kwk_fm"], tag=f"{tg}_evkw{dc}")
            proj_fm(mk, mkn, SLOT["wkT"], WBUF["wkT"], evac_k, tg + "_pk")
            s.emit("sp", lambda a: a.dma_start(
                out=kwkT[:],
                in_=kwk_fm.rearrange("p cc t -> p (cc t)"), transpose=True),
                reads=["kwk_fm"], writes=["kwkT"], dma=True, tag=f"{tg}_kwt")
            mv, mvn = mk_mix(colA[:, :, 1], ["colA"], ln1x_fm, ["ln1x_fm"], dfm, ["dfm"], tg + "_mxv")
            def evac_v(tt):
                for dh in range(2):
                    if dh == 0:
                        s.emit("dve", lambda v, tt=tt, dh=dh: v.tensor_copy(
                            v_tm[:, tt, dh * 512:(dh + 1) * 512], PP[dh][:, 0:512]),
                            reads=[f"pp{dh}", f"pp{dh}_acc"], writes=["v_tm"], tag=f"{tg}_evv{tt}.{dh}")
                    else:
                        s.emit("act", lambda a, tt=tt, dh=dh: a.activation(
                            v_tm[:, tt, dh * 512:(dh + 1) * 512], PP[dh][:, 0:512], Act.Identity),
                            reads=[f"pp{dh}", f"pp{dh}_acc"], writes=["v_tm"], tag=f"{tg}_evv{tt}.{dh}")
            proj_tm2(mv, [mvn], SLOT["wvT"], WBUF["wvT"], evac_v, tg + "_pv")
            dma("pool", k_dram[c], k_fm[:], ["k_fm"], ["k_dram"], f"{tg}_spk")
            dma("pool", kw_dram[c], kwkT[:], ["kwkT"], ["kw_dram"], f"{tg}_spkw")
            dma("pool", v_dram[c], v_tm[:], ["v_tm"], ["v_dram"], f"{tg}_spv")

        if full and st("rg"):
            mg, mgn = mk_mix(colA[:, :, 3], ["colA"], ln1x_fm_c, [ln1x_fm_cn], dfm_c, [dfm_cn], tg + "_mxg")
            s.emit("pool", lambda v, r_c=r_c: v.tensor_tensor(r_wb[:], r_c, m_fm[:], Alu.mult),
                   reads=[r_cn, "m_fm"], writes=["r_wb"], tag=f"{tg}_rwb")

            if not st("scan"):
                return
            for hp in range(8):
                e, o = 2 * hp, 2 * hp + 1
                for par, h in ((0, e), (1, o)):
                    r0 = 64 * par
                    spx = [sp0, sp1][par]
                    spn = f"sp{par}"
                    s.emit("pe", lambda t, hp=hp, r0=r0, spx=spx, kf=k_fm_c, rc=r_c: t.matmul(
                        spx[:, 0:256],
                        kf[r0:r0 + 64, hp, 0:128], rc[r0:r0 + 64, hp, :],
                        start=True, stop=True,
                        tile_position=(r0, 0)),
                        reads=[k_fm_cn, r_cn, spn],
                        writes=[spn], tag=f"{tg}_sj0.{h}")
                    s.emit("pe", lambda t, hp=hp, r0=r0, spx=spx, kf=k_fm_c, rc=r_c: t.matmul(
                        spx[:, 256:384],
                        kf[r0:r0 + 64, hp, 128:256], rc[r0:r0 + 64, hp, 128:256],
                        start=True, stop=True,
                        tile_position=(r0, 0)),
                        reads=[k_fm_cn, r_cn],
                        writes=[spn + "_acc"], tag=f"{tg}_sj1.{h}")
                for par, h in ((0, e), (1, o)):
                    spx = [sp0, sp1][par]
                    spn = f"sp{par}"
                    s.emit("dve", lambda v, h=h, spx=spx: v.tensor_tensor(
                        sbar[:, h, 0:256], spx[:, 0:256],
                        wmat[:, h, :], Alu.mult),
                        reads=[spn, spn + "_acc", "wmat"],
                        writes=[f"sbar.{h}", "sbar"], tag=f"{tg}_sb.{h}")
                    s.emit("dve", lambda v, h=h, spx=spx: v.tensor_tensor(
                        sbar[:, h, 256:384], spx[:, 256:384],
                        wmat[:, h, 0:128], Alu.mult),
                        reads=[spn, spn + "_acc", "wmat"],
                        writes=[f"sbar.{h}", "sbar"], tag=f"{tg}_sb1.{h}")

            if not st("y"):
                return
            if c % 2 == 0:
                yn_c, yn_cn = yn_tm, "yn_tm"
            else:
                yn_c, yn_cn = yn_tmB, "mk_fm"
            for it in range(2):
                for h in range(H):
                    hp, par = h // 2, h % 2
                    r0 = 64 * par
                    hc = h * HN
                    if it == 0:
                        s.emit("pe", lambda t, h=h, hc=hc, vt=v_tm_c: t.matmul(
                            yps[:, hc:hc + HN], sbar[:, h, 0:128], vt[:, 0, hc:hc + HN],
                            start=True, stop=False),
                            reads=[f"sbar.{h}", v_tm_cn] + (["yps"] if h == 0 else []),
                            writes=["yps"] if h == 0 else ["yps_acc"],
                            tag=f"{tg}_yi0.{h}")
                    else:
                        s.emit("pe", lambda t, h=h, hc=hc, vt=v_tm_c: t.matmul(
                            yps[:, hc:hc + HN], sbar[:, h, 128:256], vt[:, 0, hc:hc + HN],
                            start=True, stop=False),
                            reads=[f"sbar.{h}", v_tm_cn] + (["yps"] if h == 0 else []),
                            writes=["yps"] if h == 0 else ["yps_acc"],
                            tag=f"{tg}_yi1a.{h}")
                        s.emit("pe", lambda t, h=h, hc=hc, vt=v_tm_c: t.matmul(
                            yps[:, hc:hc + HN], sbar[:, h, 256:384], vt[:, 1, hc:hc + HN],
                            start=False, stop=False),
                            reads=[f"sbar.{h}", v_tm_cn], writes=["yps_acc"],
                            tag=f"{tg}_yi1b.{h}")
                    s.emit("pe", lambda t, h=h, hc=hc, hp=hp, r0=r0, it=it: t.matmul(
                        yps[:, hc:hc + HN],
                        r_wb[r0:r0 + 64, hp, it * 128:(it + 1) * 128],
                        state_b[r0:r0 + 64, hp, :],
                        start=False, stop=True, tile_position=(r0, 0)),
                        reads=["r_wb", "state_b"], writes=["yps_acc"],
                        tag=f"{tg}_yx{it}.{h}")
                s.emit("dve", lambda v: v.tensor_copy(y_sb[:], yps[:]),
                       reads=["yps", "yps_acc"], writes=["y_sb"], tag=f"{tg}_yev{it}")
                s.emit("dve", lambda v: v.tensor_reduce(
                    st_sum[:], y_sb.rearrange("p (h n) -> p h n", h=H), AXX, Alu.add),
                    reads=["y_sb"], writes=["st_sum"], tag=f"{tg}_gns{it}")
                s.emit("act", lambda a: a.activation(
                    sqs[:], y_sb[:], Act.Square),
                    reads=["y_sb"], writes=["sqs"], tag=f"{tg}_gnq{it}")
                s.emit("dve", lambda v: v.tensor_reduce(
                    st_sq[:], sqs.rearrange("p (h n) -> p h n", h=H), AXX, Alu.add),
                    reads=["sqs"], writes=["st_sq"], tag=f"{tg}_gnq2{it}")
                inv = 1.0 / HN
                s.emit("dve", lambda v: v.tensor_scalar(
                    st_mu[:], st_sum[:], inv, None, Alu.mult),
                    reads=["st_sum"], writes=["st_mu"], tag=f"{tg}_gmu{it}")
                s.emit("dve", lambda v: v.tensor_tensor(
                    st_var[:], st_mu[:], st_mu[:], Alu.mult),
                    reads=["st_mu"], writes=["st_var"], tag=f"{tg}_gm2{it}")
                s.emit("dve", lambda v: v.tensor_scalar(
                    st_sq[:], st_sq[:], inv, None, Alu.mult),
                    reads=["st_sq"], writes=["st_sq"], tag=f"{tg}_gq2{it}")
                s.emit("dve", lambda v: v.tensor_tensor(
                    st_var[:], st_sq[:], st_var[:], Alu.subtract),
                    reads=["st_sq", "st_var"], writes=["st_var"], tag=f"{tg}_gvr{it}")
                s.emit("dve", lambda v: v.tensor_scalar(
                    st_var[:], st_var[:], EPS_GN, None, Alu.add),
                    reads=["st_var"], writes=["st_var"], tag=f"{tg}_gep{it}")
                s.emit("act", lambda a: a.activation(
                    st_rstd[:], st_var[:], Act.Sqrt),
                    reads=["st_var"], writes=["st_rstd"], tag=f"{tg}_gsd{it}")
                s.emit("dve", lambda v: v.reciprocal(st_rstd[:], st_rstd[:]),
                       reads=["st_rstd"], writes=["st_rstd"], tag=f"{tg}_grs{it}")
                s.emit("dve", lambda v: v.tensor_tensor(
                    st_nmr[:], st_mu[:], st_rstd[:], Alu.mult),
                    reads=["st_mu", "st_rstd"], writes=["st_nmr"], tag=f"{tg}_gmr{it}")
                s.emit("dve", lambda v: v.tensor_scalar(
                    st_nmr[:], st_nmr[:], -1.0, None, Alu.mult),
                    reads=["st_nmr"], writes=["st_nmr"], tag=f"{tg}_gnm{it}")
                for h in range(H):
                    hc = h * HN
                    if h % 2 == 0:
                        s.emit("dve", lambda v, h=h, hc=hc, it=it, yn_c=yn_c: v.tensor_scalar(
                            yn_c[:, it, hc:hc + HN], y_sb[:, hc:hc + HN],
                            st_rstd[:, h:h + 1], st_nmr[:, h:h + 1], Alu.mult, Alu.add),
                            reads=["y_sb", "st_rstd", "st_nmr"],
                            writes=[yn_cn], tag=f"{tg}_gap{it}.{h}")
                    else:
                        s.emit("act", lambda a, h=h, hc=hc, it=it, yn_c=yn_c: a.activation(
                            yn_c[:, it, hc:hc + HN], y_sb[:, hc:hc + HN], Act.Identity,
                            bias=st_nmr[:, h:h + 1], scale=st_rstd[:, h:h + 1]),
                            reads=["y_sb", "st_rstd", "st_nmr"],
                            writes=[yn_cn], tag=f"{tg}_gap{it}.{h}")

        tap(f"{tg}_ln1xfm", ln1x_fm[:], (128, 8, Q), ["ln1x_fm"])
        tap(f"{tg}_kfm", k_fm[:], (128, 8, Q), ["k_fm"])
        tap(f"{tg}_vtm", v_tm[:], (128, 2, D), ["v_tm"])
        tap(f"{tg}_kwkT", kwkT[:], (128, 8, 2, 128), ["kwkT"])
        if not st("su"):
            return
        for hp in range(8):
            suk = SU[hp % 2]
            sname = f"su{hp % 2}"
            for par in range(2):
                h = 2 * hp + par
                for jt in range(2):
                    first = (jt == 0 and par == 0)
                    s.emit("pe", lambda t, hp=hp, jt=jt, par=par, h=h, suk=suk,
                           kw=kwkT_c, vt=v_tm_c: raw_matmul(
                        t, nc, suk[64 * par:64 * par + 64, 0:HN],
                        kw[:, hp, jt, 64 * par:64 * par + 64],
                        vt[:, jt, h * HN:(h + 1) * HN],
                        start=(jt == 0), stop=(jt == 1),
                        tile_position=(0, 64 * par)),
                        reads=[kwkT_cn, v_tm_cn] + ([sname] if first else []),
                        writes=[sname] if first else [sname + "_acc"],
                        tag=f"{tg}_su{hp}.{jt}.{par}")
            s.emit("dve", lambda v, hp=hp, suk=suk: v.scalar_tensor_tensor(
                state_f[:, hp, :], state_f[:, hp, :], ws_rep[:, hp:hp + 1],
                suk[:, 0:HN], Alu.mult, Alu.add),
                reads=["state_f", "ws_rep", sname, sname + "_acc"],
                writes=["state_f"], tag=f"{tg}_stu{hp}")
            s.emit("dve", lambda v, hp=hp: v.tensor_copy(
                state_b[:, hp, :], state_f[:, hp, :]),
                reads=["state_f"], writes=["state_b"], tag=f"{tg}_stc{hp}")

        tap(f"{tg}_state", state_f[:], (128, 8, HN), ["state_f"])
        if full and st("yo"):
            tap(f"{tg}_rfm", r_fm[:], (128, 8, Q), ["r_fm"])
            tap(f"{tg}_sbar", sbar[:], (128, H, 384), ["sbar"])
            TCTX[c] = dict(yn=yn_c, ynn=yn_cn, mg=mg, mgn=mgn, xch=xch, xchn=xchn)

    def attn_tail(c):
        tg = f"A{c}"
        T = TCTX.pop(c)
        yn_c, yn_cn = T["yn"], T["ynn"]
        mg, mgn = T["mg"], T["mgn"]
        xch, xchn = T["xch"], T["xchn"]
        tap(f"{tg}_yn", yn_c[:], (128, 2, D), [yn_cn])
        transpose_tm_to_fm(yn_fm, "yn_fm", yn_c, [yn_cn], tg + "_ynt")
        def evac_g(dc, p, pname):
            s.emit("act", lambda a, dc=dc, p=p: a.activation(
                g_fm[:, dc, :], p[:, 0:Q], Act.Silu),
                reads=[pname, pname + "_acc"], writes=["g_fm"], tag=f"{tg}_evg{dc}")
        proj_fm(mg, mgn, SLOT["wgT"], WBUF["wgT"], evac_g, tg + "_pg")
        tap(f"{tg}_gfm", g_fm[:], (128, 8, Q), ["g_fm"])
        ylt, yln = MIXB[mix_rr[0] % mix_n[0]]
        mix_rr[0] += 1
        for cc in range(8):
            s.emit("act", lambda a, cc=cc, ylt=ylt: a.activation(
                ylt[:, cc, :], yn_fm[:, cc, :], Act.Identity,
                bias=colB[:, cc, 1:2], scale=colB[:, cc, 0:1]),
                reads=["yn_fm", "colB"], writes=[yln], tag=f"{tg}_lnaf{cc}")
        s.emit("pool", lambda v, ylt=ylt: v.tensor_tensor(yg_fm[:], ylt[:], g_fm[:], Alu.mult),
               reads=[yln, "g_fm"], writes=["yg_fm"], tag=f"{tg}_yg")
        tap(f"{tg}_yg", yg_fm[:], (128, 8, Q), ["yg_fm"])
        stage = x1s[:, 0, :]
        def evac_o(tt):
            s.emit("dve", lambda v: v.tensor_copy(stage[:, 0:512], PP[0][:, 0:512]),
                   reads=["pp0", "pp0_acc"], writes=["x1s"], tag=f"{tg}_oev{tt}a")
            s.emit("act", lambda a: a.activation(
                stage[:, 512:1024], PP[1][:, 0:512], Act.Identity),
                reads=["pp1", "pp1_acc"], writes=["x1s"], tag=f"{tg}_oev{tt}b")
            ln_stats_tt(lambda _tt, g: stage[:, g * 512:(g + 1) * 512],
                        ["x1s"], GO, tt, f"{tg}_l2s{tt}")
            ln_fin(GO, tt, f"{tg}_l2f{tt}")
            s.emit("act", lambda a, tt=tt: a.activation(
                stage, stage, Act.Identity,
                bias=stOn[:, tt:tt + 1], scale=stOr[:, tt:tt + 1]),
                reads=["x1s", f"stOr.{tt}", f"stOn.{tt}"], writes=["x1s"],
                tag=f"{tg}_l2ap{tt}")
            s.emit("dve", lambda v, tt=tt, xch=xch: v.tensor_tensor(
                xch[:, tt, :], stage, xch[:, tt, :], Alu.add),
                reads=["x1s", xchn], writes=[xchn], tag=f"{tg}_l2rx{tt}")
        proj_tm2(yg_fm, ["yg_fm"], SLOT["woT"], WBUF["woT"], evac_o, tg + "_po")

        tap(f"{tg}_x1", xch[:], (128, 2, D), [xchn])
        dma("sp", x1_dram[c * Q:(c + 1) * Q, :].rearrange("(tt p) d -> p tt d", p=128),
            xch[:], [xchn], ["x1_dram"], f"{tg}_stx1")

    FCTX3 = {}

    X1SL = [
        (lambda tt: xch_b[:, 0, tt, :], ["xch0"]),
        (lambda tt: xch_b[:, 1, tt, :], ["xch1"]),
        (lambda tt: gv_x1 if tt == 0 else ygv_x1, ["g_fm", "yg_fm"]),
        (lambda tt: rwv_x1 if tt == 0 else mfv_x1, ["r_wb", "m_fm"]),
        (lambda tt: mkv_x1 if tt == 0 else rfv_x1, ["mk_fm", "r_fm"]),
        (lambda tt: vtv_x1 if tt == 0 else kTv_x1, ["v_tm", "kwkT"]),
    ]

    def x1ap(c, tt):
        return X1SL[c % 6][0](tt)

    def x1nm(c):
        return X1SL[c % 6][1]

    def ffn_x1_load(c):
        for tt in range(2):
            dma("sp", x1ap(c, tt), x1_dram[c * Q + tt * 128:c * Q + (tt + 1) * 128, :],
                ["x1_dram"], x1nm(c), f"F{c}_ldx1.{tt}")

    def fctx(c):
        par = c % 2
        d = {}
        if par == 0:
            d["och"], d["ochn"] = och, "och"
            d["l1"], d["l1n"] = ln3x_fm[:], "ln1x_fm"
            d["dm"], d["dmn"] = dfm[:], "dfm"
            d["ka"], d["kan"] = kact_fm[:], "k_fm"
            d["rp"], d["rpn"] = rp_tm[:], "yn_tm"
            d["l3tm"], d["l3tmn"] = ln3x_tm, "ln1x_tm"
            d["GF"], d["GO"] = GF, GO
            d["rpp"] = None
            d["kpp"] = None
        else:
            d["och"], d["ochn"] = x1s, "x1s"
            d["l1"], d["l1n"] = ln1x_fmB, "ln1x_fmB"
            d["dm"], d["dmn"] = dfmB, "dfmB"
            d["ka"], d["kan"] = k_fmB, "k_fmB"
            d["rp"], d["rpn"] = rp_tmB_v, "yn_fm"
            d["l3tm"], d["l3tmn"] = ln3x_tmB, "kwk_fm"
            d["GF"], d["GO"] = GFb, GOb
            d["rpp"] = [(sp0[:, 0:512], "sp0"), (sp1[:, 0:512], "sp1")]
            d["kpp"] = [(su0[:, 0:512], "su0"), (su1[:, 0:512], "su1")]
        return d

    def ffn_front_ln(c):
        tg, C = f"F{c}", fctx(c)
        b6, b6n, a2, a2n, w, wn, rr, rrn, nn_, nnn = C["GF"]
        l3, l3n = C["l3tm"], C["l3tmn"]
        for tt in range(2):
            ln_stats_tt(lambda tt, g: x1ap(c, tt)[:, g * 512:(g + 1) * 512],
                        x1nm(c), C["GF"], tt, tg + "_ln3")
            ln_fin(C["GF"], tt, tg + "_ln3")
            aw = ["ln1x_tm"] if c == 0 else []
            s.emit("act", lambda a, tt=tt: a.activation(
                l3[:, tt, :], x1ap(c, tt), Act.Identity,
                bias=nn_[:, tt:tt + 1], scale=rr[:, tt:tt + 1]),
                reads=x1nm(c) + [f"{rrn}.{tt}", f"{nnn}.{tt}"] + aw,
                writes=[f"{l3n}.{tt}"] + aw, tag=f"{tg}_ln3_ap{tt}")
            s.emit("sp", lambda a, tt=tt: a.dma_start(
                out=C["l1"][:, :, tt * 128:(tt + 1) * 128], in_=l3[:, tt, :],
                transpose=True),
                reads=[f"{l3n}.{tt}"], writes=[C["l1n"]], dma=True,
                tag=f"{tg}_l3t_tr{tt}")

    def ffn_front_shift(c):
        tg, C = f"F{c}", fctx(c)
        mk_shift_delta(C["dmn"], C["dm"], C["l1"], C["l1n"], bnd3, "bnd3", tg + "_sh")
        s.emit("dve", lambda v, f_l1=C["l1"]: v.tensor_copy(bnd3[:], f_l1[:, :, Q - 1:Q]),
               reads=[C["l1n"]], writes=["bnd3"], tag=f"{tg}_bnd3")

    FMIX = {}

    def ffn_front_mix(c):
        tg, C = f"F{c}", fctx(c)
        mr, mrn = mk_mix(colC[:, :, 0], ["colC"], C["l1"], [C["l1n"]],
                         C["dm"], [C["dmn"]], tg + "_mxr")
        mk2, mk2n = mk_mix(colC[:, :, 1], ["colC"], C["l1"], [C["l1n"]],
                           C["dm"], [C["dmn"]], tg + "_mxk")
        FMIX[c] = (mr, mrn, mk2, mk2n)

    def ffn_front_projr(c):
        tg, C = f"F{c}", fctx(c)
        f_rp, f_rpn = C["rp"], C["rpn"]
        rpp = C["rpp"] or [(pp0[:, 0:512], "pp0"), (pp1[:, 0:512], "pp1")]
        mr, mrn = FMIX[c][0], FMIX[c][1]
        def evac_rp(tt):
            for dh in range(2):
                pap, pname = rpp[dh]
                s.emit("act", lambda a, tt=tt, dh=dh, pap=pap, f_rp=f_rp: a.activation(
                    f_rp[:, tt, dh * 512:(dh + 1) * 512], pap, Act.Sigmoid),
                    reads=[pname, pname + "_acc"], writes=[f_rpn],
                    tag=f"{tg}_evr{tt}.{dh}")
        proj_tm2(mr, [mrn], SLOT["fwrT"], WBUF["fwrT"], evac_rp, tg + "_prp",
                 pps=rpp)

    def ffn_front_projk(c):
        tg, C = f"F{c}", fctx(c)
        f_ka, f_kan = C["ka"], C["kan"]
        mk2, mk2n = FMIX.pop(c)[2:4]
        sq = y_sb[:, 0:Q] if c % 2 == 0 else sqs[:, 0:Q]
        sqn = "y_sb" if c % 2 == 0 else "sqs"
        def evac_kp(dc, p, pname):
            s.emit("act", lambda a, dc=dc, p=p: a.activation(
                sq, p[:, 0:Q], Act.Relu),
                reads=[pname, pname + "_acc"], writes=[sqn], tag=f"{tg}_evk{dc}a")
            s.emit("act", lambda a, dc=dc, f_ka=f_ka: a.activation(
                f_ka[:, dc, :], sq, Act.Square),
                reads=[sqn], writes=[f_kan], tag=f"{tg}_evk{dc}b")
        proj_fm(mk2, mk2n, SLOT["fwkT"], WBUF["fwkT"], evac_kp, tg + "_pkp",
                pps=C["kpp"])

    def ffn_front(c):
        ffn_front_ln(c)
        ffn_front_shift(c)
        ffn_front_mix(c)
        ffn_front_projr(c)
        ffn_front_projk(c)

    YPP = [(yps[:, 0:512], "yps"), (yps[:, 512:1024], "yps")]

    def ffn_body_v(c):
        tg, C = f"F{c}", fctx(c)
        f_och, f_ochn = C["och"], C["ochn"]
        f_rp, f_rpn = C["rp"], C["rpn"]
        vpp = YPP if c % 2 == 0 else [(pp0[:, 0:512], "pp0"), (pp1[:, 0:512], "pp1")]
        def evac_vp(tt):
            for dh in range(2):
                pap, pname = vpp[dh]
                s.emit("dve", lambda v, tt=tt, dh=dh, f_rp=f_rp, pap=pap,
                       f_och=f_och: v.tensor_tensor(
                    f_och[:, tt, dh * 512:(dh + 1) * 512], pap,
                    f_rp[:, tt, dh * 512:(dh + 1) * 512], Alu.mult),
                    reads=[pname, pname + "_acc", f_rpn], writes=[f_ochn],
                    tag=f"{tg}_gate{tt}.{dh}")
        proj_tm2(C["ka"], [C["kan"]], SLOT["fwvT"], WBUF["fwvT"], evac_vp,
                 tg + "_pvp", pps=vpp)
        tap(f"{tg}_kact", C["ka"], (128, 8, Q), [C["kan"]])

    def ffn_body_ln4(c):
        tg, C = f"F{c}", fctx(c)
        f_och, f_ochn = C["och"], C["ochn"]
        tap(f"{tg}_preln4", f_och[:], (128, 2, D), [f_ochn])
        for tt in range(2):
            ln_stats_tt(lambda tt, g: f_och[:, tt, g * 512:(g + 1) * 512],
                        [f_ochn], C["GO"], tt, f"{tg}_l4s{tt}")
            ln_fin(C["GO"], tt, f"{tg}_l4f{tt}")

    def ffn_body_fin(c):
        tg, C = f"F{c}", fctx(c)
        f_och, f_ochn = C["och"], C["ochn"]
        _, _, _, _, _, _, rr, rrn, nn_, nnn = C["GO"]
        for tt in range(2):
            s.emit("act", lambda a, tt=tt, f_och=f_och: a.activation(
                f_och[:, tt, :], f_och[:, tt, :], Act.Identity,
                bias=nn_[:, tt:tt + 1], scale=rr[:, tt:tt + 1]),
                reads=[f_ochn, f"{rrn}.{tt}", f"{nnn}.{tt}"], writes=[f_ochn],
                tag=f"{tg}_l4a{tt}")
            s.emit("dve", lambda v, tt=tt, f_och=f_och: v.tensor_tensor(
                f_och[:, tt, :], f_och[:, tt, :], x1ap(c, tt), Alu.add),
                reads=[f_ochn] + x1nm(c), writes=[f_ochn], tag=f"{tg}_l4r{tt}")
        dma("sp", out_d[c * Q:(c + 1) * Q, :].rearrange("(tt p) d -> p tt d", p=128),
            f_och[:], [f_ochn], ["out_d"], f"{tg}_sto")

    def kvctx(c):
        par = c % 2
        d = {}
        d["x"], d["xn"] = xch_b[:, par], f"xch{par}"
        if par == 0:
            d["l1tm"], d["l1tmn"] = ln1x_tm, "ln1x_tm"
            d["l1"], d["l1n"] = ln1x_fm[:], "ln1x_fm"
            d["dm"], d["dmn"] = dfm[:], "dfm"
            d["kf"], d["kfn"] = k_fm[:], "k_fm"
            d["kwf"], d["kwfn"] = kwk_fm[:], "kwk_fm"
            d["kwT"], d["kwTn"] = kwkT[:], "kwkT"
            d["vt"], d["vtn"] = v_tm, "v_tm"
            d["G"] = GF
            d["pps"] = None
        else:
            d["l1tm"], d["l1tmn"] = yn_tm, "yn_tm"
            d["l1"], d["l1n"] = r_fm[:], "r_fm"
            d["dm"], d["dmn"] = g_fm[:], "g_fm"
            d["kf"], d["kfn"] = yg_fm[:], "yg_fm"
            d["kwf"], d["kwfn"] = yn_fm[:], "yn_fm"
            d["kwT"], d["kwTn"] = kwkTB_P1, "r_wb"
            d["vt"], d["vtn"] = v_tmB_P1, "sbar"
            d["G"] = GFb
            d["pps"] = [(sp0[:, 0:512], "sp0"), (sp1[:, 0:512], "sp1")]
        return d

    def kv_ldx(c):
        C = kvctx(c)
        dma("sp", C["x"], xo[c * Q:(c + 1) * Q, :].rearrange("(tt p) d -> p tt d", p=128),
            [], [C["xn"]], f"P{c}_ldx")

    def kv_front(c):
        tg, C = f"P{c}", kvctx(c)
        b6, b6n, a2, a2n, w, wn, rr, rrn, nn_, nnn = C["G"]
        for tt in range(2):
            ln_stats_tt(lambda tt, g: C["x"][:, tt, g * 512:(g + 1) * 512],
                        [C["xn"]], C["G"], tt, tg + "_ln1")
            ln_fin(C["G"], tt, tg + "_ln1")
            s.emit("act", lambda a, tt=tt: a.activation(
                C["l1tm"][:, tt, :], C["x"][:, tt, :], Act.Identity,
                bias=nn_[:, tt:tt + 1], scale=rr[:, tt:tt + 1]),
                reads=[C["xn"], f"{rrn}.{tt}", f"{nnn}.{tt}"],
                writes=[f"{C['l1tmn']}.{tt}"], tag=f"{tg}_l1ap{tt}")
            s.emit("sp", lambda a, tt=tt: a.dma_start(
                out=C["l1"][:, :, tt * 128:(tt + 1) * 128], in_=C["l1tm"][:, tt, :],
                transpose=True),
                reads=[f"{C['l1tmn']}.{tt}"], writes=[C["l1n"]], dma=True,
                tag=f"{tg}_l1t{tt}")
        s.emit("dve", lambda v, c=c: v.tensor_copy(
            bnd_hist[:, :, c:c + 1], C["l1"][:, :, Q - 1:Q]),
            reads=[C["l1n"]], writes=["bnd_hist"], tag=f"{tg}_bnd1")
        dma("pool", l1_dram[c], C["l1"], [C["l1n"]], ["l1_dram"], f"{tg}_spl1")

    KVMIX = {}

    def kv_mix(c):
        tg, C = f"P{c}", kvctx(c)
        bnd_ap = bnd1[:] if c == 0 else bnd_hist[:, :, c - 1:c]
        bnd_nm = "bnd1" if c == 0 else "bnd_hist"
        mk_shift_delta(C["dmn"], C["dm"], C["l1"], C["l1n"], bnd_ap, bnd_nm, tg + "_sh")
        mk = mk_mix(colA[:, :, 0], ["colA"], C["l1"], [C["l1n"]],
                    C["dm"], [C["dmn"]], tg + "_mxk")
        mv = mk_mix(colA[:, :, 1], ["colA"], C["l1"], [C["l1n"]],
                    C["dm"], [C["dmn"]], tg + "_mxv")
        KVMIX[c] = (mk, mv)

    def kv_projk(c):
        tg, C = f"P{c}", kvctx(c)
        (mk, mkn), _ = KVMIX[c]
        kf, kfn, kwf, kwfn = C["kf"], C["kfn"], C["kwf"], C["kwfn"]
        def evac_k(dc, p, pname):
            s.emit("act", lambda a, dc=dc, p=p: a.activation(
                kf[:, dc, :], p[:, 0:Q], Act.Identity),
                reads=[pname, pname + "_acc"], writes=[kfn], tag=f"{tg}_evk{dc}")
            s.emit("dve", lambda v, dc=dc, p=p: v.tensor_tensor(
                kwf[:, dc, :], p[:, 0:Q], mk_fm[:, dc, :], Alu.mult),
                reads=[pname, pname + "_acc", "mk_fm"], writes=[kwfn],
                tag=f"{tg}_evkw{dc}")
        proj_fm(mk, mkn, SLOT["wkT"], WBUF["wkT"], evac_k, tg + "_pk",
                pps=C["pps"])
        s.emit("sp", lambda a: a.dma_start(
            out=C["kwT"], in_=C["kwf"].rearrange("p cc t -> p (cc t)"),
            transpose=True),
            reads=[C["kwfn"]], writes=[C["kwTn"]], dma=True, tag=f"{tg}_kwt")

    def kv_projv(c):
        tg, C = f"P{c}", kvctx(c)
        _, (mv, mvn) = KVMIX.pop(c)
        vt, vtn = C["vt"], C["vtn"]
        vpp = C["pps"] or [(pp0[:, 0:512], "pp0"), (pp1[:, 0:512], "pp1")]
        def evac_v(tt):
            for dh in range(2):
                pap, pname = vpp[dh]
                if dh == 0:
                    s.emit("dve", lambda v, tt=tt, pap=pap: v.tensor_copy(
                        vt[:, tt, 0:512], pap),
                        reads=[pname, pname + "_acc"], writes=[vtn],
                        tag=f"{tg}_evv{tt}.{dh}")
                else:
                    s.emit("act", lambda a, tt=tt, pap=pap: a.activation(
                        vt[:, tt, 512:1024], pap, Act.Identity),
                        reads=[pname, pname + "_acc"], writes=[vtn],
                        tag=f"{tg}_evv{tt}.{dh}")
        proj_tm2(mv, [mvn], SLOT["wvT"], WBUF["wvT"], evac_v, tg + "_pv",
                 pps=vpp)

    def kv_su(c):
        tg, C = f"P{c}", kvctx(c)
        kwT, kwTn, vt, vtn = C["kwT"], C["kwTn"], C["vt"], C["vtn"]
        for hp in range(8):
            suk = SU[hp % 2]
            sname = f"su{hp % 2}"
            for par in range(2):
                h = 2 * hp + par
                for jt in range(2):
                    first = (jt == 0 and par == 0)
                    s.emit("pe", lambda t, hp=hp, jt=jt, par=par, h=h, suk=suk: raw_matmul(
                        t, nc, suk[64 * par:64 * par + 64, 0:HN],
                        kwT[:, hp, jt, 64 * par:64 * par + 64],
                        vt[:, jt, h * HN:(h + 1) * HN],
                        start=(jt == 0), stop=(jt == 1),
                        tile_position=(0, 64 * par)),
                        reads=[kwTn, vtn] + ([sname] if first else []),
                        writes=[sname] if first else [sname + "_acc"],
                        tag=f"{tg}_su{hp}.{jt}.{par}")
            s.emit("dve", lambda v, hp=hp, suk=suk: v.scalar_tensor_tensor(
                state_f[:, hp, :], state_f[:, hp, :], ws_rep[:, hp:hp + 1],
                suk[:, 0:HN], Alu.mult, Alu.add),
                reads=["state_f", "ws_rep", sname, sname + "_acc"],
                writes=["state_f"], tag=f"{tg}_stu{hp}")

    def kv_spill(c):
        tg, C = f"P{c}", kvctx(c)
        dma("pool", k_dram[c], C["kf"], [C["kfn"]], ["k_dram"], f"{tg}_spk")
        dma("pool", kw_dram[c], C["kwT"], [C["kwTn"]], ["kw_dram"], f"{tg}_spkw")
        dma("pool", v_dram[c], C["vt"][:], [C["vtn"]], ["v_dram"], f"{tg}_spv")

    n_p1, n_p2, n_f = LIMITS if LIMITS is not None else (NCHUNK, NCHUNK, NCHUNK)
    if n_p1 > 0:
        kv_ldx(0)
        if n_p1 > 1:
            kv_ldx(1)
        kv_front(0)
        if n_p1 > 1:
            kv_front(1)
    for p in range(0, n_p1, 2):
        a, b = p, p + 1
        hb = b < n_p1
        na, nb = a + 2, b + 2
        if na < n_p1:
            kv_ldx(na)
        if hb and nb < n_p1:
            kv_ldx(nb)
        kv_mix(a)
        if hb:
            kv_mix(b)
        kv_projk(a)
        kv_projv(a)
        if hb:
            kv_projk(b)
            kv_projv(b)
        if na < n_p1:
            kv_front(na)
        if hb and nb < n_p1:
            kv_front(nb)
        kv_su(a)
        kv_spill(a)
        if hb:
            kv_su(b)
            kv_spill(b)
    tap("state_p1", state_f[:], (128, 8, HN), ["state_f"])
    P2_PE_GUARD[0] = s.counts.get("s_pe", 0)
    s.emit("dve", lambda v: v.memset(mixB2[:, 0:1, 0:1], 0.0),
           writes=["mix2"], extra_waits=[("s_pe", P2_PE_GUARD[0])], tag="ulk_m2")
    s.emit("dve", lambda v: v.memset(mixB3[:, 0:1, 0:1], 0.0),
           writes=["mix3"], extra_waits=[("s_pe", P2_PE_GUARD[0])], tag="ulk_m3")
    MIXB[2] = (mixB2, "mix2")
    MIXB[3] = (mixB3, "mix3")
    if n_p2 > 0:
        attn_chunk(xo, 0, mode="full", tag="A", phase="load")
        if n_p2 > 1:
            attn_chunk(xo, 1, mode="full", tag="A", phase="load")
    if n_p1 == NCHUNK:
        dma("pool", my1_dram.ap(),
            state_f.rearrange("p a b -> p (a b)"), ["state_f"], ["my1"], "ex1_st")
        s.emit("pool", lambda g: g.collective_compute(
            "AllGather", Alu.bypass, replica_groups=[[0, 1, 2, 3, 4, 5, 6, 7]],
            ins=[my1_dram.ap()], outs=[g1_dram.ap()]),
            reads=["my1"], writes=["g1"], tag="ex1_ag")
        prev_r = [None]
        def ex1_ld(g):
            pid = (g.partition_id() + 7) % 8
            return g.dma_start(
                out=state_f.rearrange("p a b -> p (a b)"),
                in_=g1_dram.ap().rearrange("r p c -> p r c")[:, ds(pid, 1), 0:512].squeeze(1))
        s.emit("pool", ex1_ld, reads=["g1"], writes=["state_f"], dma=True, tag="ex1_ld")
        s.emit("dve", lambda v: v.tensor_scalar(
            state_f.rearrange("p a b -> p (a b)"), state_f.rearrange("p a b -> p (a b)"),
            hmask[:, 0:1], None, Alu.mult),
            reads=["state_f", "hmask"], writes=["state_f"], tag="ex1_msk")
        for hp in range(8):
            s.emit("dve", lambda v, hp=hp: v.tensor_copy(
                state_b[:, hp, :], state_f[:, hp, :]),
                reads=["state_f"], writes=["state_b"], tag=f"ex1_cb{hp}")
    if n_p2 > 0:
        attn_chunk(xo, 0, mode="full", tag="A", phase="ldx")
        if n_p2 > 1:
            attn_chunk(xo, 1, mode="full", tag="A", phase="ldx")
        attn_chunk(xo, 0, mode="full", tag="A", phase="front")
    for c in range(n_p2):
        if 2 <= c + 1 < n_p2:
            attn_chunk(xo, c + 1, mode="full", tag="A", phase="load")
        attn_chunk(xo, c, mode="full", tag="A", phase="body")
        if c >= 1:
            attn_tail(c - 1)
            if c + 1 < n_p2:
                attn_chunk(xo, c + 1, mode="full", tag="A", phase="ldx")
        if c + 1 < n_p2:
            attn_chunk(xo, c + 1, mode="full", tag="A", phase="front")
    if n_p2 > 0:
        attn_tail(n_p2 - 1)
    tap("state_p2", state_f[:], (128, 8, HN), ["state_f"])
    if n_f > 0:
        dma("pool", warena[:, SLOT["fwrT"], :, :],
            w_dram["fwrT"].rearrange("(cc p) d -> p cc d", p=128),
            [], [WBUF["fwrT"]], "ldw_fwrT")
    if n_p2 == NCHUNK:
        ln_stats(xch_b[:, 1], ["xch1"], D, "l3p")
        ln_apply(ln3x_tm, "ln1x_tm", xch_b[:, 1], ["xch1"], "l3p")
        transpose_tm_to_fm(ln3x_fm, "ln1x_fm", ln3x_tm, ["ln1x_tm"], "l3pt")
        dma("pool", my2_dram.ap(), ln3x_fm[:, :, Q - 1], ["ln1x_fm"], ["my2"], "ex2_bn", slow=True)
        s.emit("pool", lambda g: g.collective_compute(
            "AllGather", Alu.bypass, replica_groups=[[0, 1, 2, 3, 4, 5, 6, 7]],
            ins=[my2_dram.ap()], outs=[g2_dram.ap()]),
            reads=["my2"], writes=["g2"], tag="ex2_ag")
        def ex2_lb(g):
            pid = (g.partition_id() + 7) % 8
            return g.dma_start(
                out=stO6.rearrange("p a b -> p (a b)")[:, 0:8],
                in_=g2_dram.ap().rearrange("r p c -> p r c")[:, ds(pid, 1), :].squeeze(1))
        s.emit("pool", ex2_lb, reads=["g2"], writes=["stO6"], dma=True, tag="ex2_lb")
        s.emit("dve", lambda v: v.tensor_scalar(
            bnd3.rearrange("p a b -> p (a b)"), stO6.rearrange("p a b -> p (a b)")[:, 0:8],
            hmask[:, 0:1], None, Alu.mult),
            reads=["stO6", "hmask"], writes=["bnd3"], tag="ex2_mb")
    if n_f > 0:
        for nm in ["fwkT", "fwvT"]:
            sl = SLOT[nm]
            dma("pool", warena[:, sl, :, :],
                w_dram[nm].rearrange("(cc p) d -> p cc d", p=128),
                [], [WBUF[nm]], f"ldw_{nm}")
    if n_f > 0:
        for c in range(min(4, n_f)):
            ffn_x1_load(c)
        ffn_front(0)
        if n_f > 1:
            ffn_front(1)
    for p in range(0, n_f, 2):
        a, b = p, p + 1
        hb = b < n_f
        na, nb = a + 2, b + 2
        if a + 4 < n_f:
            ffn_x1_load(a + 4)
        if hb and b + 4 < n_f:
            ffn_x1_load(b + 4)
        if na < n_f:
            ffn_front_ln(na)
        if hb and nb < n_f:
            ffn_front_ln(nb)
        ffn_body_v(a)
        if hb:
            ffn_body_v(b)
        if na < n_f:
            ffn_front_shift(na)
        if hb and nb < n_f:
            ffn_front_shift(nb)
        if na < n_f:
            ffn_front_mix(na)
        if hb and nb < n_f:
            ffn_front_mix(nb)
        if na < n_f:
            ffn_front_projr(na)
        if hb and nb < n_f:
            ffn_front_projr(nb)
        if na < n_f:
            ffn_front_projk(na)
        if hb and nb < n_f:
            ffn_front_projk(nb)
        ffn_body_ln4(a)
        if hb:
            ffn_body_ln4(b)
        ffn_body_fin(a)
        if hb:
            ffn_body_fin(b)

    fin = s.barrier()
    s.emit("sp", lambda sp: sp.nop(), extra_waits=fin, tag="final_wait")
    s.build()
    return nc


def _prep_core_inputs(inputs, b, h):
    x = inputs["L0_attn_x"]
    d = {}
    d["xo"] = np.ascontiguousarray(x[b, h * 4096:(h + 1) * 4096, :])
    d["hmask"] = np.full((128, 1), 1.0 if h == 1 else 0.0, np.float32)
    if h == 1:
        row = np.asarray(x[b, 4095, :], np.float32)
    else:
        row = np.zeros(D, np.float32)
    d["xprow"] = np.ascontiguousarray(row[None, :])
    d["xpcol"] = np.ascontiguousarray(row.reshape(8, 128).T)
    for nm, key in [("wrT", "L0_attn_w_r"), ("wkT", "L0_attn_w_k"),
                    ("wvT", "L0_attn_w_v"), ("wgT", "L0_attn_w_g"),
                    ("woT", "L0_attn_w_o"), ("fwrT", "L0_ffn_W_r"),
                    ("fwkT", "L0_ffn_W_k"), ("fwvT", "L0_ffn_W_v")]:
        d[nm] = np.ascontiguousarray(np.asarray(inputs[key]).T)
    def col8(v):
        return np.ascontiguousarray(np.asarray(v).reshape(8, 128).T)
    d["maa"] = np.ascontiguousarray(np.stack(
        [col8(inputs["L0_attn_time_maa_k"]), col8(inputs["L0_attn_time_maa_v"]),
         col8(inputs["L0_attn_time_maa_r"]), col8(inputs["L0_attn_time_maa_g"])],
        axis=2))
    d["lnw"] = np.ascontiguousarray(np.stack(
        [col8(inputs["L0_attn_ln_g"]), col8(inputs["L0_attn_ln_b"])], axis=2))
    d["mus"] = np.ascontiguousarray(np.stack(
        [col8(inputs["L0_ffn_mu_r"]), col8(inputs["L0_ffn_mu_k"])], axis=2))
    dec = np.asarray(inputs["L0_attn_time_decay"])
    u = np.asarray(inputs["L0_attn_time_faaaa"])
    rep = np.zeros((128, 8), np.float32)
    for hp in range(8):
        rep[0:64, hp] = dec[2 * hp]
        rep[64:128, hp] = dec[2 * hp + 1]
    d["dec_rep"] = rep
    d["dec_full"] = np.ascontiguousarray(np.tile(dec[None, :], (128, 1)).astype(np.float32))
    d["u_full"] = np.ascontiguousarray(np.tile(u[None, :], (128, 1)).astype(np.float32))
    return d


def kernel(**inputs):
    nc = build_nc()
    in_maps = []
    for b in range(4):
        for h in range(2):
            in_maps.append(_prep_core_inputs(inputs, b, h))
    res = run_bass_kernel_spmd(nc, in_maps, core_ids=list(range(8)))
    out = np.zeros((4, 8192, D), np.float32)
    i = 0
    for b in range(4):
        for h in range(2):
            out[b, h * 4096:(h + 1) * 4096, :] = res.results[i]["out"]
            i += 1
    return out


# version 22
# speedup vs baseline: 1.0246x; 1.0246x over previous
import numpy as np
import concourse.bass as bass
import concourse.mybir as mybir
from concourse.bass_utils import run_bass_kernel_spmd
ds = bass.ds

F32 = mybir.dt.float32
BF16 = mybir.dt.bfloat16
Act = mybir.ActivationFunctionType
Alu = mybir.AluOpType
AXX = mybir.AxisListType.X

NCHUNK = 16
Q = 256
D = 1024
H = 16
HN = 64
EPS_LN = 1e-5
EPS_GN = 1e-5 * 64

WANT_TAPS = {}
LIMITS = None
STAGES = None

ENGS = ("pe", "dve", "act", "pool", "sp")
ENG_SEM = {"pe": "s_pe", "dve": "s_dv", "act": "s_ac", "pool": "s_po", "sp": "s_sp"}


def raw_matmul(t, nc, out, lhsT, rhs, start, stop, tile_position=None, ldw=True):
    if ldw:
        return t.matmul(out, lhsT, rhs, start=start, stop=stop,
                        tile_position=tile_position)
    m = mybir.InstMatmult(
        name=f"I-{nc.next_id()}",
        ins=[t.lower_ap(rhs), t.lower_ap(lhsT, for_matmul_weights=True)],
        outs=[t.lower_ap(out)],
        replication_resolution=0, replication_shift_amnt=0, replication_num_rows=0,
        start_tensor_calc=start, stop_tensor_calc=stop,
        tile_size=(lhsT.shape[0], lhsT.shape[-1]),
        tile_position=tile_position if tile_position is not None else (0, 0),
        ldweights=False,
    )
    return t.add_instruction(m)


class Sched:

    NDMA_LANES = 4

    def __init__(self, nc):
        self.nc = nc
        self.ops = {e: [] for e in ENGS}
        self.sems = {}
        self.counts = {}
        self.bufs = {}
        self.dma_rr = {e: 0 for e in ENGS}

    def sem(self, name):
        if name not in self.sems:
            self.sems[name] = self.nc.alloc_semaphore(name)
            self.counts[name] = 0
        return self.sems[name]

    def _buf(self, name):
        if name not in self.bufs:
            self.bufs[name] = {"w": None, "r": []}
        return self.bufs[name]

    def emit(self, eng, fn, reads=(), writes=(), extra_waits=(), dma=False, tag=""):
        if dma:
            lane = self.dma_rr[eng] % self.NDMA_LANES
            self.dma_rr[eng] += 1
            inc_sem = f"s_dma_{eng}{lane}"
        else:
            inc_sem = ENG_SEM[eng]
        self.sem(inc_sem)
        waits = list(extra_waits)
        if dma and self.counts[inc_sem] > 0:
            waits.append((inc_sem, self.counts[inc_sem]))
        def need(dep_eng, tok):
            if dep_eng == "pe" and eng == "pe" and not self._is_dma_tok(tok):
                return False
            return True
        PSUM_NAMES = ("pp0", "pp1", "yps", "sp0", "sp1", "su0", "su1")
        for bn in reads:
            b = self._buf(bn)
            if b["w"] is not None:
                weng, wtok = b["w"]
                if need(weng, wtok):
                    waits.append(wtok)
            if bn.split("_acc")[0] in PSUM_NAMES and eng in ("dve", "act"):
                other = "act" if eng == "dve" else "dve"
                for reng, rtok in b["r"]:
                    if reng == other:
                        waits.append(rtok)
        for bn in writes:
            b = self._buf(bn)
            if b["w"] is not None:
                weng, wtok = b["w"]
                if need(weng, wtok):
                    waits.append(wtok)
            for reng, rtok in b["r"]:
                if need(reng, rtok):
                    waits.append(rtok)
        wmap = {}
        for (sname, v) in waits:
            assert v <= self.counts[sname], f"wait {sname}>={v} not yet emitted (tag={tag})"
            wmap[sname] = max(wmap.get(sname, 0), v)
        amt = 16 if dma else 1
        self.counts[inc_sem] += amt
        tok = (inc_sem, self.counts[inc_sem])
        self.ops[eng].append((fn, tuple(sorted(wmap.items())), inc_sem, amt, tag))
        for bn in reads:
            self._buf(bn)["r"].append((eng, tok))
        for bn in writes:
            b = self._buf(bn)
            b["w"] = (eng, tok)
            b["r"] = []
        return tok

    @staticmethod
    def _is_dma_tok(tok):
        return tok[0].startswith("s_dma_")

    def barrier(self):
        return [(s, c) for s, c in self.counts.items() if c > 0]

    def build(self):
        nc = self.nc
        with nc.Block() as block:
            def mk(eng_key):
                def body(eng):
                    last = {}
                    for (fn, waits, inc, amt, tag) in self.ops[eng_key]:
                        for (s, v) in waits:
                            if last.get(s, -1) >= v:
                                continue
                            eng.wait_ge(self.sems[s], v)
                            last[s] = v
                        ins = fn(eng)
                        ins.then_inc(self.sems[inc], amt)
                return body
            block.tensor(mk("pe"))
            block.vector(mk("dve"))
            block.scalar(mk("act"))
            block.gpsimd(mk("pool"))
            block.sync(mk("sp"))


def build_nc(debug_taps=None):
    taps = dict(WANT_TAPS)
    if debug_taps:
        taps.update(debug_taps)
    nc = bass.Bass("TRN2", target_bir_lowering=False)
    s = Sched(nc)

    xo = nc.dram_tensor("xo", [4096, D], F32, kind="ExternalInput").ap()
    hmask_in = nc.dram_tensor("hmask", [128, 1], F32, kind="ExternalInput").ap()
    xprow_in = nc.dram_tensor("xprow", [1, D], F32, kind="ExternalInput").ap()
    xpcol_in = nc.dram_tensor("xpcol", [128, 8], F32, kind="ExternalInput").ap()
    k_dram = nc.dram_tensor("k_spill", [NCHUNK, 128, 8, Q], BF16, kind="Internal").ap()
    v_dram = nc.dram_tensor("v_spill", [NCHUNK, 128, 2, D], BF16, kind="Internal").ap()
    kw_dram = nc.dram_tensor("kw_spill", [NCHUNK, 128, 8, 2, 128], BF16, kind="Internal").ap()
    l1_dram = nc.dram_tensor("l1_spill", [NCHUNK, 128, 8, Q], BF16, kind="Internal").ap()
    my1_dram = nc.dram_tensor("my1", [128, 512], F32, kind="Internal")
    g1_dram = nc.dram_tensor("g1", [8, 128, 512], F32, kind="Internal")
    my2_dram = nc.dram_tensor("my2", [128, 8], F32, kind="Internal")
    g2_dram = nc.dram_tensor("g2", [8, 128, 8], F32, kind="Internal")
    w_names = ["wrT", "wkT", "wvT", "wgT", "woT", "fwrT", "fwkT", "fwvT"]
    w_dram = {n: nc.dram_tensor(n, [D, D], F32, kind="ExternalInput").ap() for n in w_names}
    maa_in = nc.dram_tensor("maa", [128, 8, 4], F32, kind="ExternalInput").ap()
    lnw_in = nc.dram_tensor("lnw", [128, 8, 2], F32, kind="ExternalInput").ap()
    mus_in = nc.dram_tensor("mus", [128, 8, 2], F32, kind="ExternalInput").ap()
    dec_rep_in = nc.dram_tensor("dec_rep", [128, 8], F32, kind="ExternalInput").ap()
    dec_full_in = nc.dram_tensor("dec_full", [128, 16], F32, kind="ExternalInput").ap()
    u_full_in = nc.dram_tensor("u_full", [128, 16], F32, kind="ExternalInput").ap()
    out_d = nc.dram_tensor("out", [4096, D], F32, kind="ExternalOutput").ap()
    x1_dram = nc.dram_tensor("x1_scratch", [4096, D], F32, kind="Internal").ap()

    jj = np.arange(128)[:, None]
    ii = np.arange(128)[None, :]
    EA_np = np.where(jj < ii, ii - jj - 1, -1.0).astype(np.float32)
    MK_np = (jj < ii).astype(np.float32)
    EB_np = (ii + 128 - jj - 1).astype(np.float32)
    EYE_np = np.eye(128, dtype=np.float32)
    IOTA_np = np.tile(np.arange(Q, dtype=np.float32), (128, 1))
    ONES1_np = np.ones((1, 128), dtype=np.float32)
    EA_t = nc.inline_tensor(EA_np, name="EA").ap()
    MKm_t = nc.inline_tensor(MK_np, name="MKm").ap()
    EB_t = nc.inline_tensor(EB_np, name="EB").ap()
    EYE_t = nc.inline_tensor(EYE_np, name="EYE").ap()
    IOTA_t = nc.inline_tensor(IOTA_np, name="IOTA").ap()
    ONES1_t = nc.inline_tensor(ONES1_np, name="ONES1").ap()

    A = nc.alloc_sbuf_tensor
    warena = A("warena", [128, 5, 8, D], BF16)
    SLOT = {"wkT": 0, "wvT": 1, "wrT": 2, "wgT": 3, "woT": 4,
            "fwrT": 2, "fwkT": 3, "fwvT": 4}
    WBUF = {"wkT": "wsl0", "wvT": "wsl1", "wrT": "wsl2", "wgT": "wsl3", "woT": "wsl4",
            "fwrT": "wsl2", "fwkT": "wsl3", "fwvT": "wsl4"}

    y_sb2 = None
    colA = A("colA", [128, 8, 4], F32)
    colB = A("colB", [128, 8, 2], F32)
    colC = A("colC", [128, 8, 2], F32)
    dcr = A("dcr", [128, 8], F32)
    dcf = A("dcf", [128, 16], F32)
    ufl = A("ufl", [128, 16], F32)
    logw_r = A("logw_r", [128, 8], F32)
    logw_f = A("logw_f", [128, 16], F32)
    ws_rep = A("ws_rep", [128, 8], F32)
    tsc = A("tsc", [128, Q], F32)
    tsc2 = A("tsc2", [128, 128], F32)
    wmat = A("wmat", [128, H, Q], BF16)
    m_fm = A("m_fm", [128, 8, Q], BF16)
    mk_fm = A("mk_fm", [128, 8, Q], BF16)

    hmask = A("hmask_s", [128, 1], F32)
    bnd_hist = A("bnd_hist", [128, 8, NCHUNK], BF16)
    state_f = A("state_f", [128, 8, HN], F32)
    state_b = A("state_b", [128, 8, HN], BF16)
    state_bB = A("state_bB", [128, 8, HN], BF16)

    xch_b = A("xch_b", [128, 2, 2, D], F32)
    ln1x_tm = A("ln1x_tm", [128, 2, D], BF16)
    ln1x_fm = A("ln1x_fm", [128, 8, Q], BF16)
    bnd1 = A("bnd1", [128, 8, 1], BF16)
    bnd3 = A("bnd3", [128, 8, 1], BF16)
    dfm = A("dfm", [128, 8, Q], BF16)
    mix0 = A("mix0", [128, 8, Q], BF16)
    mix1 = A("mix1", [128, 8, Q], BF16)
    MIXB = [(mix0, "mix0"), (mix1, "mix1")]
    y_sb = A("y_sb", [128, D], F32)
    consts = y_sb.rearrange("p (a b) -> p a b", a=8)[:, 0:6, :]
    iota = y_sb[:, 640:896]
    _sl0 = warena[:, 0]
    _sl1 = warena[:, 1]
    ln1x_fmB = _sl0[:, :, 0:256]
    kwkTB = _sl0[:, :, 256:512].rearrange("p a (b c) -> p a b c", b=2)
    dfmB = _sl0[:, :, 512:768]
    k_fmB = _sl0[:, :, 768:1024]
    v_tmB = _sl1[:, 0:2, :]
    mixB2 = _sl1[:, 2:4, :].rearrange("p a (b c) -> p (a b) c", c=256)
    mixB3 = _sl1[:, 4:6, :].rearrange("p a (b c) -> p (a b) c", c=256)
    r_fmB = _sl1[:, 6:8, :].rearrange("p a (b c) -> p (a b) c", c=256)
    rp_tmB = None
    r_fm = A("r_fm", [128, 8, Q], BF16)
    k_fm = A("k_fm", [128, 8, Q], BF16)
    g_fm = A("g_fm", [128, 8, Q], BF16)
    v_tm = A("v_tm", [128, 2, D], BF16)
    kwk_fm = A("kwk_fm", [128, 8, Q], BF16)
    kwkT = A("kwkT", [128, 8, 2, 128], BF16)
    r_wb = A("r_wb", [128, 8, Q], BF16)
    sbar = A("sbar", [128, H, 384], BF16)
    sqs = A("sqs", [128, D], F32)
    stF = A("stF", [128, 2, 2], F32)
    stFb = A("stFb", [128, 4, 6], F32)
    stFr = A("stFr", [128, 2], F32)
    stFn = A("stFn", [128, 2], F32)
    st_sum = A("st_sum", [128, 16], F32)
    st_sq = A("st_sq", [128, 16], F32)
    st_mu = A("st_mu", [128, 16], F32)
    st_var = A("st_var", [128, 16], F32)
    st_rstd = A("st_rstd", [128, 16], F32)
    st_nmr = A("st_nmr", [128, 16], F32)
    stO6 = A("stO6", [128, 4, 6], F32)
    stOw = A("stOw", [128, 2, 2], F32)
    stOr = A("stOr", [128, 2], F32)
    stOn = A("stOn", [128, 2], F32)
    stFb2 = A("stFb2", [128, 4, 6], F32)
    stF2 = A("stF2", [128, 2, 2], F32)
    stFr2 = A("stFr2", [128, 2], F32)
    stFn2 = A("stFn2", [128, 2], F32)
    stO6b = A("stO6b", [128, 4, 6], F32)
    stOwb = A("stOwb", [128, 2, 2], F32)
    stOrb = A("stOrb", [128, 2], F32)
    stOnb = A("stOnb", [128, 2], F32)
    stP0 = A("stP0", [128, 2, 2], F32)
    yn_tm = A("yn_tm", [128, 2, D], BF16)
    yn_fm = A("yn_fm", [128, 8, Q], BF16)
    rp_tmB_v = yn_fm.rearrange("p a b -> p (a b)").rearrange("p (a b) -> p a b", a=2)
    yg_fm = A("yg_fm", [128, 8, Q], BF16)
    x1s = A("x1s", [128, 2, D], F32)
    ln3x_tm = ln1x_tm
    ln3x_fm = ln1x_fm
    rp_tm = yn_tm
    kact_fm = k_fm
    och = sbar.bitcast(F32).rearrange("p a b -> p (a b)")[:, 0:2048].rearrange(
        "p (a b) -> p a b", a=2)
    ln3x_tmB = kwk_fm.rearrange("p a b -> p (a b)").rearrange(
        "p (t d) -> p t d", t=2)
    yn_tmB = mk_fm.rearrange("p a b -> p (a b)").rearrange(
        "p (t d) -> p t d", t=2)
    kwkTB_P1 = r_wb.rearrange("p a (b c) -> p a b c", b=2)
    v_tmB_P1 = sbar.rearrange("p h q -> p (h q)")[:, 0:2048].rearrange(
        "p (t d) -> p t d", t=2)
    _x1bf = x1s.bitcast(BF16).rearrange("p a b -> p (a b)")
    x1mA = _x1bf[:, 0:2048].rearrange("p (a b) -> p a b", a=8)
    x1mB = _x1bf[:, 2048:4096].rearrange("p (a b) -> p a b", a=8)
    MIXB.append((x1mA, "x1mA"))
    MIXB.append((x1mB, "x1mB"))
    gv_x1 = g_fm.bitcast(F32).rearrange("p a b -> p (a b)")
    ygv_x1 = yg_fm.bitcast(F32).rearrange("p a b -> p (a b)")
    rwv_x1 = r_wb.bitcast(F32).rearrange("p a b -> p (a b)")
    mfv_x1 = m_fm.bitcast(F32).rearrange("p a b -> p (a b)")
    mkv_x1 = mk_fm.bitcast(F32).rearrange("p a b -> p (a b)")
    rfv_x1 = r_fm.bitcast(F32).rearrange("p a b -> p (a b)")
    vtv_x1 = v_tm.bitcast(F32).rearrange("p a b -> p (a b)")
    kTv_x1 = kwkT.bitcast(F32).rearrange("p a b c -> p (a b c)")

    pp0 = nc.alloc_psum_tensor("pp0", [128, 512], F32)
    pp1 = nc.alloc_psum_tensor("pp1", [128, 512], F32)
    yps = nc.alloc_psum_tensor("yps", [128, 1024], F32)
    sp0 = nc.alloc_psum_tensor("sp0", [128, 512], F32)
    sp1 = nc.alloc_psum_tensor("sp1", [128, 512], F32)
    su0 = nc.alloc_psum_tensor("su0", [128, 512], F32)
    su1 = nc.alloc_psum_tensor("su1", [128, 512], F32)
    PP = [pp0, pp1]
    SU = [su0, su1]

    def tap(name, src_ap, shape, reads):
        if name not in taps:
            return
        d = nc.dram_tensor("tap_" + name, list(shape), F32, kind="ExternalOutput").ap()
        s.emit("pool", lambda g, a=src_ap, d=d: g.dma_start(out=d, in_=a),
               reads=list(reads), writes=[], dma=True, tag="tapdma_" + name)

    def dma(eng, out_ap, in_ap, reads, writes, tag, slow=False):
        return s.emit(eng, lambda e, o=out_ap, i=in_ap: e.dma_start(
            out=o, in_=i, allow_slow_non_contiguous=slow),
            reads=reads, writes=writes, dma=True, tag=tag)

    for idx, nm in enumerate(["wkT", "wvT", "wrT", "wgT", "woT"]):
        sl = SLOT[nm]
        dma("pool", warena[:, sl, :, :],
            w_dram[nm].rearrange("(cc p) d -> p cc d", p=128),
            [], [WBUF[nm]], f"ldw_{nm}")
    dma("sp", colA[:], maa_in, [], ["colA"], "ld_maa")
    dma("sp", hmask[:], hmask_in, [], ["hmask"], "ld_hm")
    dma("sp", colB[:], lnw_in, [], ["colB"], "ld_lnw")
    dma("sp", colC[:], mus_in, [], ["colC_raw"], "ld_mus")
    dma("sp", dcr[:], dec_rep_in, [], ["dcr"], "ld_dcr")
    dma("sp", dcf[:], dec_full_in, [], ["dcf"], "ld_dcf")
    dma("sp", ufl[:], u_full_in, [], ["ufl"], "ld_ufl")
    dma("sp", consts[:, 0, :], EA_t, [], ["cEA"], "ld_EA")
    dma("sp", consts[:, 1, :], MKm_t, [], ["cMK"], "ld_MK")
    dma("sp", consts[:, 2, :], EB_t, [], ["cEB"], "ld_EB")
    dma("sp", consts[:, 3, :], EYE_t, [], ["cEYE"], "ld_EYE")
    dma("sp", iota[:], IOTA_t, [], ["iota"], "ld_IOTA")

    s.emit("act", lambda a: a.activation(logw_r[:], dcr[:], Act.Exp),
           reads=["dcr"], writes=["logw_r"], tag="exp_dcr")
    s.emit("dve", lambda v: v.tensor_scalar(logw_r[:], logw_r[:], -1.0, None, Alu.mult),
           reads=["logw_r"], writes=["logw_r"], tag="neg_dcr")
    s.emit("act", lambda a: a.activation(logw_f[:], dcf[:], Act.Exp),
           reads=["dcf"], writes=["logw_f"], tag="exp_dcf")
    s.emit("dve", lambda v: v.tensor_scalar(logw_f[:], logw_f[:], -1.0, None, Alu.mult),
           reads=["logw_f"], writes=["logw_f"], tag="neg_dcf")
    s.emit("act", lambda a: a.activation(ws_rep[:], logw_r[:], Act.Exp, scale=float(Q)),
           reads=["logw_r"], writes=["ws_rep"], tag="ws_rep")
    s.emit("dve", lambda v: v.tensor_scalar(colC[:], colC[:], -1.0, 1.0, Alu.mult, Alu.add),
           reads=["colC_raw"], writes=["colC"], tag="colC")

    for cc in range(8):
        s.emit("dve", lambda v, cc=cc: v.tensor_scalar(
            tsc[:], iota[:], logw_r[:, cc:cc + 1], None, Alu.mult),
            reads=["iota", "logw_r"], writes=["tsc"], tag=f"m_mul{cc}")
        s.emit("act", lambda a, cc=cc: a.activation(m_fm[:, cc, :], tsc[:], Act.Exp),
               reads=["tsc"], writes=["m_fm"], tag=f"m_exp{cc}")
        s.emit("dve", lambda v, cc=cc: v.tensor_scalar(
            tsc[:], iota[:], -1.0, float(Q - 1), Alu.mult, Alu.add),
            reads=["iota"], writes=["tsc"], tag=f"mk_rev{cc}")
        s.emit("dve", lambda v, cc=cc: v.tensor_scalar(
            tsc[:], tsc[:], logw_r[:, cc:cc + 1], None, Alu.mult),
            reads=["tsc", "logw_r"], writes=["tsc"], tag=f"mk_mul{cc}")
        s.emit("act", lambda a, cc=cc: a.activation(mk_fm[:, cc, :], tsc[:], Act.Exp),
               reads=["tsc"], writes=["mk_fm"], tag=f"mk_exp{cc}")

    for h in range(H):
        s.emit("dve", lambda v, h=h: v.tensor_scalar(
            tsc2[:], consts[:, 0, :], logw_f[:, h:h + 1], None, Alu.mult),
            reads=["cEA", "logw_f"], writes=["tsc2"], tag=f"wa1_{h}")
        s.emit("act", lambda a, h=h: a.activation(tsc2[:], tsc2[:], Act.Exp),
               reads=["tsc2"], writes=["tsc2"], tag=f"wa2_{h}")
        s.emit("dve", lambda v, h=h: v.tensor_tensor(
            tsc2[:], tsc2[:], consts[:, 1, :], Alu.mult),
            reads=["tsc2", "cMK"], writes=["tsc2"], tag=f"wa3_{h}")
        s.emit("dve", lambda v, h=h: v.tensor_scalar(
            consts[:, 4, :], consts[:, 3, :], ufl[:, h:h + 1], None, Alu.mult),
            reads=["cEYE", "ufl"], writes=["cU"], tag=f"wa4_{h}")
        s.emit("dve", lambda v, h=h: v.tensor_tensor(
            wmat[:, h, 0:128], tsc2[:], consts[:, 4, :], Alu.add),
            reads=["tsc2", "cU"], writes=["wmat"], tag=f"wa5_{h}")
        s.emit("dve", lambda v, h=h: v.tensor_scalar(
            tsc2[:], consts[:, 2, :], logw_f[:, h:h + 1], None, Alu.mult),
            reads=["cEB", "logw_f"], writes=["tsc2"], tag=f"wb1_{h}")
        s.emit("act", lambda a, h=h: a.activation(wmat[:, h, 128:256], tsc2[:], Act.Exp),
               reads=["tsc2"], writes=["wmat"], tag=f"wb2_{h}")

    s.emit("dve", lambda v: v.memset(state_f[:], 0.0), writes=["state_f"], tag="z_sf")
    s.emit("dve", lambda v: v.memset(state_b[:], 0.0), writes=["state_b"], tag="z_sb")
    s.emit("dve", lambda v: v.memset(bnd1[:], 0.0), writes=["bnd1"], tag="z_b1")
    s.emit("dve", lambda v: v.memset(bnd3[:], 0.0), writes=["bnd3"], tag="z_b3")

    rowbuf = och
    dma("sp", rowbuf[0:1, 0, :], xprow_in, [], ["och"], "br_ldrow")
    dma("sp", stO6.rearrange("p a b -> p (a b)")[:, 0:8], xpcol_in, [], ["stO6"], "br_ldcol")
    dma("sp", rowbuf[0:1, 1, 0:128], ONES1_t, [], ["ones1"], "br_ldone")
    s.emit("dve", lambda v: v.tensor_reduce(
        stP0[0:1, 0, 0:1], rowbuf[0:1, 0, :], AXX, Alu.add),
        reads=["och"], writes=["stP0"], tag="br_sum")
    s.emit("act", lambda a: a.activation(
        rowbuf[0:1, 1, 128:128 + D - 128], rowbuf[0:1, 0, 0:D - 128], Act.Square,
        accum_out=stP0[0:1, 0, 1:2]),
        reads=["och"], writes=["och2", "stP0"], tag="br_sq1")
    s.emit("act", lambda a: a.activation(
        rowbuf[0:1, 0, 0:128], rowbuf[0:1, 0, D - 128:D], Act.Square,
        accum_out=stP0[0:1, 1, 0:1]),
        reads=["och"], writes=["och3", "stP0"], tag="br_sq2")
    s.emit("dve", lambda v: v.tensor_tensor(
        stP0[0:1, 0, 1:2], stP0[0:1, 0, 1:2], stP0[0:1, 1, 0:1], Alu.add),
        reads=["stP0", "och2", "och3"], writes=["stP0"], tag="br_sqs")
    s.emit("dve", lambda v: v.tensor_scalar(
        stP0[0:1, 0, 0:1], stP0[0:1, 0, 0:1], 1.0 / D, None, Alu.mult),
        reads=["stP0"], writes=["stP0"], tag="br_mu")
    s.emit("dve", lambda v: v.tensor_tensor(
        stP0[0:1, 1, 1:2], stP0[0:1, 0, 0:1], stP0[0:1, 0, 0:1], Alu.mult),
        reads=["stP0"], writes=["stP0"], tag="br_mu2")
    s.emit("dve", lambda v: v.tensor_scalar(
        stP0[0:1, 0, 1:2], stP0[0:1, 0, 1:2], 1.0 / D, None, Alu.mult),
        reads=["stP0"], writes=["stP0"], tag="br_v1")
    s.emit("dve", lambda v: v.tensor_tensor(
        stP0[0:1, 0, 1:2], stP0[0:1, 0, 1:2], stP0[0:1, 1, 1:2], Alu.subtract),
        reads=["stP0"], writes=["stP0"], tag="br_v2")
    s.emit("dve", lambda v: v.tensor_scalar(
        stP0[0:1, 0, 1:2], stP0[0:1, 0, 1:2], EPS_LN, None, Alu.add),
        reads=["stP0"], writes=["stP0"], tag="br_v3")
    s.emit("act", lambda a: a.activation(
        stP0[0:1, 1, 0:1], stP0[0:1, 0, 1:2], Act.Sqrt),
        reads=["stP0"], writes=["stP0"], tag="br_sd")
    s.emit("dve", lambda v: v.reciprocal(stP0[0:1, 1, 0:1], stP0[0:1, 1, 0:1]),
           reads=["stP0"], writes=["stP0"], tag="br_rs")
    s.emit("dve", lambda v: v.tensor_scalar(
        stP0[0:1, 1, 1:2], stP0[0:1, 0, 0:1], stP0[0:1, 1, 0:1], -1.0, Alu.mult, Alu.mult),
        reads=["stP0"], writes=["stP0"], tag="br_nm")
    s.emit("pe", lambda t: t.matmul(
        sp0[:, 0:2], rowbuf[0:1, 1, 0:128], stP0[0:1, 1, 0:2],
        start=True, stop=True),
        reads=["ones1", "stP0", "sp0"], writes=["sp0"], tag="br_bc")
    s.emit("dve", lambda v: v.tensor_scalar(
        bnd1.rearrange("p a b -> p (a b)"),
        stO6.rearrange("p a b -> p (a b)")[:, 0:8],
        sp0[:, 0:1], sp0[:, 1:2], Alu.mult, Alu.add),
        reads=["stO6", "sp0", "hmask"], writes=["bnd1"], tag="br_ap")

    def ln_stats_tt(get_src, src_names, G, tt, tag):
        b6, b6n, a2, a2n, w, wn, rr, rrn, nn_, nnn = G
        for g in range(2):
            s.emit("dve", lambda v, tt=tt, g=g: v.bn_stats(
                b6[:, tt * 2 + g, :], get_src(tt, g)),
                reads=src_names, writes=[f"{b6n}.{tt}"], tag=f"{tag}_bs{tt}.{g}")
        s.emit("dve", lambda v, tt=tt: v.bn_aggr(
            w[:, tt, 0:2],
            b6[:, tt * 2:(tt + 1) * 2, :].rearrange("p a b -> p (a b)")),
            reads=[f"{b6n}.{tt}"], writes=[f"{wn}.{tt}"], tag=f"{tag}_ag{tt}")

    def ln_fin(G, tt, tag):
        b6, b6n, a2, a2n, w, wn, rr, rrn, nn_, nnn = G
        s.emit("dve", lambda v, tt=tt: v.tensor_scalar(
            w[:, tt, 1:2], w[:, tt, 1:2], EPS_LN, None, Alu.add),
            reads=[f"{wn}.{tt}"], writes=[f"{wn}.{tt}"], tag=f"{tag}_ep{tt}")
        s.emit("act", lambda a, tt=tt: a.activation(
            rr[:, tt:tt + 1], w[:, tt, 1:2], Act.Sqrt),
            reads=[f"{wn}.{tt}"], writes=[f"{rrn}.{tt}"], tag=f"{tag}_sd{tt}")
        s.emit("dve", lambda v, tt=tt: v.reciprocal(rr[:, tt:tt + 1], rr[:, tt:tt + 1]),
               reads=[f"{rrn}.{tt}"], writes=[f"{rrn}.{tt}"], tag=f"{tag}_rs{tt}")
        s.emit("dve", lambda v, tt=tt: v.tensor_scalar(
            nn_[:, tt:tt + 1], w[:, tt, 0:1], rr[:, tt:tt + 1], -1.0,
            Alu.mult, Alu.mult),
            reads=[f"{wn}.{tt}", f"{rrn}.{tt}"], writes=[f"{nnn}.{tt}"],
            tag=f"{tag}_nm{tt}")

    GF = (stFb, "stFb", None, "", stF, "stF", stFr, "stFr", stFn, "stFn")
    GO = (stO6, "stO6", None, "", stOw, "stOw", stOr, "stOr", stOn, "stOn")
    GFb = (stFb2, "stFb2", None, "", stF2, "stF2", stFr2, "stFr2", stFn2, "stFn2")
    GOb = (stO6b, "stO6b", None, "", stOwb, "stOwb", stOrb, "stOrb", stOnb, "stOnb")

    def ln_stats(buf_f32_ap, src_names, nfeat, tag, G=GF):
        for tt in range(2):
            ln_stats_tt(lambda tt, g: buf_f32_ap[:, tt, g * 512:(g + 1) * 512],
                        src_names, G, tt, tag)
            ln_fin(G, tt, tag)

    def ln_apply(dst_ap, dst_name, src_ap, src_names, tag, G=GF):
        b6, b6n, a2, a2n, w, wn, rr, rrn, nn_, nnn = G
        for tt in range(2):
            s.emit("act", lambda a, tt=tt: a.activation(
                dst_ap[:, tt, :], src_ap[:, tt, :], Act.Identity,
                bias=nn_[:, tt:tt + 1], scale=rr[:, tt:tt + 1]),
                reads=list(src_names) + [f"{rrn}.{tt}", f"{nnn}.{tt}"],
                writes=[dst_name], tag=f"{tag}_ap{tt}")

    def transpose_tm_to_fm(dst_store, dst_name, src_ap, src_names, tag):
        for tt in range(2):
            s.emit("sp", lambda a, tt=tt: a.dma_start(
                out=dst_store[:, :, tt * 128:(tt + 1) * 128],
                in_=src_ap[:, tt, :], transpose=True),
                reads=src_names, writes=[dst_name], dma=True, tag=f"{tag}_tr{tt}")

    def mk_shift_delta(d_name, d_ap, src_ap, src_name, bnd_ap, bnd_name, tag):
        s.emit("dve", lambda v: v.tensor_tensor(
            d_ap[:, :, 1:Q], src_ap[:, :, 0:Q - 1], src_ap[:, :, 1:Q], Alu.subtract),
            reads=[src_name], writes=[d_name], tag=f"{tag}_d")
        s.emit("dve", lambda v: v.tensor_tensor(
            d_ap[:, :, 0:1], bnd_ap[:], src_ap[:, :, 0:1], Alu.subtract),
            reads=[src_name, bnd_name], writes=[d_name], tag=f"{tag}_d0")

    mix_rr = [0]
    mix_n = [4]
    def mk_mix(coef_ap, coef_names, base_ap, base_names, d_ap, d_names, tag):
        mt, mn = MIXB[mix_rr[0] % mix_n[0]]
        mix_rr[0] += 1
        for cc in range(8):
            s.emit("dve", lambda v, cc=cc, mt=mt: v.scalar_tensor_tensor(
                mt[:, cc, :], d_ap[:, cc, :], coef_ap[:, cc:cc + 1],
                base_ap[:, cc, :], Alu.mult, Alu.add),
                reads=list(d_names) + list(base_names) + list(coef_names),
                writes=[mn], tag=f"{tag}_{cc}")
        return mt, mn

    def proj_fm(mt, mn, wslot, wname, evac, tag, pps=None):
        if pps is None:
            pps = [(pp0[:, 0:512], "pp0"), (pp1[:, 0:512], "pp1")]
        for dc in range(8):
            p, pname = pps[dc % 2]
            for kc in range(8):
                s.emit("pe", lambda t, dc=dc, kc=kc, p=p, mt=mt: raw_matmul(
                    t, nc, p[:, 0:Q],
                    warena[:, wslot, kc, dc * 128:(dc + 1) * 128],
                    mt[:, kc, :], start=(kc == 0), stop=(kc == 7)),
                    reads=[mn, wname] + ([pname] if kc == 0 else []),
                    writes=[pname] if kc == 0 else [pname + "_acc"],
                    tag=f"{tag}_mm{dc}.{kc}")
            evac(dc, p, pname)

    def proj_tm2(stat_ap, stat_names, wslot, wname, evac, tag, pps=None):
        if pps is None:
            pps = [(pp0[:, 0:512], "pp0"), (pp1[:, 0:512], "pp1")]
        for tt in range(2):
            for kc in range(8):
                for dh in range(2):
                    p, pname = pps[dh]
                    s.emit("pe", lambda t, tt=tt, dh=dh, kc=kc, p=p: raw_matmul(
                        t, nc, p[:, 0:512],
                        stat_ap[:, kc, tt * 128:(tt + 1) * 128],
                        warena[:, wslot, kc, dh * 512:(dh + 1) * 512],
                        start=(kc == 0), stop=(kc == 7), ldw=(dh == 0)),
                        reads=list(stat_names) + [wname] + ([pname] if kc == 0 else []),
                        writes=[pname] if kc == 0 else [pname + "_acc"],
                        tag=f"{tag}_mm{tt}.{dh}.{kc}")
            evac(tt)

    P2_PE_GUARD = [0]
    FCTX = {}
    TCTX = {}
    def attn_chunk(src_dram, c, mode, tag, phase=None):
        full = (mode == "full")
        tg = f"{tag}{c}"
        xch = xch_b[:, c % 2]
        xchn = f"xch{c % 2}"
        def st(name):
            return STAGES is None or name in STAGES
        if full and phase == "body":
            C = FCTX.pop(c)
            xch, xchn = C["xch"], C["xchn"]
            ln1x_fm_c, ln1x_fm_cn = C["l1"], C["l1n"]
            dfm_c, dfm_cn = C["d"], C["dn"]
            k_fm_c, k_fm_cn = C["k"], C["kn"]
            v_tm_c, v_tm_cn = C["v"], C["vn"]
            kwkT_c, kwkT_cn = C["kw"], C["kwn"]
            r_c, r_cn = C["r"], C["rn"]
        elif full:
            if c % 2 == 0:
                f_l1, f_l1n = ln1x_fm[:], "ln1x_fm"
                f_d, f_dn = dfm[:], "dfm"
                f_k, f_kn = k_fm[:], "k_fm"
                f_v, f_vn = v_tm[:], "v_tm"
                f_kw, f_kwn = kwkT[:], "kwkT"
            else:
                f_l1, f_l1n = ln1x_fmB, "ln1x_fmB"
                f_d, f_dn = dfmB, "dfmB"
                f_k, f_kn = k_fmB, "k_fmB"
                f_v, f_vn = v_tmB, "v_tmB"
                f_kw, f_kwn = kwkTB, "kwkTB"
            if phase == "ldx":
                dma("sp", xch, src_dram[c * Q:(c + 1) * Q, :].rearrange("(tt p) d -> p tt d", p=128),
                    [], [xchn], f"{tg}_ldx")
                return
            if phase == "load":
                gw = [("s_pe", P2_PE_GUARD[0])] if (c % 2 == 1 and c <= 2) else []
                s.emit("sp", lambda e, c=c, f_l1=f_l1: e.dma_start(out=f_l1, in_=l1_dram[c]),
                       reads=[], writes=[f_l1n], extra_waits=gw, dma=True, tag=f"{tg}_ldl1")
                s.emit("pool", lambda e, c=c, f_k=f_k: e.dma_start(out=f_k, in_=k_dram[c]),
                       reads=[], writes=[f_kn], extra_waits=gw, dma=True, tag=f"{tg}_ldk")
                s.emit("pool", lambda e, c=c, f_v=f_v: e.dma_start(out=f_v, in_=v_dram[c]),
                       reads=[], writes=[f_vn], extra_waits=gw, dma=True, tag=f"{tg}_ldv")
                s.emit("pool", lambda e, c=c, f_kw=f_kw: e.dma_start(out=f_kw, in_=kw_dram[c]),
                       reads=[], writes=[f_kwn], extra_waits=gw, dma=True, tag=f"{tg}_ldkw")
                return
            bnd_ap = bnd1[:] if c == 0 else bnd_hist[:, :, c - 1:c]
            bnd_nm = "bnd1" if c == 0 else "bnd_hist"
            mk_shift_delta(f_dn, f_d, f_l1, f_l1n, bnd_ap, bnd_nm, tg + "_sh")
            ln1x_fm_c, ln1x_fm_cn = f_l1, f_l1n
            dfm_c, dfm_cn = f_d, f_dn
            k_fm_c, k_fm_cn = f_k, f_kn
            v_tm_c, v_tm_cn = f_v, f_vn
            kwkT_c, kwkT_cn = f_kw, f_kwn
            if c % 2 == 0:
                r_c, r_cn = r_fm[:], "r_fm"
            else:
                r_c, r_cn = r_fmB, "r_fmB"
            mr, mrn = mk_mix(colA[:, :, 2], ["colA"], ln1x_fm_c, [ln1x_fm_cn], dfm_c, [dfm_cn], tg + "_mxr")
            def evac_r(dc, p, pname):
                s.emit("act", lambda a, dc=dc, p=p: a.activation(
                    r_c[:, dc, :], p[:, 0:Q], Act.Identity),
                    reads=[pname, pname + "_acc"], writes=[r_cn], tag=f"{tg}_evr{dc}")
            proj_fm(mr, mrn, SLOT["wrT"], WBUF["wrT"], evac_r, tg + "_pr")
            if phase == "front":
                FCTX[c] = dict(xch=xch, xchn=xchn, l1=ln1x_fm_c, l1n=ln1x_fm_cn,
                               d=dfm_c, dn=dfm_cn, k=k_fm_c, kn=k_fm_cn,
                               v=v_tm_c, vn=v_tm_cn, kw=kwkT_c, kwn=kwkT_cn,
                               r=r_c, rn=r_cn)
                return
        else:
            dma("sp", xch, src_dram[c * Q:(c + 1) * Q, :].rearrange("(tt p) d -> p tt d", p=128),
                [], [xchn], f"{tg}_ldx")
            ln_stats(xch, [xchn], D, tg + "_ln1")
            ln_apply(ln1x_tm, "ln1x_tm", xch, [xchn], tg + "_ln1")
            if not st("tr"):
                return
            transpose_tm_to_fm(ln1x_fm, "ln1x_fm", ln1x_tm, ["ln1x_tm"], tg + "_l1t")
            mk_shift_delta("dfm", dfm, ln1x_fm, "ln1x_fm",
                           bnd1[:] if c == 0 else bnd_hist[:, :, c - 1:c],
                           "bnd1" if c == 0 else "bnd_hist", tg + "_sh")
            s.emit("dve", lambda v, c=c: v.tensor_copy(
                bnd_hist[:, :, c:c + 1], ln1x_fm[:, :, Q - 1:Q]),
                reads=["ln1x_fm"], writes=["bnd_hist"], tag=f"{tg}_bnd1")
            dma("pool", l1_dram[c], ln1x_fm[:], ["ln1x_fm"], ["l1_dram"], f"{tg}_spl1")
            ln1x_fm_c, ln1x_fm_cn = ln1x_fm[:], "ln1x_fm"
            dfm_c, dfm_cn = dfm[:], "dfm"
            k_fm_c, k_fm_cn = k_fm[:], "k_fm"
            v_tm_c, v_tm_cn = v_tm[:], "v_tm"
            kwkT_c, kwkT_cn = kwkT[:], "kwkT"

        x1ch = xch
        x1n = xchn

        def emit_su(stb_w, stb_wn):
            for hp in range(8):
                suk = SU[hp % 2]
                sname = f"su{hp % 2}"
                for par in range(2):
                    h = 2 * hp + par
                    for jt in range(2):
                        first = (jt == 0 and par == 0)
                        s.emit("pe", lambda t, hp=hp, jt=jt, par=par, h=h, suk=suk,
                               kw=kwkT_c, vt=v_tm_c: raw_matmul(
                            t, nc, suk[64 * par:64 * par + 64, 0:HN],
                            kw[:, hp, jt, 64 * par:64 * par + 64],
                            vt[:, jt, h * HN:(h + 1) * HN],
                            start=(jt == 0), stop=(jt == 1),
                            tile_position=(0, 64 * par)),
                            reads=[kwkT_cn, v_tm_cn] + ([sname] if first else []),
                            writes=[sname] if first else [sname + "_acc"],
                            tag=f"{tg}_su{hp}.{jt}.{par}")
                s.emit("dve", lambda v, hp=hp, suk=suk: v.scalar_tensor_tensor(
                    state_f[:, hp, :], state_f[:, hp, :], ws_rep[:, hp:hp + 1],
                    suk[:, 0:HN], Alu.mult, Alu.add),
                    reads=["state_f", "ws_rep", sname, sname + "_acc"],
                    writes=["state_f"], tag=f"{tg}_stu{hp}")
                s.emit("dve", lambda v, hp=hp, stb_w=stb_w: v.tensor_copy(
                    stb_w[:, hp, :], state_f[:, hp, :]),
                    reads=["state_f"], writes=[stb_wn], tag=f"{tg}_stc{hp}")
        if not full:
            mk, mkn = mk_mix(colA[:, :, 0], ["colA"], ln1x_fm, ["ln1x_fm"], dfm, ["dfm"], tg + "_mxk")
            def evac_k(dc, p, pname):
                s.emit("act", lambda a, dc=dc, p=p: a.activation(
                    k_fm[:, dc, :], p[:, 0:Q], Act.Identity),
                    reads=[pname, pname + "_acc"], writes=["k_fm"], tag=f"{tg}_evk{dc}")
                s.emit("dve", lambda v, dc=dc, p=p: v.tensor_tensor(
                    kwk_fm[:, dc, :], p[:, 0:Q], mk_fm[:, dc, :], Alu.mult),
                    reads=[pname, pname + "_acc", "mk_fm"], writes=["kwk_fm"], tag=f"{tg}_evkw{dc}")
            proj_fm(mk, mkn, SLOT["wkT"], WBUF["wkT"], evac_k, tg + "_pk")
            s.emit("sp", lambda a: a.dma_start(
                out=kwkT[:],
                in_=kwk_fm.rearrange("p cc t -> p (cc t)"), transpose=True),
                reads=["kwk_fm"], writes=["kwkT"], dma=True, tag=f"{tg}_kwt")
            mv, mvn = mk_mix(colA[:, :, 1], ["colA"], ln1x_fm, ["ln1x_fm"], dfm, ["dfm"], tg + "_mxv")
            def evac_v(tt):
                for dh in range(2):
                    if dh == 0:
                        s.emit("dve", lambda v, tt=tt, dh=dh: v.tensor_copy(
                            v_tm[:, tt, dh * 512:(dh + 1) * 512], PP[dh][:, 0:512]),
                            reads=[f"pp{dh}", f"pp{dh}_acc"], writes=["v_tm"], tag=f"{tg}_evv{tt}.{dh}")
                    else:
                        s.emit("act", lambda a, tt=tt, dh=dh: a.activation(
                            v_tm[:, tt, dh * 512:(dh + 1) * 512], PP[dh][:, 0:512], Act.Identity),
                            reads=[f"pp{dh}", f"pp{dh}_acc"], writes=["v_tm"], tag=f"{tg}_evv{tt}.{dh}")
            proj_tm2(mv, [mvn], SLOT["wvT"], WBUF["wvT"], evac_v, tg + "_pv")
            dma("pool", k_dram[c], k_fm[:], ["k_fm"], ["k_dram"], f"{tg}_spk")
            dma("pool", kw_dram[c], kwkT[:], ["kwkT"], ["kw_dram"], f"{tg}_spkw")
            dma("pool", v_dram[c], v_tm[:], ["v_tm"], ["v_dram"], f"{tg}_spv")

        if full and st("rg"):
            mg, mgn = mk_mix(colA[:, :, 3], ["colA"], ln1x_fm_c, [ln1x_fm_cn], dfm_c, [dfm_cn], tg + "_mxg")
            s.emit("dve", lambda v, r_c=r_c: v.tensor_tensor(r_wb[:], r_c, m_fm[:], Alu.mult),
                   reads=[r_cn, "m_fm"], writes=["r_wb"], tag=f"{tg}_rwb")

            if not st("scan"):
                return
            if c % 2 == 0:
                stb_w, stb_wn = state_b, "state_b"
                stb_r, stb_rn = state_bB, "state_bB"
            else:
                stb_w, stb_wn = state_bB, "state_bB"
                stb_r, stb_rn = state_b, "state_b"
            for hp in range(8):
                e, o = 2 * hp, 2 * hp + 1
                for par, h in ((0, e), (1, o)):
                    r0 = 64 * par
                    spx = [sp0, sp1][par]
                    spn = f"sp{par}"
                    s.emit("pe", lambda t, hp=hp, r0=r0, spx=spx, kf=k_fm_c, rc=r_c: t.matmul(
                        spx[:, 0:256],
                        kf[r0:r0 + 64, hp, 0:128], rc[r0:r0 + 64, hp, :],
                        start=True, stop=True,
                        tile_position=(r0, 0)),
                        reads=[k_fm_cn, r_cn, spn],
                        writes=[spn], tag=f"{tg}_sj0.{h}")
                    s.emit("pe", lambda t, hp=hp, r0=r0, spx=spx, kf=k_fm_c, rc=r_c: t.matmul(
                        spx[:, 256:384],
                        kf[r0:r0 + 64, hp, 128:256], rc[r0:r0 + 64, hp, 128:256],
                        start=True, stop=True,
                        tile_position=(r0, 0)),
                        reads=[k_fm_cn, r_cn],
                        writes=[spn + "_acc"], tag=f"{tg}_sj1.{h}")
                for par, h in ((0, e), (1, o)):
                    spx = [sp0, sp1][par]
                    spn = f"sp{par}"
                    s.emit("dve", lambda v, h=h, spx=spx: v.tensor_tensor(
                        sbar[:, h, 0:256], spx[:, 0:256],
                        wmat[:, h, :], Alu.mult),
                        reads=[spn, spn + "_acc", "wmat"],
                        writes=[f"sbar.{h}", "sbar"], tag=f"{tg}_sb.{h}")
                    s.emit("dve", lambda v, h=h, spx=spx: v.tensor_tensor(
                        sbar[:, h, 256:384], spx[:, 256:384],
                        wmat[:, h, 0:128], Alu.mult),
                        reads=[spn, spn + "_acc", "wmat"],
                        writes=[f"sbar.{h}", "sbar"], tag=f"{tg}_sb1.{h}")

            emit_su(stb_w, stb_wn)

            if not st("y"):
                return
            if c % 2 == 0:
                yn_c, yn_cn = yn_tm, "yn_tm"
            else:
                yn_c, yn_cn = yn_tmB, "mk_fm"
            for it in range(2):
                for h in range(H):
                    hp, par = h // 2, h % 2
                    r0 = 64 * par
                    hc = h * HN
                    if it == 0:
                        s.emit("pe", lambda t, h=h, hc=hc, vt=v_tm_c: t.matmul(
                            yps[:, hc:hc + HN], sbar[:, h, 0:128], vt[:, 0, hc:hc + HN],
                            start=True, stop=False),
                            reads=[f"sbar.{h}", v_tm_cn] + (["yps"] if h == 0 else []),
                            writes=["yps"] if h == 0 else ["yps_acc"],
                            tag=f"{tg}_yi0.{h}")
                    else:
                        s.emit("pe", lambda t, h=h, hc=hc, vt=v_tm_c: t.matmul(
                            yps[:, hc:hc + HN], sbar[:, h, 128:256], vt[:, 0, hc:hc + HN],
                            start=True, stop=False),
                            reads=[f"sbar.{h}", v_tm_cn] + (["yps"] if h == 0 else []),
                            writes=["yps"] if h == 0 else ["yps_acc"],
                            tag=f"{tg}_yi1a.{h}")
                        s.emit("pe", lambda t, h=h, hc=hc, vt=v_tm_c: t.matmul(
                            yps[:, hc:hc + HN], sbar[:, h, 256:384], vt[:, 1, hc:hc + HN],
                            start=False, stop=False),
                            reads=[f"sbar.{h}", v_tm_cn], writes=["yps_acc"],
                            tag=f"{tg}_yi1b.{h}")
                    s.emit("pe", lambda t, h=h, hc=hc, hp=hp, r0=r0, it=it,
                           stb_r=stb_r: t.matmul(
                        yps[:, hc:hc + HN],
                        r_wb[r0:r0 + 64, hp, it * 128:(it + 1) * 128],
                        stb_r[r0:r0 + 64, hp, :],
                        start=False, stop=True, tile_position=(r0, 0)),
                        reads=["r_wb", stb_rn], writes=["yps_acc"],
                        tag=f"{tg}_yx{it}.{h}")
                s.emit("dve", lambda v: v.tensor_copy(y_sb[:], yps[:]),
                       reads=["yps", "yps_acc"], writes=["y_sb"], tag=f"{tg}_yev{it}")
                s.emit("dve", lambda v: v.tensor_reduce(
                    st_sum[:], y_sb.rearrange("p (h n) -> p h n", h=H), AXX, Alu.add),
                    reads=["y_sb"], writes=["st_sum"], tag=f"{tg}_gns{it}")
                s.emit("act", lambda a: a.activation(
                    sqs[:], y_sb[:], Act.Square),
                    reads=["y_sb"], writes=["sqs"], tag=f"{tg}_gnq{it}")
                s.emit("dve", lambda v: v.tensor_reduce(
                    st_sq[:], sqs.rearrange("p (h n) -> p h n", h=H), AXX, Alu.add),
                    reads=["sqs"], writes=["st_sq"], tag=f"{tg}_gnq2{it}")
                inv = 1.0 / HN
                s.emit("dve", lambda v: v.tensor_scalar(
                    st_mu[:], st_sum[:], inv, None, Alu.mult),
                    reads=["st_sum"], writes=["st_mu"], tag=f"{tg}_gmu{it}")
                s.emit("dve", lambda v: v.tensor_tensor(
                    st_var[:], st_mu[:], st_mu[:], Alu.mult),
                    reads=["st_mu"], writes=["st_var"], tag=f"{tg}_gm2{it}")
                s.emit("dve", lambda v: v.tensor_scalar(
                    st_sq[:], st_sq[:], inv, None, Alu.mult),
                    reads=["st_sq"], writes=["st_sq"], tag=f"{tg}_gq2{it}")
                s.emit("dve", lambda v: v.tensor_tensor(
                    st_var[:], st_sq[:], st_var[:], Alu.subtract),
                    reads=["st_sq", "st_var"], writes=["st_var"], tag=f"{tg}_gvr{it}")
                s.emit("dve", lambda v: v.tensor_scalar(
                    st_var[:], st_var[:], EPS_GN, None, Alu.add),
                    reads=["st_var"], writes=["st_var"], tag=f"{tg}_gep{it}")
                s.emit("act", lambda a: a.activation(
                    st_rstd[:], st_var[:], Act.Sqrt),
                    reads=["st_var"], writes=["st_rstd"], tag=f"{tg}_gsd{it}")
                s.emit("dve", lambda v: v.reciprocal(st_rstd[:], st_rstd[:]),
                       reads=["st_rstd"], writes=["st_rstd"], tag=f"{tg}_grs{it}")
                s.emit("dve", lambda v: v.tensor_tensor(
                    st_nmr[:], st_mu[:], st_rstd[:], Alu.mult),
                    reads=["st_mu", "st_rstd"], writes=["st_nmr"], tag=f"{tg}_gmr{it}")
                s.emit("dve", lambda v: v.tensor_scalar(
                    st_nmr[:], st_nmr[:], -1.0, None, Alu.mult),
                    reads=["st_nmr"], writes=["st_nmr"], tag=f"{tg}_gnm{it}")
                for h in range(H):
                    hc = h * HN
                    if h % 2 == 0:
                        s.emit("dve", lambda v, h=h, hc=hc, it=it, yn_c=yn_c: v.tensor_scalar(
                            yn_c[:, it, hc:hc + HN], y_sb[:, hc:hc + HN],
                            st_rstd[:, h:h + 1], st_nmr[:, h:h + 1], Alu.mult, Alu.add),
                            reads=["y_sb", "st_rstd", "st_nmr"],
                            writes=[yn_cn], tag=f"{tg}_gap{it}.{h}")
                    else:
                        s.emit("act", lambda a, h=h, hc=hc, it=it, yn_c=yn_c: a.activation(
                            yn_c[:, it, hc:hc + HN], y_sb[:, hc:hc + HN], Act.Identity,
                            bias=st_nmr[:, h:h + 1], scale=st_rstd[:, h:h + 1]),
                            reads=["y_sb", "st_rstd", "st_nmr"],
                            writes=[yn_cn], tag=f"{tg}_gap{it}.{h}")

        tap(f"{tg}_ln1xfm", ln1x_fm[:], (128, 8, Q), ["ln1x_fm"])
        tap(f"{tg}_kfm", k_fm[:], (128, 8, Q), ["k_fm"])
        tap(f"{tg}_vtm", v_tm[:], (128, 2, D), ["v_tm"])
        tap(f"{tg}_kwkT", kwkT[:], (128, 8, 2, 128), ["kwkT"])
        if not st("su"):
            return
        if not full:
            emit_su(state_b, "state_b")

        tap(f"{tg}_state", state_f[:], (128, 8, HN), ["state_f"])
        if full and st("yo"):
            tap(f"{tg}_rfm", r_fm[:], (128, 8, Q), ["r_fm"])
            tap(f"{tg}_sbar", sbar[:], (128, H, 384), ["sbar"])
            TCTX[c] = dict(yn=yn_c, ynn=yn_cn, mg=mg, mgn=mgn, xch=xch, xchn=xchn)

    def attn_tail(c):
        tg = f"A{c}"
        T = TCTX.pop(c)
        yn_c, yn_cn = T["yn"], T["ynn"]
        mg, mgn = T["mg"], T["mgn"]
        xch, xchn = T["xch"], T["xchn"]
        tap(f"{tg}_yn", yn_c[:], (128, 2, D), [yn_cn])
        transpose_tm_to_fm(yn_fm, "yn_fm", yn_c, [yn_cn], tg + "_ynt")
        def evac_g(dc, p, pname):
            s.emit("act", lambda a, dc=dc, p=p: a.activation(
                g_fm[:, dc, :], p[:, 0:Q], Act.Silu),
                reads=[pname, pname + "_acc"], writes=["g_fm"], tag=f"{tg}_evg{dc}")
        proj_fm(mg, mgn, SLOT["wgT"], WBUF["wgT"], evac_g, tg + "_pg")
        tap(f"{tg}_gfm", g_fm[:], (128, 8, Q), ["g_fm"])
        ylt, yln = MIXB[mix_rr[0] % mix_n[0]]
        mix_rr[0] += 1
        for cc in range(8):
            s.emit("act", lambda a, cc=cc, ylt=ylt: a.activation(
                ylt[:, cc, :], yn_fm[:, cc, :], Act.Identity,
                bias=colB[:, cc, 1:2], scale=colB[:, cc, 0:1]),
                reads=["yn_fm", "colB"], writes=[yln], tag=f"{tg}_lnaf{cc}")
        s.emit("dve", lambda v, ylt=ylt: v.tensor_tensor(yg_fm[:], ylt[:], g_fm[:], Alu.mult),
               reads=[yln, "g_fm"], writes=["yg_fm"], tag=f"{tg}_yg")
        tap(f"{tg}_yg", yg_fm[:], (128, 8, Q), ["yg_fm"])
        stage = x1s[:, 0, :]
        def evac_o(tt):
            s.emit("dve", lambda v: v.tensor_copy(stage[:, 0:512], PP[0][:, 0:512]),
                   reads=["pp0", "pp0_acc"], writes=["x1s"], tag=f"{tg}_oev{tt}a")
            s.emit("act", lambda a: a.activation(
                stage[:, 512:1024], PP[1][:, 0:512], Act.Identity),
                reads=["pp1", "pp1_acc"], writes=["x1s"], tag=f"{tg}_oev{tt}b")
            ln_stats_tt(lambda _tt, g: stage[:, g * 512:(g + 1) * 512],
                        ["x1s"], GO, tt, f"{tg}_l2s{tt}")
            ln_fin(GO, tt, f"{tg}_l2f{tt}")
            s.emit("act", lambda a, tt=tt: a.activation(
                stage, stage, Act.Identity,
                bias=stOn[:, tt:tt + 1], scale=stOr[:, tt:tt + 1]),
                reads=["x1s", f"stOr.{tt}", f"stOn.{tt}"], writes=["x1s"],
                tag=f"{tg}_l2ap{tt}")
            s.emit("dve", lambda v, tt=tt, xch=xch: v.tensor_tensor(
                xch[:, tt, :], stage, xch[:, tt, :], Alu.add),
                reads=["x1s", xchn], writes=[xchn], tag=f"{tg}_l2rx{tt}")
        proj_tm2(yg_fm, ["yg_fm"], SLOT["woT"], WBUF["woT"], evac_o, tg + "_po")

        tap(f"{tg}_x1", xch[:], (128, 2, D), [xchn])
        dma("sp", x1_dram[c * Q:(c + 1) * Q, :].rearrange("(tt p) d -> p tt d", p=128),
            xch[:], [xchn], ["x1_dram"], f"{tg}_stx1")

    FCTX3 = {}

    X1SL = [
        (lambda tt: xch_b[:, 0, tt, :], ["xch0"]),
        (lambda tt: xch_b[:, 1, tt, :], ["xch1"]),
        (lambda tt: gv_x1 if tt == 0 else ygv_x1, ["g_fm", "yg_fm"]),
        (lambda tt: rwv_x1 if tt == 0 else mfv_x1, ["r_wb", "m_fm"]),
        (lambda tt: mkv_x1 if tt == 0 else rfv_x1, ["mk_fm", "r_fm"]),
        (lambda tt: vtv_x1 if tt == 0 else kTv_x1, ["v_tm", "kwkT"]),
    ]

    def x1ap(c, tt):
        return X1SL[c % 6][0](tt)

    def x1nm(c):
        return X1SL[c % 6][1]

    def ffn_x1_load(c):
        for tt in range(2):
            dma("sp", x1ap(c, tt), x1_dram[c * Q + tt * 128:c * Q + (tt + 1) * 128, :],
                ["x1_dram"], x1nm(c), f"F{c}_ldx1.{tt}")

    def fctx(c):
        par = c % 2
        d = {}
        if par == 0:
            d["och"], d["ochn"] = och, "och"
            d["l1"], d["l1n"] = ln3x_fm[:], "ln1x_fm"
            d["dm"], d["dmn"] = dfm[:], "dfm"
            d["ka"], d["kan"] = kact_fm[:], "k_fm"
            d["rp"], d["rpn"] = rp_tm[:], "yn_tm"
            d["l3tm"], d["l3tmn"] = ln3x_tm, "ln1x_tm"
            d["GF"], d["GO"] = GF, GO
            d["rpp"] = None
            d["kpp"] = None
        else:
            d["och"], d["ochn"] = x1s, "x1s"
            d["l1"], d["l1n"] = ln1x_fmB, "ln1x_fmB"
            d["dm"], d["dmn"] = dfmB, "dfmB"
            d["ka"], d["kan"] = k_fmB, "k_fmB"
            d["rp"], d["rpn"] = rp_tmB_v, "yn_fm"
            d["l3tm"], d["l3tmn"] = ln3x_tmB, "kwk_fm"
            d["GF"], d["GO"] = GFb, GOb
            d["rpp"] = [(sp0[:, 0:512], "sp0"), (sp1[:, 0:512], "sp1")]
            d["kpp"] = [(su0[:, 0:512], "su0"), (su1[:, 0:512], "su1")]
        return d

    def ffn_front_ln(c):
        tg, C = f"F{c}", fctx(c)
        b6, b6n, a2, a2n, w, wn, rr, rrn, nn_, nnn = C["GF"]
        l3, l3n = C["l3tm"], C["l3tmn"]
        for tt in range(2):
            ln_stats_tt(lambda tt, g: x1ap(c, tt)[:, g * 512:(g + 1) * 512],
                        x1nm(c), C["GF"], tt, tg + "_ln3")
            ln_fin(C["GF"], tt, tg + "_ln3")
            aw = ["ln1x_tm"] if c == 0 else []
            s.emit("act", lambda a, tt=tt: a.activation(
                l3[:, tt, :], x1ap(c, tt), Act.Identity,
                bias=nn_[:, tt:tt + 1], scale=rr[:, tt:tt + 1]),
                reads=x1nm(c) + [f"{rrn}.{tt}", f"{nnn}.{tt}"] + aw,
                writes=[f"{l3n}.{tt}"] + aw, tag=f"{tg}_ln3_ap{tt}")
            s.emit("sp", lambda a, tt=tt: a.dma_start(
                out=C["l1"][:, :, tt * 128:(tt + 1) * 128], in_=l3[:, tt, :],
                transpose=True),
                reads=[f"{l3n}.{tt}"], writes=[C["l1n"]], dma=True,
                tag=f"{tg}_l3t_tr{tt}")

    def ffn_front_shift(c):
        tg, C = f"F{c}", fctx(c)
        mk_shift_delta(C["dmn"], C["dm"], C["l1"], C["l1n"], bnd3, "bnd3", tg + "_sh")
        s.emit("dve", lambda v, f_l1=C["l1"]: v.tensor_copy(bnd3[:], f_l1[:, :, Q - 1:Q]),
               reads=[C["l1n"]], writes=["bnd3"], tag=f"{tg}_bnd3")

    FMIX = {}

    def ffn_front_mix(c):
        tg, C = f"F{c}", fctx(c)
        mr, mrn = mk_mix(colC[:, :, 0], ["colC"], C["l1"], [C["l1n"]],
                         C["dm"], [C["dmn"]], tg + "_mxr")
        mk2, mk2n = mk_mix(colC[:, :, 1], ["colC"], C["l1"], [C["l1n"]],
                           C["dm"], [C["dmn"]], tg + "_mxk")
        FMIX[c] = (mr, mrn, mk2, mk2n)

    def ffn_front_projr(c):
        tg, C = f"F{c}", fctx(c)
        f_rp, f_rpn = C["rp"], C["rpn"]
        rpp = C["rpp"] or [(pp0[:, 0:512], "pp0"), (pp1[:, 0:512], "pp1")]
        mr, mrn = FMIX[c][0], FMIX[c][1]
        def evac_rp(tt):
            for dh in range(2):
                pap, pname = rpp[dh]
                s.emit("act", lambda a, tt=tt, dh=dh, pap=pap, f_rp=f_rp: a.activation(
                    f_rp[:, tt, dh * 512:(dh + 1) * 512], pap, Act.Sigmoid),
                    reads=[pname, pname + "_acc"], writes=[f_rpn],
                    tag=f"{tg}_evr{tt}.{dh}")
        proj_tm2(mr, [mrn], SLOT["fwrT"], WBUF["fwrT"], evac_rp, tg + "_prp",
                 pps=rpp)

    def ffn_front_projk(c):
        tg, C = f"F{c}", fctx(c)
        f_ka, f_kan = C["ka"], C["kan"]
        mk2, mk2n = FMIX.pop(c)[2:4]
        sq = y_sb[:, 0:Q] if c % 2 == 0 else sqs[:, 0:Q]
        sqn = "y_sb" if c % 2 == 0 else "sqs"
        def evac_kp(dc, p, pname):
            s.emit("act", lambda a, dc=dc, p=p: a.activation(
                sq, p[:, 0:Q], Act.Relu),
                reads=[pname, pname + "_acc"], writes=[sqn], tag=f"{tg}_evk{dc}a")
            s.emit("act", lambda a, dc=dc, f_ka=f_ka: a.activation(
                f_ka[:, dc, :], sq, Act.Square),
                reads=[sqn], writes=[f_kan], tag=f"{tg}_evk{dc}b")
        proj_fm(mk2, mk2n, SLOT["fwkT"], WBUF["fwkT"], evac_kp, tg + "_pkp",
                pps=C["kpp"])

    def ffn_front(c):
        ffn_front_ln(c)
        ffn_front_shift(c)
        ffn_front_mix(c)
        ffn_front_projr(c)
        ffn_front_projk(c)

    YPP = [(yps[:, 0:512], "yps"), (yps[:, 512:1024], "yps")]

    def ffn_body_v(c):
        tg, C = f"F{c}", fctx(c)
        f_och, f_ochn = C["och"], C["ochn"]
        f_rp, f_rpn = C["rp"], C["rpn"]
        vpp = YPP if c % 2 == 0 else [(pp0[:, 0:512], "pp0"), (pp1[:, 0:512], "pp1")]
        def evac_vp(tt):
            for dh in range(2):
                pap, pname = vpp[dh]
                s.emit("dve", lambda v, tt=tt, dh=dh, f_rp=f_rp, pap=pap,
                       f_och=f_och: v.tensor_tensor(
                    f_och[:, tt, dh * 512:(dh + 1) * 512], pap,
                    f_rp[:, tt, dh * 512:(dh + 1) * 512], Alu.mult),
                    reads=[pname, pname + "_acc", f_rpn], writes=[f_ochn],
                    tag=f"{tg}_gate{tt}.{dh}")
        proj_tm2(C["ka"], [C["kan"]], SLOT["fwvT"], WBUF["fwvT"], evac_vp,
                 tg + "_pvp", pps=vpp)
        tap(f"{tg}_kact", C["ka"], (128, 8, Q), [C["kan"]])

    def ffn_body_ln4(c):
        tg, C = f"F{c}", fctx(c)
        f_och, f_ochn = C["och"], C["ochn"]
        tap(f"{tg}_preln4", f_och[:], (128, 2, D), [f_ochn])
        for tt in range(2):
            ln_stats_tt(lambda tt, g: f_och[:, tt, g * 512:(g + 1) * 512],
                        [f_ochn], C["GO"], tt, f"{tg}_l4s{tt}")
            ln_fin(C["GO"], tt, f"{tg}_l4f{tt}")

    def ffn_body_fin(c):
        tg, C = f"F{c}", fctx(c)
        f_och, f_ochn = C["och"], C["ochn"]
        _, _, _, _, _, _, rr, rrn, nn_, nnn = C["GO"]
        for tt in range(2):
            s.emit("act", lambda a, tt=tt, f_och=f_och: a.activation(
                f_och[:, tt, :], f_och[:, tt, :], Act.Identity,
                bias=nn_[:, tt:tt + 1], scale=rr[:, tt:tt + 1]),
                reads=[f_ochn, f"{rrn}.{tt}", f"{nnn}.{tt}"], writes=[f_ochn],
                tag=f"{tg}_l4a{tt}")
            s.emit("dve", lambda v, tt=tt, f_och=f_och: v.tensor_tensor(
                f_och[:, tt, :], f_och[:, tt, :], x1ap(c, tt), Alu.add),
                reads=[f_ochn] + x1nm(c), writes=[f_ochn], tag=f"{tg}_l4r{tt}")
        dma("sp", out_d[c * Q:(c + 1) * Q, :].rearrange("(tt p) d -> p tt d", p=128),
            f_och[:], [f_ochn], ["out_d"], f"{tg}_sto")

    def kvctx(c):
        par = c % 2
        d = {}
        d["x"], d["xn"] = xch_b[:, par], f"xch{par}"
        if par == 0:
            d["l1tm"], d["l1tmn"] = ln1x_tm, "ln1x_tm"
            d["l1"], d["l1n"] = ln1x_fm[:], "ln1x_fm"
            d["dm"], d["dmn"] = dfm[:], "dfm"
            d["kf"], d["kfn"] = k_fm[:], "k_fm"
            d["kwf"], d["kwfn"] = kwk_fm[:], "kwk_fm"
            d["kwT"], d["kwTn"] = kwkT[:], "kwkT"
            d["vt"], d["vtn"] = v_tm, "v_tm"
            d["G"] = GF
            d["pps"] = None
        else:
            d["l1tm"], d["l1tmn"] = yn_tm, "yn_tm"
            d["l1"], d["l1n"] = r_fm[:], "r_fm"
            d["dm"], d["dmn"] = g_fm[:], "g_fm"
            d["kf"], d["kfn"] = yg_fm[:], "yg_fm"
            d["kwf"], d["kwfn"] = yn_fm[:], "yn_fm"
            d["kwT"], d["kwTn"] = kwkTB_P1, "r_wb"
            d["vt"], d["vtn"] = v_tmB_P1, "sbar"
            d["G"] = GFb
            d["pps"] = [(sp0[:, 0:512], "sp0"), (sp1[:, 0:512], "sp1")]
        return d

    def kv_ldx(c):
        C = kvctx(c)
        dma("sp", C["x"], xo[c * Q:(c + 1) * Q, :].rearrange("(tt p) d -> p tt d", p=128),
            [], [C["xn"]], f"P{c}_ldx")

    def kv_front(c):
        tg, C = f"P{c}", kvctx(c)
        b6, b6n, a2, a2n, w, wn, rr, rrn, nn_, nnn = C["G"]
        for tt in range(2):
            ln_stats_tt(lambda tt, g: C["x"][:, tt, g * 512:(g + 1) * 512],
                        [C["xn"]], C["G"], tt, tg + "_ln1")
            ln_fin(C["G"], tt, tg + "_ln1")
            s.emit("act", lambda a, tt=tt: a.activation(
                C["l1tm"][:, tt, :], C["x"][:, tt, :], Act.Identity,
                bias=nn_[:, tt:tt + 1], scale=rr[:, tt:tt + 1]),
                reads=[C["xn"], f"{rrn}.{tt}", f"{nnn}.{tt}"],
                writes=[f"{C['l1tmn']}.{tt}"], tag=f"{tg}_l1ap{tt}")
            s.emit("sp", lambda a, tt=tt: a.dma_start(
                out=C["l1"][:, :, tt * 128:(tt + 1) * 128], in_=C["l1tm"][:, tt, :],
                transpose=True),
                reads=[f"{C['l1tmn']}.{tt}"], writes=[C["l1n"]], dma=True,
                tag=f"{tg}_l1t{tt}")
        s.emit("dve", lambda v, c=c: v.tensor_copy(
            bnd_hist[:, :, c:c + 1], C["l1"][:, :, Q - 1:Q]),
            reads=[C["l1n"]], writes=["bnd_hist"], tag=f"{tg}_bnd1")
        dma("pool", l1_dram[c], C["l1"], [C["l1n"]], ["l1_dram"], f"{tg}_spl1")

    KVMIX = {}

    def kv_mix(c):
        tg, C = f"P{c}", kvctx(c)
        bnd_ap = bnd1[:] if c == 0 else bnd_hist[:, :, c - 1:c]
        bnd_nm = "bnd1" if c == 0 else "bnd_hist"
        mk_shift_delta(C["dmn"], C["dm"], C["l1"], C["l1n"], bnd_ap, bnd_nm, tg + "_sh")
        mk = mk_mix(colA[:, :, 0], ["colA"], C["l1"], [C["l1n"]],
                    C["dm"], [C["dmn"]], tg + "_mxk")
        mv = mk_mix(colA[:, :, 1], ["colA"], C["l1"], [C["l1n"]],
                    C["dm"], [C["dmn"]], tg + "_mxv")
        KVMIX[c] = (mk, mv)

    def kv_projk(c):
        tg, C = f"P{c}", kvctx(c)
        (mk, mkn), _ = KVMIX[c]
        kf, kfn, kwf, kwfn = C["kf"], C["kfn"], C["kwf"], C["kwfn"]
        def evac_k(dc, p, pname):
            s.emit("act", lambda a, dc=dc, p=p: a.activation(
                kf[:, dc, :], p[:, 0:Q], Act.Identity),
                reads=[pname, pname + "_acc"], writes=[kfn], tag=f"{tg}_evk{dc}")
            s.emit("dve", lambda v, dc=dc, p=p: v.tensor_tensor(
                kwf[:, dc, :], p[:, 0:Q], mk_fm[:, dc, :], Alu.mult),
                reads=[pname, pname + "_acc", "mk_fm"], writes=[kwfn],
                tag=f"{tg}_evkw{dc}")
        proj_fm(mk, mkn, SLOT["wkT"], WBUF["wkT"], evac_k, tg + "_pk",
                pps=C["pps"])
        s.emit("sp", lambda a: a.dma_start(
            out=C["kwT"], in_=C["kwf"].rearrange("p cc t -> p (cc t)"),
            transpose=True),
            reads=[C["kwfn"]], writes=[C["kwTn"]], dma=True, tag=f"{tg}_kwt")

    def kv_projv(c):
        tg, C = f"P{c}", kvctx(c)
        _, (mv, mvn) = KVMIX.pop(c)
        vt, vtn = C["vt"], C["vtn"]
        vpp = C["pps"] or [(pp0[:, 0:512], "pp0"), (pp1[:, 0:512], "pp1")]
        def evac_v(tt):
            for dh in range(2):
                pap, pname = vpp[dh]
                if dh == 0:
                    s.emit("dve", lambda v, tt=tt, pap=pap: v.tensor_copy(
                        vt[:, tt, 0:512], pap),
                        reads=[pname, pname + "_acc"], writes=[vtn],
                        tag=f"{tg}_evv{tt}.{dh}")
                else:
                    s.emit("act", lambda a, tt=tt, pap=pap: a.activation(
                        vt[:, tt, 512:1024], pap, Act.Identity),
                        reads=[pname, pname + "_acc"], writes=[vtn],
                        tag=f"{tg}_evv{tt}.{dh}")
        proj_tm2(mv, [mvn], SLOT["wvT"], WBUF["wvT"], evac_v, tg + "_pv",
                 pps=vpp)

    def kv_su(c):
        tg, C = f"P{c}", kvctx(c)
        kwT, kwTn, vt, vtn = C["kwT"], C["kwTn"], C["vt"], C["vtn"]
        for hp in range(8):
            suk = SU[hp % 2]
            sname = f"su{hp % 2}"
            for par in range(2):
                h = 2 * hp + par
                for jt in range(2):
                    first = (jt == 0 and par == 0)
                    s.emit("pe", lambda t, hp=hp, jt=jt, par=par, h=h, suk=suk: raw_matmul(
                        t, nc, suk[64 * par:64 * par + 64, 0:HN],
                        kwT[:, hp, jt, 64 * par:64 * par + 64],
                        vt[:, jt, h * HN:(h + 1) * HN],
                        start=(jt == 0), stop=(jt == 1),
                        tile_position=(0, 64 * par)),
                        reads=[kwTn, vtn] + ([sname] if first else []),
                        writes=[sname] if first else [sname + "_acc"],
                        tag=f"{tg}_su{hp}.{jt}.{par}")
            s.emit("dve", lambda v, hp=hp, suk=suk: v.scalar_tensor_tensor(
                state_f[:, hp, :], state_f[:, hp, :], ws_rep[:, hp:hp + 1],
                suk[:, 0:HN], Alu.mult, Alu.add),
                reads=["state_f", "ws_rep", sname, sname + "_acc"],
                writes=["state_f"], tag=f"{tg}_stu{hp}")

    def kv_spill(c):
        tg, C = f"P{c}", kvctx(c)
        dma("pool", k_dram[c], C["kf"], [C["kfn"]], ["k_dram"], f"{tg}_spk")
        dma("pool", kw_dram[c], C["kwT"], [C["kwTn"]], ["kw_dram"], f"{tg}_spkw")
        dma("pool", v_dram[c], C["vt"][:], [C["vtn"]], ["v_dram"], f"{tg}_spv")

    n_p1, n_p2, n_f = LIMITS if LIMITS is not None else (NCHUNK, NCHUNK, NCHUNK)
    if n_p1 > 0:
        kv_ldx(0)
        if n_p1 > 1:
            kv_ldx(1)
        kv_front(0)
        if n_p1 > 1:
            kv_front(1)
    for p in range(0, n_p1, 2):
        a, b = p, p + 1
        hb = b < n_p1
        na, nb = a + 2, b + 2
        if na < n_p1:
            kv_ldx(na)
        if hb and nb < n_p1:
            kv_ldx(nb)
        kv_mix(a)
        if hb:
            kv_mix(b)
        kv_projk(a)
        kv_projv(a)
        if hb:
            kv_projk(b)
            kv_projv(b)
        if na < n_p1:
            kv_front(na)
        if hb and nb < n_p1:
            kv_front(nb)
        kv_su(a)
        kv_spill(a)
        if hb:
            kv_su(b)
            kv_spill(b)
    tap("state_p1", state_f[:], (128, 8, HN), ["state_f"])
    P2_PE_GUARD[0] = s.counts.get("s_pe", 0)
    s.emit("dve", lambda v: v.memset(mixB2[:, 0:1, 0:1], 0.0),
           writes=["mix2"], extra_waits=[("s_pe", P2_PE_GUARD[0])], tag="ulk_m2")
    s.emit("dve", lambda v: v.memset(mixB3[:, 0:1, 0:1], 0.0),
           writes=["mix3"], extra_waits=[("s_pe", P2_PE_GUARD[0])], tag="ulk_m3")
    MIXB[2] = (mixB2, "mix2")
    MIXB[3] = (mixB3, "mix3")
    if n_p1 == NCHUNK:
        dma("pool", my1_dram.ap(),
            state_f.rearrange("p a b -> p (a b)"), ["state_f"], ["my1"], "ex1_st")
        s.emit("pool", lambda g: g.collective_compute(
            "AllGather", Alu.bypass, replica_groups=[[0, 1, 2, 3, 4, 5, 6, 7]],
            ins=[my1_dram.ap()], outs=[g1_dram.ap()]),
            reads=["my1"], writes=["g1"], tag="ex1_ag")
        prev_r = [None]
        def ex1_ld(g):
            pid = (g.partition_id() + 7) % 8
            return g.dma_start(
                out=state_f.rearrange("p a b -> p (a b)"),
                in_=g1_dram.ap().rearrange("r p c -> p r c")[:, ds(pid, 1), 0:512].squeeze(1))
        s.emit("pool", ex1_ld, reads=["g1"], writes=["state_f"], dma=True, tag="ex1_ld")
        s.emit("dve", lambda v: v.tensor_scalar(
            state_f.rearrange("p a b -> p (a b)"), state_f.rearrange("p a b -> p (a b)"),
            hmask[:, 0:1], None, Alu.mult),
            reads=["state_f", "hmask"], writes=["state_f"], tag="ex1_msk")
        for hp in range(8):
            s.emit("dve", lambda v, hp=hp: v.tensor_copy(
                state_bB[:, hp, :], state_f[:, hp, :]),
                reads=["state_f"], writes=["state_bB"], tag=f"ex1_cb{hp}")
    if n_p2 > 0:
        attn_chunk(xo, 0, mode="full", tag="A", phase="load")
        attn_chunk(xo, 0, mode="full", tag="A", phase="ldx")
        if n_p2 > 1:
            attn_chunk(xo, 1, mode="full", tag="A", phase="ldx")
        attn_chunk(xo, 0, mode="full", tag="A", phase="front")
    for c in range(n_p2):
        if c + 1 < n_p2:
            attn_chunk(xo, c + 1, mode="full", tag="A", phase="load")
        attn_chunk(xo, c, mode="full", tag="A", phase="body")
        if c >= 1:
            attn_tail(c - 1)
            if c + 1 < n_p2:
                attn_chunk(xo, c + 1, mode="full", tag="A", phase="ldx")
        if c + 1 < n_p2:
            attn_chunk(xo, c + 1, mode="full", tag="A", phase="front")
    if n_p2 > 0:
        attn_tail(n_p2 - 1)
    tap("state_p2", state_f[:], (128, 8, HN), ["state_f"])
    if n_p2 == NCHUNK:
        ln_stats(xch_b[:, 1], ["xch1"], D, "l3p")
        ln_apply(ln3x_tm, "ln1x_tm", xch_b[:, 1], ["xch1"], "l3p")
        transpose_tm_to_fm(ln3x_fm, "ln1x_fm", ln3x_tm, ["ln1x_tm"], "l3pt")
        dma("pool", my2_dram.ap(), ln3x_fm[:, :, Q - 1], ["ln1x_fm"], ["my2"], "ex2_bn", slow=True)
        s.emit("pool", lambda g: g.collective_compute(
            "AllGather", Alu.bypass, replica_groups=[[0, 1, 2, 3, 4, 5, 6, 7]],
            ins=[my2_dram.ap()], outs=[g2_dram.ap()]),
            reads=["my2"], writes=["g2"], tag="ex2_ag")
        def ex2_lb(g):
            pid = (g.partition_id() + 7) % 8
            return g.dma_start(
                out=stO6.rearrange("p a b -> p (a b)")[:, 0:8],
                in_=g2_dram.ap().rearrange("r p c -> p r c")[:, ds(pid, 1), :].squeeze(1))
        s.emit("pool", ex2_lb, reads=["g2"], writes=["stO6"], dma=True, tag="ex2_lb")
        s.emit("dve", lambda v: v.tensor_scalar(
            bnd3.rearrange("p a b -> p (a b)"), stO6.rearrange("p a b -> p (a b)")[:, 0:8],
            hmask[:, 0:1], None, Alu.mult),
            reads=["stO6", "hmask"], writes=["bnd3"], tag="ex2_mb")
    if n_f > 0:
        for nm in ["fwrT", "fwkT", "fwvT"]:
            sl = SLOT[nm]
            dma("pool", warena[:, sl, :, :],
                w_dram[nm].rearrange("(cc p) d -> p cc d", p=128),
                [], [WBUF[nm]], f"ldw_{nm}")
    if n_f > 0:
        for c in range(min(4, n_f)):
            ffn_x1_load(c)
        ffn_front(0)
        if n_f > 1:
            ffn_front(1)
    for p in range(0, n_f, 2):
        a, b = p, p + 1
        hb = b < n_f
        na, nb = a + 2, b + 2
        if a + 4 < n_f:
            ffn_x1_load(a + 4)
        if hb and b + 4 < n_f:
            ffn_x1_load(b + 4)
        if na < n_f:
            ffn_front_ln(na)
        if hb and nb < n_f:
            ffn_front_ln(nb)
        ffn_body_v(a)
        if hb:
            ffn_body_v(b)
        if na < n_f:
            ffn_front_shift(na)
        if hb and nb < n_f:
            ffn_front_shift(nb)
        if na < n_f:
            ffn_front_mix(na)
        if hb and nb < n_f:
            ffn_front_mix(nb)
        if na < n_f:
            ffn_front_projr(na)
        if hb and nb < n_f:
            ffn_front_projr(nb)
        if na < n_f:
            ffn_front_projk(na)
        if hb and nb < n_f:
            ffn_front_projk(nb)
        ffn_body_ln4(a)
        if hb:
            ffn_body_ln4(b)
        ffn_body_fin(a)
        if hb:
            ffn_body_fin(b)

    fin = s.barrier()
    s.emit("sp", lambda sp: sp.nop(), extra_waits=fin, tag="final_wait")
    s.build()
    return nc


def _prep_core_inputs(inputs, b, h):
    x = inputs["L0_attn_x"]
    d = {}
    d["xo"] = np.ascontiguousarray(x[b, h * 4096:(h + 1) * 4096, :])
    d["hmask"] = np.full((128, 1), 1.0 if h == 1 else 0.0, np.float32)
    if h == 1:
        row = np.asarray(x[b, 4095, :], np.float32)
    else:
        row = np.zeros(D, np.float32)
    d["xprow"] = np.ascontiguousarray(row[None, :])
    d["xpcol"] = np.ascontiguousarray(row.reshape(8, 128).T)
    for nm, key in [("wrT", "L0_attn_w_r"), ("wkT", "L0_attn_w_k"),
                    ("wvT", "L0_attn_w_v"), ("wgT", "L0_attn_w_g"),
                    ("woT", "L0_attn_w_o"), ("fwrT", "L0_ffn_W_r"),
                    ("fwkT", "L0_ffn_W_k"), ("fwvT", "L0_ffn_W_v")]:
        d[nm] = np.ascontiguousarray(np.asarray(inputs[key]).T)
    def col8(v):
        return np.ascontiguousarray(np.asarray(v).reshape(8, 128).T)
    d["maa"] = np.ascontiguousarray(np.stack(
        [col8(inputs["L0_attn_time_maa_k"]), col8(inputs["L0_attn_time_maa_v"]),
         col8(inputs["L0_attn_time_maa_r"]), col8(inputs["L0_attn_time_maa_g"])],
        axis=2))
    d["lnw"] = np.ascontiguousarray(np.stack(
        [col8(inputs["L0_attn_ln_g"]), col8(inputs["L0_attn_ln_b"])], axis=2))
    d["mus"] = np.ascontiguousarray(np.stack(
        [col8(inputs["L0_ffn_mu_r"]), col8(inputs["L0_ffn_mu_k"])], axis=2))
    dec = np.asarray(inputs["L0_attn_time_decay"])
    u = np.asarray(inputs["L0_attn_time_faaaa"])
    rep = np.zeros((128, 8), np.float32)
    for hp in range(8):
        rep[0:64, hp] = dec[2 * hp]
        rep[64:128, hp] = dec[2 * hp + 1]
    d["dec_rep"] = rep
    d["dec_full"] = np.ascontiguousarray(np.tile(dec[None, :], (128, 1)).astype(np.float32))
    d["u_full"] = np.ascontiguousarray(np.tile(u[None, :], (128, 1)).astype(np.float32))
    return d


def kernel(**inputs):
    nc = build_nc()
    in_maps = []
    for b in range(4):
        for h in range(2):
            in_maps.append(_prep_core_inputs(inputs, b, h))
    res = run_bass_kernel_spmd(nc, in_maps, core_ids=list(range(8)))
    out = np.zeros((4, 8192, D), np.float32)
    i = 0
    for b in range(4):
        for h in range(2):
            out[b, h * 4096:(h + 1) * 4096, :] = res.results[i]["out"]
            i += 1
    return out


# version 23 (final)
# speedup vs baseline: 1.0468x; 1.0216x over previous
import numpy as np
import concourse.bass as bass
import concourse.mybir as mybir
from concourse.bass_utils import run_bass_kernel_spmd
ds = bass.ds

F32 = mybir.dt.float32
BF16 = mybir.dt.bfloat16
Act = mybir.ActivationFunctionType
Alu = mybir.AluOpType
AXX = mybir.AxisListType.X

NCHUNK = 16
Q = 256
D = 1024
H = 16
HN = 64
EPS_LN = 1e-5
EPS_GN = 1e-5 * 64

WANT_TAPS = {}
LIMITS = None
STAGES = None

ENGS = ("pe", "dve", "act", "pool", "sp")
ENG_SEM = {"pe": "s_pe", "dve": "s_dv", "act": "s_ac", "pool": "s_po", "sp": "s_sp"}


def raw_matmul(t, nc, out, lhsT, rhs, start, stop, tile_position=None, ldw=True):
    if ldw:
        return t.matmul(out, lhsT, rhs, start=start, stop=stop,
                        tile_position=tile_position)
    m = mybir.InstMatmult(
        name=f"I-{nc.next_id()}",
        ins=[t.lower_ap(rhs), t.lower_ap(lhsT, for_matmul_weights=True)],
        outs=[t.lower_ap(out)],
        replication_resolution=0, replication_shift_amnt=0, replication_num_rows=0,
        start_tensor_calc=start, stop_tensor_calc=stop,
        tile_size=(lhsT.shape[0], lhsT.shape[-1]),
        tile_position=tile_position if tile_position is not None else (0, 0),
        ldweights=False,
    )
    return t.add_instruction(m)


class Sched:

    NDMA_LANES = 4

    def __init__(self, nc):
        self.nc = nc
        self.ops = {e: [] for e in ENGS}
        self.sems = {}
        self.counts = {}
        self.bufs = {}
        self.dma_rr = {e: 0 for e in ENGS}

    def sem(self, name):
        if name not in self.sems:
            self.sems[name] = self.nc.alloc_semaphore(name)
            self.counts[name] = 0
        return self.sems[name]

    def _buf(self, name):
        if name not in self.bufs:
            self.bufs[name] = {"w": None, "r": []}
        return self.bufs[name]

    def emit(self, eng, fn, reads=(), writes=(), extra_waits=(), dma=False, tag=""):
        if dma:
            lane = self.dma_rr[eng] % self.NDMA_LANES
            self.dma_rr[eng] += 1
            inc_sem = f"s_dma_{eng}{lane}"
        else:
            inc_sem = ENG_SEM[eng]
        self.sem(inc_sem)
        waits = list(extra_waits)
        if dma and self.counts[inc_sem] > 0:
            waits.append((inc_sem, self.counts[inc_sem]))
        def need(dep_eng, tok):
            if dep_eng == "pe" and eng == "pe" and not self._is_dma_tok(tok):
                return False
            return True
        PSUM_NAMES = ("pp0", "pp1", "yps", "sp0", "sp1", "su0", "su1")
        for bn in reads:
            b = self._buf(bn)
            if b["w"] is not None:
                weng, wtok = b["w"]
                if need(weng, wtok):
                    waits.append(wtok)
            if bn.split("_acc")[0] in PSUM_NAMES and eng in ("dve", "act"):
                other = "act" if eng == "dve" else "dve"
                for reng, rtok in b["r"]:
                    if reng == other:
                        waits.append(rtok)
        for bn in writes:
            b = self._buf(bn)
            if b["w"] is not None:
                weng, wtok = b["w"]
                if need(weng, wtok):
                    waits.append(wtok)
            for reng, rtok in b["r"]:
                if need(reng, rtok):
                    waits.append(rtok)
        wmap = {}
        for (sname, v) in waits:
            assert v <= self.counts[sname], f"wait {sname}>={v} not yet emitted (tag={tag})"
            wmap[sname] = max(wmap.get(sname, 0), v)
        amt = 16 if dma else 1
        self.counts[inc_sem] += amt
        tok = (inc_sem, self.counts[inc_sem])
        self.ops[eng].append((fn, tuple(sorted(wmap.items())), inc_sem, amt, tag))
        for bn in reads:
            self._buf(bn)["r"].append((eng, tok))
        for bn in writes:
            b = self._buf(bn)
            b["w"] = (eng, tok)
            b["r"] = []
        return tok

    @staticmethod
    def _is_dma_tok(tok):
        return tok[0].startswith("s_dma_")

    def barrier(self):
        return [(s, c) for s, c in self.counts.items() if c > 0]

    def build(self):
        nc = self.nc
        with nc.Block() as block:
            def mk(eng_key):
                def body(eng):
                    last = {}
                    for (fn, waits, inc, amt, tag) in self.ops[eng_key]:
                        for (s, v) in waits:
                            if last.get(s, -1) >= v:
                                continue
                            eng.wait_ge(self.sems[s], v)
                            last[s] = v
                        ins = fn(eng)
                        ins.then_inc(self.sems[inc], amt)
                return body
            block.tensor(mk("pe"))
            block.vector(mk("dve"))
            block.scalar(mk("act"))
            block.gpsimd(mk("pool"))
            block.sync(mk("sp"))


def build_nc(debug_taps=None):
    taps = dict(WANT_TAPS)
    if debug_taps:
        taps.update(debug_taps)
    nc = bass.Bass("TRN2", target_bir_lowering=False)
    s = Sched(nc)

    xo = nc.dram_tensor("xo", [4096, D], F32, kind="ExternalInput").ap()
    hmask_in = nc.dram_tensor("hmask", [128, 1], F32, kind="ExternalInput").ap()
    xprow_in = nc.dram_tensor("xprow", [1, D], F32, kind="ExternalInput").ap()
    xpcol_in = nc.dram_tensor("xpcol", [128, 8], F32, kind="ExternalInput").ap()
    k_dram = nc.dram_tensor("k_spill", [NCHUNK, 128, 8, Q], BF16, kind="Internal").ap()
    v_dram = nc.dram_tensor("v_spill", [NCHUNK, 128, 2, D], BF16, kind="Internal").ap()
    kw_dram = nc.dram_tensor("kw_spill", [NCHUNK, 128, 8, 2, 128], BF16, kind="Internal").ap()
    l1_dram = nc.dram_tensor("l1_spill", [NCHUNK, 128, 8, Q], BF16, kind="Internal").ap()
    my1_dram = nc.dram_tensor("my1", [128, 512], F32, kind="Internal")
    g1_dram = nc.dram_tensor("g1", [8, 128, 512], F32, kind="Internal")
    my2_dram = nc.dram_tensor("my2", [128, 8], F32, kind="Internal")
    g2_dram = nc.dram_tensor("g2", [8, 128, 8], F32, kind="Internal")
    w_names = ["wrT", "wkT", "wvT", "wgT", "woT", "fwrT", "fwkT", "fwvT"]
    w_dram = {n: nc.dram_tensor(n, [D, D], F32, kind="ExternalInput").ap() for n in w_names}
    maa_in = nc.dram_tensor("maa", [128, 8, 4], F32, kind="ExternalInput").ap()
    lnw_in = nc.dram_tensor("lnw", [128, 8, 2], F32, kind="ExternalInput").ap()
    mus_in = nc.dram_tensor("mus", [128, 8, 2], F32, kind="ExternalInput").ap()
    dec_rep_in = nc.dram_tensor("dec_rep", [128, 8], F32, kind="ExternalInput").ap()
    dec_full_in = nc.dram_tensor("dec_full", [128, 16], F32, kind="ExternalInput").ap()
    u_full_in = nc.dram_tensor("u_full", [128, 16], F32, kind="ExternalInput").ap()
    out_d = nc.dram_tensor("out", [4096, D], F32, kind="ExternalOutput").ap()
    x1_dram = nc.dram_tensor("x1_scratch", [4096, D], F32, kind="Internal").ap()

    jj = np.arange(128)[:, None]
    ii = np.arange(128)[None, :]
    EA_np = np.where(jj < ii, ii - jj - 1, -1.0).astype(np.float32)
    MK_np = (jj < ii).astype(np.float32)
    EB_np = (ii + 128 - jj - 1).astype(np.float32)
    EYE_np = np.eye(128, dtype=np.float32)
    IOTA_np = np.tile(np.arange(Q, dtype=np.float32), (128, 1))
    ONES1_np = np.ones((1, 128), dtype=np.float32)
    EA_t = nc.inline_tensor(EA_np, name="EA").ap()
    MKm_t = nc.inline_tensor(MK_np, name="MKm").ap()
    EB_t = nc.inline_tensor(EB_np, name="EB").ap()
    EYE_t = nc.inline_tensor(EYE_np, name="EYE").ap()
    IOTA_t = nc.inline_tensor(IOTA_np, name="IOTA").ap()
    ONES1_t = nc.inline_tensor(ONES1_np, name="ONES1").ap()

    A = nc.alloc_sbuf_tensor
    warena = A("warena", [128, 5, 8, D], BF16)
    SLOT = {"wkT": 0, "wvT": 1, "wrT": 2, "wgT": 3, "woT": 4,
            "fwrT": 2, "fwkT": 3, "fwvT": 4}
    WBUF = {"wkT": "wsl0", "wvT": "wsl1", "wrT": "wsl2", "wgT": "wsl3", "woT": "wsl4",
            "fwrT": "wsl2", "fwkT": "wsl3", "fwvT": "wsl4"}

    y_sb2 = None
    colA = A("colA", [128, 8, 4], F32)
    colB = A("colB", [128, 8, 2], F32)
    colC = A("colC", [128, 8, 2], F32)
    dcr = A("dcr", [128, 8], F32)
    dcf = A("dcf", [128, 16], F32)
    ufl = A("ufl", [128, 16], F32)
    logw_r = A("logw_r", [128, 8], F32)
    logw_f = A("logw_f", [128, 16], F32)
    ws_rep = A("ws_rep", [128, 8], F32)
    tsc = A("tsc", [128, Q], F32)
    tsc2 = A("tsc2", [128, 128], F32)
    wmat = A("wmat", [128, H, Q], BF16)
    m_fm = A("m_fm", [128, 8, Q], BF16)
    mk_fm = A("mk_fm", [128, 8, Q], BF16)

    hmask = A("hmask_s", [128, 1], F32)
    bnd_hist = A("bnd_hist", [128, 8, NCHUNK], BF16)
    state_f = A("state_f", [128, 8, HN], F32)
    state_b = A("state_b", [128, 8, HN], BF16)

    xch_b = A("xch_b", [128, 2, 2, D], F32)
    ln1x_tm = A("ln1x_tm", [128, 2, D], BF16)
    ln1x_fm = A("ln1x_fm", [128, 8, Q], BF16)
    bnd1 = A("bnd1", [128, 8, 1], BF16)
    bnd3 = A("bnd3", [128, 8, 1], BF16)
    dfm = A("dfm", [128, 8, Q], BF16)
    mix0 = A("mix0", [128, 8, Q], BF16)
    mix1 = A("mix1", [128, 8, Q], BF16)
    MIXB = [(mix0, "mix0"), (mix1, "mix1")]
    y_sb = A("y_sb", [128, D], F32)
    consts = y_sb.rearrange("p (a b) -> p a b", a=8)[:, 0:6, :]
    iota = y_sb[:, 640:896]
    _sl0 = warena[:, 0]
    _sl1 = warena[:, 1]
    ln1x_fmB = _sl0[:, :, 0:256]
    kwkTB = _sl0[:, :, 256:512].rearrange("p a (b c) -> p a b c", b=2)
    dfmB = _sl0[:, :, 512:768]
    k_fmB = _sl0[:, :, 768:1024]
    v_tmB = _sl1[:, 0:2, :]
    mixB2 = _sl1[:, 2:4, :].rearrange("p a (b c) -> p (a b) c", c=256)
    mixB3 = _sl1[:, 4:6, :].rearrange("p a (b c) -> p (a b) c", c=256)
    r_fmB = _sl1[:, 6:8, :].rearrange("p a (b c) -> p (a b) c", c=256)
    rp_tmB = None
    r_fm = A("r_fm", [128, 8, Q], BF16)
    k_fm = A("k_fm", [128, 8, Q], BF16)
    g_fm = A("g_fm", [128, 8, Q], BF16)
    v_tm = A("v_tm", [128, 2, D], BF16)
    kwk_fm = A("kwk_fm", [128, 8, Q], BF16)
    kwkT = A("kwkT", [128, 8, 2, 128], BF16)
    r_wb = A("r_wb", [128, 8, Q], BF16)
    sbar = A("sbar", [128, H, 384], BF16)
    sqs = A("sqs", [128, D], F32)
    stF = A("stF", [128, 2, 2], F32)
    stFb = A("stFb", [128, 4, 6], F32)
    stFr = A("stFr", [128, 2], F32)
    stFn = A("stFn", [128, 2], F32)
    st_sum = A("st_sum", [128, 16], F32)
    st_sq = A("st_sq", [128, 16], F32)
    st_mu = A("st_mu", [128, 16], F32)
    st_var = A("st_var", [128, 16], F32)
    st_rstd = A("st_rstd", [128, 16], F32)
    st_nmr = A("st_nmr", [128, 16], F32)
    stO6 = A("stO6", [128, 4, 6], F32)
    stOw = A("stOw", [128, 2, 2], F32)
    stOr = A("stOr", [128, 2], F32)
    stOn = A("stOn", [128, 2], F32)
    stFb2 = A("stFb2", [128, 4, 6], F32)
    stF2 = A("stF2", [128, 2, 2], F32)
    stFr2 = A("stFr2", [128, 2], F32)
    stFn2 = A("stFn2", [128, 2], F32)
    stO6b = A("stO6b", [128, 4, 6], F32)
    stOwb = A("stOwb", [128, 2, 2], F32)
    stOrb = A("stOrb", [128, 2], F32)
    stOnb = A("stOnb", [128, 2], F32)
    stP0 = A("stP0", [128, 2, 2], F32)
    yn_tm = A("yn_tm", [128, 2, D], BF16)
    yn_fm = A("yn_fm", [128, 8, Q], BF16)
    rp_tmB_v = yn_fm.rearrange("p a b -> p (a b)").rearrange("p (a b) -> p a b", a=2)
    yg_fm = A("yg_fm", [128, 8, Q], BF16)
    x1s = A("x1s", [128, 2, D], F32)
    ln3x_tm = ln1x_tm
    ln3x_fm = ln1x_fm
    rp_tm = yn_tm
    kact_fm = k_fm
    och = sbar.bitcast(F32).rearrange("p a b -> p (a b)")[:, 0:2048].rearrange(
        "p (a b) -> p a b", a=2)
    ln3x_tmB = kwk_fm.rearrange("p a b -> p (a b)").rearrange(
        "p (t d) -> p t d", t=2)
    yn_tmB = mk_fm.rearrange("p a b -> p (a b)").rearrange(
        "p (t d) -> p t d", t=2)
    kwkTB_P1 = r_wb.rearrange("p a (b c) -> p a b c", b=2)
    v_tmB_P1 = sbar.rearrange("p h q -> p (h q)")[:, 0:2048].rearrange(
        "p (t d) -> p t d", t=2)
    _x1bf = x1s.bitcast(BF16).rearrange("p a b -> p (a b)")
    x1mA = _x1bf[:, 0:2048].rearrange("p (a b) -> p a b", a=8)
    x1mB = _x1bf[:, 2048:4096].rearrange("p (a b) -> p a b", a=8)
    MIXB.append((x1mA, "x1mA"))
    MIXB.append((x1mB, "x1mB"))
    gv_x1 = g_fm.bitcast(F32).rearrange("p a b -> p (a b)")
    ygv_x1 = yg_fm.bitcast(F32).rearrange("p a b -> p (a b)")
    rwv_x1 = r_wb.bitcast(F32).rearrange("p a b -> p (a b)")
    mfv_x1 = m_fm.bitcast(F32).rearrange("p a b -> p (a b)")
    mkv_x1 = mk_fm.bitcast(F32).rearrange("p a b -> p (a b)")
    rfv_x1 = r_fm.bitcast(F32).rearrange("p a b -> p (a b)")
    vtv_x1 = v_tm.bitcast(F32).rearrange("p a b -> p (a b)")
    kTv_x1 = kwkT.bitcast(F32).rearrange("p a b c -> p (a b c)")

    pp0 = nc.alloc_psum_tensor("pp0", [128, 512], F32)
    pp1 = nc.alloc_psum_tensor("pp1", [128, 512], F32)
    yps = nc.alloc_psum_tensor("yps", [128, 1024], F32)
    sp0 = nc.alloc_psum_tensor("sp0", [128, 512], F32)
    sp1 = nc.alloc_psum_tensor("sp1", [128, 512], F32)
    su0 = nc.alloc_psum_tensor("su0", [128, 512], F32)
    su1 = nc.alloc_psum_tensor("su1", [128, 512], F32)
    PP = [pp0, pp1]
    SU = [su0, su1]

    def tap(name, src_ap, shape, reads):
        if name not in taps:
            return
        d = nc.dram_tensor("tap_" + name, list(shape), F32, kind="ExternalOutput").ap()
        s.emit("pool", lambda g, a=src_ap, d=d: g.dma_start(out=d, in_=a),
               reads=list(reads), writes=[], dma=True, tag="tapdma_" + name)

    def dma(eng, out_ap, in_ap, reads, writes, tag, slow=False):
        return s.emit(eng, lambda e, o=out_ap, i=in_ap: e.dma_start(
            out=o, in_=i, allow_slow_non_contiguous=slow),
            reads=reads, writes=writes, dma=True, tag=tag)

    for idx, nm in enumerate(["wkT", "wvT", "wrT", "wgT", "woT"]):
        sl = SLOT[nm]
        dma("pool", warena[:, sl, :, :],
            w_dram[nm].rearrange("(cc p) d -> p cc d", p=128),
            [], [WBUF[nm]], f"ldw_{nm}")
    dma("sp", colA[:], maa_in, [], ["colA"], "ld_maa")
    dma("sp", hmask[:], hmask_in, [], ["hmask"], "ld_hm")
    dma("sp", colB[:], lnw_in, [], ["colB"], "ld_lnw")
    dma("sp", colC[:], mus_in, [], ["colC_raw"], "ld_mus")
    dma("sp", dcr[:], dec_rep_in, [], ["dcr"], "ld_dcr")
    dma("sp", dcf[:], dec_full_in, [], ["dcf"], "ld_dcf")
    dma("sp", ufl[:], u_full_in, [], ["ufl"], "ld_ufl")
    dma("sp", consts[:, 0, :], EA_t, [], ["cEA"], "ld_EA")
    dma("sp", consts[:, 1, :], MKm_t, [], ["cMK"], "ld_MK")
    dma("sp", consts[:, 2, :], EB_t, [], ["cEB"], "ld_EB")
    dma("sp", consts[:, 3, :], EYE_t, [], ["cEYE"], "ld_EYE")
    dma("sp", iota[:], IOTA_t, [], ["iota"], "ld_IOTA")

    s.emit("act", lambda a: a.activation(logw_r[:], dcr[:], Act.Exp),
           reads=["dcr"], writes=["logw_r"], tag="exp_dcr")
    s.emit("dve", lambda v: v.tensor_scalar(logw_r[:], logw_r[:], -1.0, None, Alu.mult),
           reads=["logw_r"], writes=["logw_r"], tag="neg_dcr")
    s.emit("act", lambda a: a.activation(logw_f[:], dcf[:], Act.Exp),
           reads=["dcf"], writes=["logw_f"], tag="exp_dcf")
    s.emit("dve", lambda v: v.tensor_scalar(logw_f[:], logw_f[:], -1.0, None, Alu.mult),
           reads=["logw_f"], writes=["logw_f"], tag="neg_dcf")
    s.emit("act", lambda a: a.activation(ws_rep[:], logw_r[:], Act.Exp, scale=float(Q)),
           reads=["logw_r"], writes=["ws_rep"], tag="ws_rep")
    s.emit("dve", lambda v: v.tensor_scalar(colC[:], colC[:], -1.0, 1.0, Alu.mult, Alu.add),
           reads=["colC_raw"], writes=["colC"], tag="colC")

    for cc in range(8):
        s.emit("dve", lambda v, cc=cc: v.tensor_scalar(
            tsc[:], iota[:], logw_r[:, cc:cc + 1], None, Alu.mult),
            reads=["iota", "logw_r"], writes=["tsc"], tag=f"m_mul{cc}")
        s.emit("act", lambda a, cc=cc: a.activation(m_fm[:, cc, :], tsc[:], Act.Exp),
               reads=["tsc"], writes=["m_fm"], tag=f"m_exp{cc}")
        s.emit("dve", lambda v, cc=cc: v.tensor_scalar(
            tsc[:], iota[:], -1.0, float(Q - 1), Alu.mult, Alu.add),
            reads=["iota"], writes=["tsc"], tag=f"mk_rev{cc}")
        s.emit("dve", lambda v, cc=cc: v.tensor_scalar(
            tsc[:], tsc[:], logw_r[:, cc:cc + 1], None, Alu.mult),
            reads=["tsc", "logw_r"], writes=["tsc"], tag=f"mk_mul{cc}")
        s.emit("act", lambda a, cc=cc: a.activation(mk_fm[:, cc, :], tsc[:], Act.Exp),
               reads=["tsc"], writes=["mk_fm"], tag=f"mk_exp{cc}")

    for h in range(H):
        s.emit("dve", lambda v, h=h: v.tensor_scalar(
            tsc2[:], consts[:, 0, :], logw_f[:, h:h + 1], None, Alu.mult),
            reads=["cEA", "logw_f"], writes=["tsc2"], tag=f"wa1_{h}")
        s.emit("act", lambda a, h=h: a.activation(tsc2[:], tsc2[:], Act.Exp),
               reads=["tsc2"], writes=["tsc2"], tag=f"wa2_{h}")
        s.emit("dve", lambda v, h=h: v.tensor_tensor(
            tsc2[:], tsc2[:], consts[:, 1, :], Alu.mult),
            reads=["tsc2", "cMK"], writes=["tsc2"], tag=f"wa3_{h}")
        s.emit("dve", lambda v, h=h: v.tensor_scalar(
            consts[:, 4, :], consts[:, 3, :], ufl[:, h:h + 1], None, Alu.mult),
            reads=["cEYE", "ufl"], writes=["cU"], tag=f"wa4_{h}")
        s.emit("dve", lambda v, h=h: v.tensor_tensor(
            wmat[:, h, 0:128], tsc2[:], consts[:, 4, :], Alu.add),
            reads=["tsc2", "cU"], writes=["wmat"], tag=f"wa5_{h}")
        s.emit("dve", lambda v, h=h: v.tensor_scalar(
            tsc2[:], consts[:, 2, :], logw_f[:, h:h + 1], None, Alu.mult),
            reads=["cEB", "logw_f"], writes=["tsc2"], tag=f"wb1_{h}")
        s.emit("act", lambda a, h=h: a.activation(wmat[:, h, 128:256], tsc2[:], Act.Exp),
               reads=["tsc2"], writes=["wmat"], tag=f"wb2_{h}")

    s.emit("dve", lambda v: v.memset(state_f[:], 0.0), writes=["state_f"], tag="z_sf")
    s.emit("dve", lambda v: v.memset(state_b[:], 0.0), writes=["state_b"], tag="z_sb")
    s.emit("dve", lambda v: v.memset(bnd1[:], 0.0), writes=["bnd1"], tag="z_b1")
    s.emit("dve", lambda v: v.memset(bnd3[:], 0.0), writes=["bnd3"], tag="z_b3")

    rowbuf = och
    dma("sp", rowbuf[0:1, 0, :], xprow_in, [], ["och"], "br_ldrow")
    dma("sp", stO6.rearrange("p a b -> p (a b)")[:, 0:8], xpcol_in, [], ["stO6"], "br_ldcol")
    dma("sp", rowbuf[0:1, 1, 0:128], ONES1_t, [], ["ones1"], "br_ldone")
    s.emit("dve", lambda v: v.tensor_reduce(
        stP0[0:1, 0, 0:1], rowbuf[0:1, 0, :], AXX, Alu.add),
        reads=["och"], writes=["stP0"], tag="br_sum")
    s.emit("act", lambda a: a.activation(
        rowbuf[0:1, 1, 128:128 + D - 128], rowbuf[0:1, 0, 0:D - 128], Act.Square,
        accum_out=stP0[0:1, 0, 1:2]),
        reads=["och"], writes=["och2", "stP0"], tag="br_sq1")
    s.emit("act", lambda a: a.activation(
        rowbuf[0:1, 0, 0:128], rowbuf[0:1, 0, D - 128:D], Act.Square,
        accum_out=stP0[0:1, 1, 0:1]),
        reads=["och"], writes=["och3", "stP0"], tag="br_sq2")
    s.emit("dve", lambda v: v.tensor_tensor(
        stP0[0:1, 0, 1:2], stP0[0:1, 0, 1:2], stP0[0:1, 1, 0:1], Alu.add),
        reads=["stP0", "och2", "och3"], writes=["stP0"], tag="br_sqs")
    s.emit("dve", lambda v: v.tensor_scalar(
        stP0[0:1, 0, 0:1], stP0[0:1, 0, 0:1], 1.0 / D, None, Alu.mult),
        reads=["stP0"], writes=["stP0"], tag="br_mu")
    s.emit("dve", lambda v: v.tensor_tensor(
        stP0[0:1, 1, 1:2], stP0[0:1, 0, 0:1], stP0[0:1, 0, 0:1], Alu.mult),
        reads=["stP0"], writes=["stP0"], tag="br_mu2")
    s.emit("dve", lambda v: v.tensor_scalar(
        stP0[0:1, 0, 1:2], stP0[0:1, 0, 1:2], 1.0 / D, None, Alu.mult),
        reads=["stP0"], writes=["stP0"], tag="br_v1")
    s.emit("dve", lambda v: v.tensor_tensor(
        stP0[0:1, 0, 1:2], stP0[0:1, 0, 1:2], stP0[0:1, 1, 1:2], Alu.subtract),
        reads=["stP0"], writes=["stP0"], tag="br_v2")
    s.emit("dve", lambda v: v.tensor_scalar(
        stP0[0:1, 0, 1:2], stP0[0:1, 0, 1:2], EPS_LN, None, Alu.add),
        reads=["stP0"], writes=["stP0"], tag="br_v3")
    s.emit("act", lambda a: a.activation(
        stP0[0:1, 1, 0:1], stP0[0:1, 0, 1:2], Act.Sqrt),
        reads=["stP0"], writes=["stP0"], tag="br_sd")
    s.emit("dve", lambda v: v.reciprocal(stP0[0:1, 1, 0:1], stP0[0:1, 1, 0:1]),
           reads=["stP0"], writes=["stP0"], tag="br_rs")
    s.emit("dve", lambda v: v.tensor_scalar(
        stP0[0:1, 1, 1:2], stP0[0:1, 0, 0:1], stP0[0:1, 1, 0:1], -1.0, Alu.mult, Alu.mult),
        reads=["stP0"], writes=["stP0"], tag="br_nm")
    s.emit("pe", lambda t: t.matmul(
        sp0[:, 0:2], rowbuf[0:1, 1, 0:128], stP0[0:1, 1, 0:2],
        start=True, stop=True),
        reads=["ones1", "stP0", "sp0"], writes=["sp0"], tag="br_bc")
    s.emit("dve", lambda v: v.tensor_scalar(
        bnd1.rearrange("p a b -> p (a b)"),
        stO6.rearrange("p a b -> p (a b)")[:, 0:8],
        sp0[:, 0:1], sp0[:, 1:2], Alu.mult, Alu.add),
        reads=["stO6", "sp0", "hmask"], writes=["bnd1"], tag="br_ap")

    def ln_stats_tt(get_src, src_names, G, tt, tag):
        b6, b6n, a2, a2n, w, wn, rr, rrn, nn_, nnn = G
        for g in range(2):
            s.emit("dve", lambda v, tt=tt, g=g: v.bn_stats(
                b6[:, tt * 2 + g, :], get_src(tt, g)),
                reads=src_names, writes=[f"{b6n}.{tt}"], tag=f"{tag}_bs{tt}.{g}")
        s.emit("dve", lambda v, tt=tt: v.bn_aggr(
            w[:, tt, 0:2],
            b6[:, tt * 2:(tt + 1) * 2, :].rearrange("p a b -> p (a b)")),
            reads=[f"{b6n}.{tt}"], writes=[f"{wn}.{tt}"], tag=f"{tag}_ag{tt}")

    def ln_fin(G, tt, tag):
        b6, b6n, a2, a2n, w, wn, rr, rrn, nn_, nnn = G
        s.emit("dve", lambda v, tt=tt: v.tensor_scalar(
            w[:, tt, 1:2], w[:, tt, 1:2], EPS_LN, None, Alu.add),
            reads=[f"{wn}.{tt}"], writes=[f"{wn}.{tt}"], tag=f"{tag}_ep{tt}")
        s.emit("act", lambda a, tt=tt: a.activation(
            rr[:, tt:tt + 1], w[:, tt, 1:2], Act.Sqrt),
            reads=[f"{wn}.{tt}"], writes=[f"{rrn}.{tt}"], tag=f"{tag}_sd{tt}")
        s.emit("dve", lambda v, tt=tt: v.reciprocal(rr[:, tt:tt + 1], rr[:, tt:tt + 1]),
               reads=[f"{rrn}.{tt}"], writes=[f"{rrn}.{tt}"], tag=f"{tag}_rs{tt}")
        s.emit("dve", lambda v, tt=tt: v.tensor_scalar(
            nn_[:, tt:tt + 1], w[:, tt, 0:1], rr[:, tt:tt + 1], -1.0,
            Alu.mult, Alu.mult),
            reads=[f"{wn}.{tt}", f"{rrn}.{tt}"], writes=[f"{nnn}.{tt}"],
            tag=f"{tag}_nm{tt}")

    GF = (stFb, "stFb", None, "", stF, "stF", stFr, "stFr", stFn, "stFn")
    GO = (stO6, "stO6", None, "", stOw, "stOw", stOr, "stOr", stOn, "stOn")
    GFb = (stFb2, "stFb2", None, "", stF2, "stF2", stFr2, "stFr2", stFn2, "stFn2")
    GOb = (stO6b, "stO6b", None, "", stOwb, "stOwb", stOrb, "stOrb", stOnb, "stOnb")

    def ln_stats(buf_f32_ap, src_names, nfeat, tag, G=GF):
        for tt in range(2):
            ln_stats_tt(lambda tt, g: buf_f32_ap[:, tt, g * 512:(g + 1) * 512],
                        src_names, G, tt, tag)
            ln_fin(G, tt, tag)

    def ln_apply(dst_ap, dst_name, src_ap, src_names, tag, G=GF):
        b6, b6n, a2, a2n, w, wn, rr, rrn, nn_, nnn = G
        for tt in range(2):
            s.emit("act", lambda a, tt=tt: a.activation(
                dst_ap[:, tt, :], src_ap[:, tt, :], Act.Identity,
                bias=nn_[:, tt:tt + 1], scale=rr[:, tt:tt + 1]),
                reads=list(src_names) + [f"{rrn}.{tt}", f"{nnn}.{tt}"],
                writes=[dst_name], tag=f"{tag}_ap{tt}")

    def transpose_tm_to_fm(dst_store, dst_name, src_ap, src_names, tag):
        for tt in range(2):
            s.emit("sp", lambda a, tt=tt: a.dma_start(
                out=dst_store[:, :, tt * 128:(tt + 1) * 128],
                in_=src_ap[:, tt, :], transpose=True),
                reads=src_names, writes=[dst_name], dma=True, tag=f"{tag}_tr{tt}")

    def mk_shift_delta(d_name, d_ap, src_ap, src_name, bnd_ap, bnd_name, tag):
        s.emit("dve", lambda v: v.tensor_tensor(
            d_ap[:, :, 1:Q], src_ap[:, :, 0:Q - 1], src_ap[:, :, 1:Q], Alu.subtract),
            reads=[src_name], writes=[d_name], tag=f"{tag}_d")
        s.emit("dve", lambda v: v.tensor_tensor(
            d_ap[:, :, 0:1], bnd_ap[:], src_ap[:, :, 0:1], Alu.subtract),
            reads=[src_name, bnd_name], writes=[d_name], tag=f"{tag}_d0")

    mix_rr = [0]
    mix_n = [4]
    def mk_mix(coef_ap, coef_names, base_ap, base_names, d_ap, d_names, tag):
        mt, mn = MIXB[mix_rr[0] % mix_n[0]]
        mix_rr[0] += 1
        for cc in range(8):
            s.emit("dve", lambda v, cc=cc, mt=mt: v.scalar_tensor_tensor(
                mt[:, cc, :], d_ap[:, cc, :], coef_ap[:, cc:cc + 1],
                base_ap[:, cc, :], Alu.mult, Alu.add),
                reads=list(d_names) + list(base_names) + list(coef_names),
                writes=[mn], tag=f"{tag}_{cc}")
        return mt, mn

    def proj_fm(mt, mn, wslot, wname, evac, tag, pps=None):
        if pps is None:
            pps = [(pp0[:, 0:512], "pp0"), (pp1[:, 0:512], "pp1")]
        for dc in range(8):
            p, pname = pps[dc % 2]
            for kc in range(8):
                s.emit("pe", lambda t, dc=dc, kc=kc, p=p, mt=mt: raw_matmul(
                    t, nc, p[:, 0:Q],
                    warena[:, wslot, kc, dc * 128:(dc + 1) * 128],
                    mt[:, kc, :], start=(kc == 0), stop=(kc == 7)),
                    reads=[mn, wname] + ([pname] if kc == 0 else []),
                    writes=[pname] if kc == 0 else [pname + "_acc"],
                    tag=f"{tag}_mm{dc}.{kc}")
            evac(dc, p, pname)

    def proj_tm2(stat_ap, stat_names, wslot, wname, evac, tag, pps=None):
        if pps is None:
            pps = [(pp0[:, 0:512], "pp0"), (pp1[:, 0:512], "pp1")]
        for tt in range(2):
            for kc in range(8):
                for dh in range(2):
                    p, pname = pps[dh]
                    s.emit("pe", lambda t, tt=tt, dh=dh, kc=kc, p=p: raw_matmul(
                        t, nc, p[:, 0:512],
                        stat_ap[:, kc, tt * 128:(tt + 1) * 128],
                        warena[:, wslot, kc, dh * 512:(dh + 1) * 512],
                        start=(kc == 0), stop=(kc == 7), ldw=(dh == 0)),
                        reads=list(stat_names) + [wname] + ([pname] if kc == 0 else []),
                        writes=[pname] if kc == 0 else [pname + "_acc"],
                        tag=f"{tag}_mm{tt}.{dh}.{kc}")
            evac(tt)

    P2_PE_GUARD = [0]
    FCTX = {}
    TCTX = {}
    def attn_chunk(src_dram, c, mode, tag, phase=None):
        full = (mode == "full")
        tg = f"{tag}{c}"
        xch = xch_b[:, c % 2]
        xchn = f"xch{c % 2}"
        def st(name):
            return STAGES is None or name in STAGES
        if full and phase == "body":
            C = FCTX.pop(c)
            xch, xchn = C["xch"], C["xchn"]
            ln1x_fm_c, ln1x_fm_cn = C["l1"], C["l1n"]
            dfm_c, dfm_cn = C["d"], C["dn"]
            k_fm_c, k_fm_cn = C["k"], C["kn"]
            v_tm_c, v_tm_cn = C["v"], C["vn"]
            kwkT_c, kwkT_cn = C["kw"], C["kwn"]
            r_c, r_cn = C["r"], C["rn"]
        elif full:
            if c % 2 == 0:
                f_l1, f_l1n = ln1x_fm[:], "ln1x_fm"
                f_d, f_dn = dfm[:], "dfm"
                f_k, f_kn = k_fm[:], "k_fm"
                f_v, f_vn = v_tm[:], "v_tm"
                f_kw, f_kwn = kwkT[:], "kwkT"
            else:
                f_l1, f_l1n = ln1x_fmB, "ln1x_fmB"
                f_d, f_dn = dfmB, "dfmB"
                f_k, f_kn = k_fmB, "k_fmB"
                f_v, f_vn = v_tmB, "v_tmB"
                f_kw, f_kwn = kwkTB, "kwkTB"
            if phase == "ldx":
                dma("sp", xch, src_dram[c * Q:(c + 1) * Q, :].rearrange("(tt p) d -> p tt d", p=128),
                    [], [xchn], f"{tg}_ldx")
                return
            if phase == "load":
                gw = [("s_pe", P2_PE_GUARD[0])] if (c % 2 == 1 and c <= 2) else []
                s.emit("sp", lambda e, c=c, f_l1=f_l1: e.dma_start(out=f_l1, in_=l1_dram[c]),
                       reads=[], writes=[f_l1n], extra_waits=gw, dma=True, tag=f"{tg}_ldl1")
                s.emit("pool", lambda e, c=c, f_k=f_k: e.dma_start(out=f_k, in_=k_dram[c]),
                       reads=[], writes=[f_kn], extra_waits=gw, dma=True, tag=f"{tg}_ldk")
                s.emit("pool", lambda e, c=c, f_v=f_v: e.dma_start(out=f_v, in_=v_dram[c]),
                       reads=[], writes=[f_vn], extra_waits=gw, dma=True, tag=f"{tg}_ldv")
                s.emit("pool", lambda e, c=c, f_kw=f_kw: e.dma_start(out=f_kw, in_=kw_dram[c]),
                       reads=[], writes=[f_kwn], extra_waits=gw, dma=True, tag=f"{tg}_ldkw")
                return
            bnd_ap = bnd1[:] if c == 0 else bnd_hist[:, :, c - 1:c]
            bnd_nm = "bnd1" if c == 0 else "bnd_hist"
            mk_shift_delta(f_dn, f_d, f_l1, f_l1n, bnd_ap, bnd_nm, tg + "_sh")
            ln1x_fm_c, ln1x_fm_cn = f_l1, f_l1n
            dfm_c, dfm_cn = f_d, f_dn
            k_fm_c, k_fm_cn = f_k, f_kn
            v_tm_c, v_tm_cn = f_v, f_vn
            kwkT_c, kwkT_cn = f_kw, f_kwn
            if c % 2 == 0:
                r_c, r_cn = r_fm[:], "r_fm"
            else:
                r_c, r_cn = r_fmB, "r_fmB"
            mr, mrn = mk_mix(colA[:, :, 2], ["colA"], ln1x_fm_c, [ln1x_fm_cn], dfm_c, [dfm_cn], tg + "_mxr")
            def evac_r(dc, p, pname):
                s.emit("act", lambda a, dc=dc, p=p: a.activation(
                    r_c[:, dc, :], p[:, 0:Q], Act.Identity),
                    reads=[pname, pname + "_acc"], writes=[r_cn], tag=f"{tg}_evr{dc}")
            proj_fm(mr, mrn, SLOT["wrT"], WBUF["wrT"], evac_r, tg + "_pr")
            if phase == "front":
                FCTX[c] = dict(xch=xch, xchn=xchn, l1=ln1x_fm_c, l1n=ln1x_fm_cn,
                               d=dfm_c, dn=dfm_cn, k=k_fm_c, kn=k_fm_cn,
                               v=v_tm_c, vn=v_tm_cn, kw=kwkT_c, kwn=kwkT_cn,
                               r=r_c, rn=r_cn)
                return
        else:
            dma("sp", xch, src_dram[c * Q:(c + 1) * Q, :].rearrange("(tt p) d -> p tt d", p=128),
                [], [xchn], f"{tg}_ldx")
            ln_stats(xch, [xchn], D, tg + "_ln1")
            ln_apply(ln1x_tm, "ln1x_tm", xch, [xchn], tg + "_ln1")
            if not st("tr"):
                return
            transpose_tm_to_fm(ln1x_fm, "ln1x_fm", ln1x_tm, ["ln1x_tm"], tg + "_l1t")
            mk_shift_delta("dfm", dfm, ln1x_fm, "ln1x_fm",
                           bnd1[:] if c == 0 else bnd_hist[:, :, c - 1:c],
                           "bnd1" if c == 0 else "bnd_hist", tg + "_sh")
            s.emit("dve", lambda v, c=c: v.tensor_copy(
                bnd_hist[:, :, c:c + 1], ln1x_fm[:, :, Q - 1:Q]),
                reads=["ln1x_fm"], writes=["bnd_hist"], tag=f"{tg}_bnd1")
            dma("pool", l1_dram[c], ln1x_fm[:], ["ln1x_fm"], ["l1_dram"], f"{tg}_spl1")
            ln1x_fm_c, ln1x_fm_cn = ln1x_fm[:], "ln1x_fm"
            dfm_c, dfm_cn = dfm[:], "dfm"
            k_fm_c, k_fm_cn = k_fm[:], "k_fm"
            v_tm_c, v_tm_cn = v_tm[:], "v_tm"
            kwkT_c, kwkT_cn = kwkT[:], "kwkT"

        x1ch = xch
        x1n = xchn
        if not full:
            mk, mkn = mk_mix(colA[:, :, 0], ["colA"], ln1x_fm, ["ln1x_fm"], dfm, ["dfm"], tg + "_mxk")
            def evac_k(dc, p, pname):
                s.emit("act", lambda a, dc=dc, p=p: a.activation(
                    k_fm[:, dc, :], p[:, 0:Q], Act.Identity),
                    reads=[pname, pname + "_acc"], writes=["k_fm"], tag=f"{tg}_evk{dc}")
                s.emit("dve", lambda v, dc=dc, p=p: v.tensor_tensor(
                    kwk_fm[:, dc, :], p[:, 0:Q], mk_fm[:, dc, :], Alu.mult),
                    reads=[pname, pname + "_acc", "mk_fm"], writes=["kwk_fm"], tag=f"{tg}_evkw{dc}")
            proj_fm(mk, mkn, SLOT["wkT"], WBUF["wkT"], evac_k, tg + "_pk")
            s.emit("sp", lambda a: a.dma_start(
                out=kwkT[:],
                in_=kwk_fm.rearrange("p cc t -> p (cc t)"), transpose=True),
                reads=["kwk_fm"], writes=["kwkT"], dma=True, tag=f"{tg}_kwt")
            mv, mvn = mk_mix(colA[:, :, 1], ["colA"], ln1x_fm, ["ln1x_fm"], dfm, ["dfm"], tg + "_mxv")
            def evac_v(tt):
                for dh in range(2):
                    if dh == 0:
                        s.emit("dve", lambda v, tt=tt, dh=dh: v.tensor_copy(
                            v_tm[:, tt, dh * 512:(dh + 1) * 512], PP[dh][:, 0:512]),
                            reads=[f"pp{dh}", f"pp{dh}_acc"], writes=["v_tm"], tag=f"{tg}_evv{tt}.{dh}")
                    else:
                        s.emit("act", lambda a, tt=tt, dh=dh: a.activation(
                            v_tm[:, tt, dh * 512:(dh + 1) * 512], PP[dh][:, 0:512], Act.Identity),
                            reads=[f"pp{dh}", f"pp{dh}_acc"], writes=["v_tm"], tag=f"{tg}_evv{tt}.{dh}")
            proj_tm2(mv, [mvn], SLOT["wvT"], WBUF["wvT"], evac_v, tg + "_pv")
            dma("pool", k_dram[c], k_fm[:], ["k_fm"], ["k_dram"], f"{tg}_spk")
            dma("pool", kw_dram[c], kwkT[:], ["kwkT"], ["kw_dram"], f"{tg}_spkw")
            dma("pool", v_dram[c], v_tm[:], ["v_tm"], ["v_dram"], f"{tg}_spv")

        if full and st("rg"):
            mg, mgn = mk_mix(colA[:, :, 3], ["colA"], ln1x_fm_c, [ln1x_fm_cn], dfm_c, [dfm_cn], tg + "_mxg")
            s.emit("dve", lambda v, r_c=r_c: v.tensor_tensor(r_wb[:], r_c, m_fm[:], Alu.mult),
                   reads=[r_cn, "m_fm"], writes=["r_wb"], tag=f"{tg}_rwb")

            if not st("scan"):
                return
            for hp in range(8):
                e, o = 2 * hp, 2 * hp + 1
                for par, h in ((0, e), (1, o)):
                    r0 = 64 * par
                    spx = [sp0, sp1][par]
                    spn = f"sp{par}"
                    s.emit("pe", lambda t, hp=hp, r0=r0, spx=spx, kf=k_fm_c, rc=r_c: t.matmul(
                        spx[:, 0:256],
                        kf[r0:r0 + 64, hp, 0:128], rc[r0:r0 + 64, hp, :],
                        start=True, stop=True,
                        tile_position=(r0, 0)),
                        reads=[k_fm_cn, r_cn, spn],
                        writes=[spn], tag=f"{tg}_sj0.{h}")
                    s.emit("pe", lambda t, hp=hp, r0=r0, spx=spx, kf=k_fm_c, rc=r_c: t.matmul(
                        spx[:, 256:384],
                        kf[r0:r0 + 64, hp, 128:256], rc[r0:r0 + 64, hp, 128:256],
                        start=True, stop=True,
                        tile_position=(r0, 0)),
                        reads=[k_fm_cn, r_cn],
                        writes=[spn + "_acc"], tag=f"{tg}_sj1.{h}")
                for par, h in ((0, e), (1, o)):
                    spx = [sp0, sp1][par]
                    spn = f"sp{par}"
                    s.emit("dve", lambda v, h=h, spx=spx: v.tensor_tensor(
                        sbar[:, h, 0:256], spx[:, 0:256],
                        wmat[:, h, :], Alu.mult),
                        reads=[spn, spn + "_acc", "wmat"],
                        writes=[f"sbar.{h}", "sbar"], tag=f"{tg}_sb.{h}")
                    s.emit("dve", lambda v, h=h, spx=spx: v.tensor_tensor(
                        sbar[:, h, 256:384], spx[:, 256:384],
                        wmat[:, h, 0:128], Alu.mult),
                        reads=[spn, spn + "_acc", "wmat"],
                        writes=[f"sbar.{h}", "sbar"], tag=f"{tg}_sb1.{h}")

            if not st("y"):
                return
            if c % 2 == 0:
                yn_c, yn_cn = yn_tm, "yn_tm"
            else:
                yn_c, yn_cn = yn_tmB, "mk_fm"
            for it in range(2):
                for h in range(H):
                    hp, par = h // 2, h % 2
                    r0 = 64 * par
                    hc = h * HN
                    if it == 0:
                        s.emit("pe", lambda t, h=h, hc=hc, vt=v_tm_c: t.matmul(
                            yps[:, hc:hc + HN], sbar[:, h, 0:128], vt[:, 0, hc:hc + HN],
                            start=True, stop=False),
                            reads=[f"sbar.{h}", v_tm_cn] + (["yps"] if h == 0 else []),
                            writes=["yps"] if h == 0 else ["yps_acc"],
                            tag=f"{tg}_yi0.{h}")
                    else:
                        s.emit("pe", lambda t, h=h, hc=hc, vt=v_tm_c: t.matmul(
                            yps[:, hc:hc + HN], sbar[:, h, 128:256], vt[:, 0, hc:hc + HN],
                            start=True, stop=False),
                            reads=[f"sbar.{h}", v_tm_cn] + (["yps"] if h == 0 else []),
                            writes=["yps"] if h == 0 else ["yps_acc"],
                            tag=f"{tg}_yi1a.{h}")
                        s.emit("pe", lambda t, h=h, hc=hc, vt=v_tm_c: t.matmul(
                            yps[:, hc:hc + HN], sbar[:, h, 256:384], vt[:, 1, hc:hc + HN],
                            start=False, stop=False),
                            reads=[f"sbar.{h}", v_tm_cn], writes=["yps_acc"],
                            tag=f"{tg}_yi1b.{h}")
                    s.emit("pe", lambda t, h=h, hc=hc, hp=hp, r0=r0, it=it: t.matmul(
                        yps[:, hc:hc + HN],
                        r_wb[r0:r0 + 64, hp, it * 128:(it + 1) * 128],
                        state_b[r0:r0 + 64, hp, :],
                        start=False, stop=True, tile_position=(r0, 0)),
                        reads=["r_wb", "state_b"], writes=["yps_acc"],
                        tag=f"{tg}_yx{it}.{h}")
                s.emit("dve", lambda v: v.tensor_copy(y_sb[:], yps[:]),
                       reads=["yps", "yps_acc"], writes=["y_sb"], tag=f"{tg}_yev{it}")
                s.emit("dve", lambda v: v.tensor_reduce(
                    st_sum[:], y_sb.rearrange("p (h n) -> p h n", h=H), AXX, Alu.add),
                    reads=["y_sb"], writes=["st_sum"], tag=f"{tg}_gns{it}")
                s.emit("act", lambda a: a.activation(
                    sqs[:], y_sb[:], Act.Square),
                    reads=["y_sb"], writes=["sqs"], tag=f"{tg}_gnq{it}")
                s.emit("dve", lambda v: v.tensor_reduce(
                    st_sq[:], sqs.rearrange("p (h n) -> p h n", h=H), AXX, Alu.add),
                    reads=["sqs"], writes=["st_sq"], tag=f"{tg}_gnq2{it}")
                inv = 1.0 / HN
                s.emit("dve", lambda v: v.tensor_scalar(
                    st_mu[:], st_sum[:], inv, None, Alu.mult),
                    reads=["st_sum"], writes=["st_mu"], tag=f"{tg}_gmu{it}")
                s.emit("dve", lambda v: v.tensor_tensor(
                    st_var[:], st_mu[:], st_mu[:], Alu.mult),
                    reads=["st_mu"], writes=["st_var"], tag=f"{tg}_gm2{it}")
                s.emit("dve", lambda v: v.tensor_scalar(
                    st_sq[:], st_sq[:], inv, None, Alu.mult),
                    reads=["st_sq"], writes=["st_sq"], tag=f"{tg}_gq2{it}")
                s.emit("dve", lambda v: v.tensor_tensor(
                    st_var[:], st_sq[:], st_var[:], Alu.subtract),
                    reads=["st_sq", "st_var"], writes=["st_var"], tag=f"{tg}_gvr{it}")
                s.emit("dve", lambda v: v.tensor_scalar(
                    st_var[:], st_var[:], EPS_GN, None, Alu.add),
                    reads=["st_var"], writes=["st_var"], tag=f"{tg}_gep{it}")
                s.emit("act", lambda a: a.activation(
                    st_rstd[:], st_var[:], Act.Sqrt),
                    reads=["st_var"], writes=["st_rstd"], tag=f"{tg}_gsd{it}")
                s.emit("dve", lambda v: v.reciprocal(st_rstd[:], st_rstd[:]),
                       reads=["st_rstd"], writes=["st_rstd"], tag=f"{tg}_grs{it}")
                s.emit("dve", lambda v: v.tensor_tensor(
                    st_nmr[:], st_mu[:], st_rstd[:], Alu.mult),
                    reads=["st_mu", "st_rstd"], writes=["st_nmr"], tag=f"{tg}_gmr{it}")
                s.emit("dve", lambda v: v.tensor_scalar(
                    st_nmr[:], st_nmr[:], -1.0, None, Alu.mult),
                    reads=["st_nmr"], writes=["st_nmr"], tag=f"{tg}_gnm{it}")
                for h in range(H):
                    hc = h * HN
                    s.emit("act", lambda a, h=h, hc=hc, it=it, yn_c=yn_c: a.activation(
                        yn_c[:, it, hc:hc + HN], y_sb[:, hc:hc + HN], Act.Identity,
                        bias=st_nmr[:, h:h + 1], scale=st_rstd[:, h:h + 1]),
                        reads=["y_sb", "st_rstd", "st_nmr"],
                        writes=[yn_cn], tag=f"{tg}_gap{it}.{h}")

        tap(f"{tg}_ln1xfm", ln1x_fm[:], (128, 8, Q), ["ln1x_fm"])
        tap(f"{tg}_kfm", k_fm[:], (128, 8, Q), ["k_fm"])
        tap(f"{tg}_vtm", v_tm[:], (128, 2, D), ["v_tm"])
        tap(f"{tg}_kwkT", kwkT[:], (128, 8, 2, 128), ["kwkT"])
        if not st("su"):
            return
        for hp in range(8):
            suk = SU[hp % 2]
            sname = f"su{hp % 2}"
            for par in range(2):
                h = 2 * hp + par
                for jt in range(2):
                    first = (jt == 0 and par == 0)
                    s.emit("pe", lambda t, hp=hp, jt=jt, par=par, h=h, suk=suk,
                           kw=kwkT_c, vt=v_tm_c: raw_matmul(
                        t, nc, suk[64 * par:64 * par + 64, 0:HN],
                        kw[:, hp, jt, 64 * par:64 * par + 64],
                        vt[:, jt, h * HN:(h + 1) * HN],
                        start=(jt == 0), stop=(jt == 1),
                        tile_position=(0, 64 * par)),
                        reads=[kwkT_cn, v_tm_cn] + ([sname] if first else []),
                        writes=[sname] if first else [sname + "_acc"],
                        tag=f"{tg}_su{hp}.{jt}.{par}")
            s.emit("dve", lambda v, hp=hp, suk=suk: v.scalar_tensor_tensor(
                state_f[:, hp, :], state_f[:, hp, :], ws_rep[:, hp:hp + 1],
                suk[:, 0:HN], Alu.mult, Alu.add),
                reads=["state_f", "ws_rep", sname, sname + "_acc"],
                writes=["state_f"], tag=f"{tg}_stu{hp}")
            s.emit("dve", lambda v, hp=hp: v.tensor_copy(
                state_b[:, hp, :], state_f[:, hp, :]),
                reads=["state_f"], writes=["state_b"], tag=f"{tg}_stc{hp}")

        tap(f"{tg}_state", state_f[:], (128, 8, HN), ["state_f"])
        if full and st("yo"):
            tap(f"{tg}_rfm", r_fm[:], (128, 8, Q), ["r_fm"])
            tap(f"{tg}_sbar", sbar[:], (128, H, 384), ["sbar"])
            TCTX[c] = dict(yn=yn_c, ynn=yn_cn, mg=mg, mgn=mgn, xch=xch, xchn=xchn)

    def attn_tail(c):
        tg = f"A{c}"
        T = TCTX.pop(c)
        yn_c, yn_cn = T["yn"], T["ynn"]
        mg, mgn = T["mg"], T["mgn"]
        xch, xchn = T["xch"], T["xchn"]
        tap(f"{tg}_yn", yn_c[:], (128, 2, D), [yn_cn])
        transpose_tm_to_fm(yn_fm, "yn_fm", yn_c, [yn_cn], tg + "_ynt")
        def evac_g(dc, p, pname):
            s.emit("act", lambda a, dc=dc, p=p: a.activation(
                g_fm[:, dc, :], p[:, 0:Q], Act.Silu),
                reads=[pname, pname + "_acc"], writes=["g_fm"], tag=f"{tg}_evg{dc}")
        proj_fm(mg, mgn, SLOT["wgT"], WBUF["wgT"], evac_g, tg + "_pg")
        tap(f"{tg}_gfm", g_fm[:], (128, 8, Q), ["g_fm"])
        ylt, yln = MIXB[mix_rr[0] % mix_n[0]]
        mix_rr[0] += 1
        for cc in range(8):
            s.emit("act", lambda a, cc=cc, ylt=ylt: a.activation(
                ylt[:, cc, :], yn_fm[:, cc, :], Act.Identity,
                bias=colB[:, cc, 1:2], scale=colB[:, cc, 0:1]),
                reads=["yn_fm", "colB"], writes=[yln], tag=f"{tg}_lnaf{cc}")
        s.emit("dve", lambda v, ylt=ylt: v.tensor_tensor(yg_fm[:], ylt[:], g_fm[:], Alu.mult),
               reads=[yln, "g_fm"], writes=["yg_fm"], tag=f"{tg}_yg")
        tap(f"{tg}_yg", yg_fm[:], (128, 8, Q), ["yg_fm"])
        stage = x1s[:, 0, :]
        def evac_o(tt):
            s.emit("dve", lambda v: v.tensor_copy(stage[:, 0:512], PP[0][:, 0:512]),
                   reads=["pp0", "pp0_acc"], writes=["x1s"], tag=f"{tg}_oev{tt}a")
            s.emit("act", lambda a: a.activation(
                stage[:, 512:1024], PP[1][:, 0:512], Act.Identity),
                reads=["pp1", "pp1_acc"], writes=["x1s"], tag=f"{tg}_oev{tt}b")
            ln_stats_tt(lambda _tt, g: stage[:, g * 512:(g + 1) * 512],
                        ["x1s"], GO, tt, f"{tg}_l2s{tt}")
            ln_fin(GO, tt, f"{tg}_l2f{tt}")
            s.emit("act", lambda a, tt=tt: a.activation(
                stage, stage, Act.Identity,
                bias=stOn[:, tt:tt + 1], scale=stOr[:, tt:tt + 1]),
                reads=["x1s", f"stOr.{tt}", f"stOn.{tt}"], writes=["x1s"],
                tag=f"{tg}_l2ap{tt}")
            s.emit("dve", lambda v, tt=tt, xch=xch: v.tensor_tensor(
                xch[:, tt, :], stage, xch[:, tt, :], Alu.add),
                reads=["x1s", xchn], writes=[xchn], tag=f"{tg}_l2rx{tt}")
        proj_tm2(yg_fm, ["yg_fm"], SLOT["woT"], WBUF["woT"], evac_o, tg + "_po")

        tap(f"{tg}_x1", xch[:], (128, 2, D), [xchn])
        dma("sp", x1_dram[c * Q:(c + 1) * Q, :].rearrange("(tt p) d -> p tt d", p=128),
            xch[:], [xchn], ["x1_dram"], f"{tg}_stx1")

    FCTX3 = {}

    X1SL = [
        (lambda tt: xch_b[:, 0, tt, :], ["xch0"]),
        (lambda tt: xch_b[:, 1, tt, :], ["xch1"]),
        (lambda tt: gv_x1 if tt == 0 else ygv_x1, ["g_fm", "yg_fm"]),
        (lambda tt: rwv_x1 if tt == 0 else mfv_x1, ["r_wb", "m_fm"]),
        (lambda tt: mkv_x1 if tt == 0 else rfv_x1, ["mk_fm", "r_fm"]),
        (lambda tt: vtv_x1 if tt == 0 else kTv_x1, ["v_tm", "kwkT"]),
    ]

    def x1ap(c, tt):
        return X1SL[c % 6][0](tt)

    def x1nm(c):
        return X1SL[c % 6][1]

    def ffn_x1_load(c):
        for tt in range(2):
            dma("sp", x1ap(c, tt), x1_dram[c * Q + tt * 128:c * Q + (tt + 1) * 128, :],
                ["x1_dram"], x1nm(c), f"F{c}_ldx1.{tt}")

    def fctx(c):
        par = c % 2
        d = {}
        if par == 0:
            d["och"], d["ochn"] = och, "och"
            d["l1"], d["l1n"] = ln3x_fm[:], "ln1x_fm"
            d["dm"], d["dmn"] = dfm[:], "dfm"
            d["ka"], d["kan"] = kact_fm[:], "k_fm"
            d["rp"], d["rpn"] = rp_tm[:], "yn_tm"
            d["l3tm"], d["l3tmn"] = ln3x_tm, "ln1x_tm"
            d["GF"], d["GO"] = GF, GO
            d["rpp"] = None
            d["kpp"] = None
        else:
            d["och"], d["ochn"] = x1s, "x1s"
            d["l1"], d["l1n"] = ln1x_fmB, "ln1x_fmB"
            d["dm"], d["dmn"] = dfmB, "dfmB"
            d["ka"], d["kan"] = k_fmB, "k_fmB"
            d["rp"], d["rpn"] = rp_tmB_v, "yn_fm"
            d["l3tm"], d["l3tmn"] = ln3x_tmB, "kwk_fm"
            d["GF"], d["GO"] = GFb, GOb
            d["rpp"] = [(sp0[:, 0:512], "sp0"), (sp1[:, 0:512], "sp1")]
            d["kpp"] = [(su0[:, 0:512], "su0"), (su1[:, 0:512], "su1")]
        return d

    def ffn_front_ln(c):
        tg, C = f"F{c}", fctx(c)
        b6, b6n, a2, a2n, w, wn, rr, rrn, nn_, nnn = C["GF"]
        l3, l3n = C["l3tm"], C["l3tmn"]
        for tt in range(2):
            ln_stats_tt(lambda tt, g: x1ap(c, tt)[:, g * 512:(g + 1) * 512],
                        x1nm(c), C["GF"], tt, tg + "_ln3")
            ln_fin(C["GF"], tt, tg + "_ln3")
            aw = ["ln1x_tm"] if c == 0 else []
            s.emit("act", lambda a, tt=tt: a.activation(
                l3[:, tt, :], x1ap(c, tt), Act.Identity,
                bias=nn_[:, tt:tt + 1], scale=rr[:, tt:tt + 1]),
                reads=x1nm(c) + [f"{rrn}.{tt}", f"{nnn}.{tt}"] + aw,
                writes=[f"{l3n}.{tt}"] + aw, tag=f"{tg}_ln3_ap{tt}")
            s.emit("sp", lambda a, tt=tt: a.dma_start(
                out=C["l1"][:, :, tt * 128:(tt + 1) * 128], in_=l3[:, tt, :],
                transpose=True),
                reads=[f"{l3n}.{tt}"], writes=[C["l1n"]], dma=True,
                tag=f"{tg}_l3t_tr{tt}")

    def ffn_front_shift(c):
        tg, C = f"F{c}", fctx(c)
        mk_shift_delta(C["dmn"], C["dm"], C["l1"], C["l1n"], bnd3, "bnd3", tg + "_sh")
        s.emit("dve", lambda v, f_l1=C["l1"]: v.tensor_copy(bnd3[:], f_l1[:, :, Q - 1:Q]),
               reads=[C["l1n"]], writes=["bnd3"], tag=f"{tg}_bnd3")

    FMIX = {}

    def ffn_front_mix(c):
        tg, C = f"F{c}", fctx(c)
        mr, mrn = mk_mix(colC[:, :, 0], ["colC"], C["l1"], [C["l1n"]],
                         C["dm"], [C["dmn"]], tg + "_mxr")
        mk2, mk2n = mk_mix(colC[:, :, 1], ["colC"], C["l1"], [C["l1n"]],
                           C["dm"], [C["dmn"]], tg + "_mxk")
        FMIX[c] = (mr, mrn, mk2, mk2n)

    def ffn_front_projr(c):
        tg, C = f"F{c}", fctx(c)
        f_rp, f_rpn = C["rp"], C["rpn"]
        rpp = C["rpp"] or [(pp0[:, 0:512], "pp0"), (pp1[:, 0:512], "pp1")]
        mr, mrn = FMIX[c][0], FMIX[c][1]
        def evac_rp(tt):
            for dh in range(2):
                pap, pname = rpp[dh]
                s.emit("act", lambda a, tt=tt, dh=dh, pap=pap, f_rp=f_rp: a.activation(
                    f_rp[:, tt, dh * 512:(dh + 1) * 512], pap, Act.Sigmoid),
                    reads=[pname, pname + "_acc"], writes=[f_rpn],
                    tag=f"{tg}_evr{tt}.{dh}")
        proj_tm2(mr, [mrn], SLOT["fwrT"], WBUF["fwrT"], evac_rp, tg + "_prp",
                 pps=rpp)

    def ffn_front_projk(c):
        tg, C = f"F{c}", fctx(c)
        f_ka, f_kan = C["ka"], C["kan"]
        mk2, mk2n = FMIX.pop(c)[2:4]
        sq = y_sb[:, 0:Q] if c % 2 == 0 else sqs[:, 0:Q]
        sqn = "y_sb" if c % 2 == 0 else "sqs"
        def evac_kp(dc, p, pname):
            s.emit("act", lambda a, dc=dc, p=p: a.activation(
                sq, p[:, 0:Q], Act.Relu),
                reads=[pname, pname + "_acc"], writes=[sqn], tag=f"{tg}_evk{dc}a")
            s.emit("act", lambda a, dc=dc, f_ka=f_ka: a.activation(
                f_ka[:, dc, :], sq, Act.Square),
                reads=[sqn], writes=[f_kan], tag=f"{tg}_evk{dc}b")
        proj_fm(mk2, mk2n, SLOT["fwkT"], WBUF["fwkT"], evac_kp, tg + "_pkp",
                pps=C["kpp"])

    def ffn_front(c):
        ffn_front_ln(c)
        ffn_front_shift(c)
        ffn_front_mix(c)
        ffn_front_projr(c)
        ffn_front_projk(c)

    YPP = [(yps[:, 0:512], "yps"), (yps[:, 512:1024], "yps")]

    def ffn_body_v(c):
        tg, C = f"F{c}", fctx(c)
        f_och, f_ochn = C["och"], C["ochn"]
        f_rp, f_rpn = C["rp"], C["rpn"]
        vpp = YPP if c % 2 == 0 else [(pp0[:, 0:512], "pp0"), (pp1[:, 0:512], "pp1")]
        def evac_vp(tt):
            for dh in range(2):
                pap, pname = vpp[dh]
                s.emit("dve", lambda v, tt=tt, dh=dh, f_rp=f_rp, pap=pap,
                       f_och=f_och: v.tensor_tensor(
                    f_och[:, tt, dh * 512:(dh + 1) * 512], pap,
                    f_rp[:, tt, dh * 512:(dh + 1) * 512], Alu.mult),
                    reads=[pname, pname + "_acc", f_rpn], writes=[f_ochn],
                    tag=f"{tg}_gate{tt}.{dh}")
        proj_tm2(C["ka"], [C["kan"]], SLOT["fwvT"], WBUF["fwvT"], evac_vp,
                 tg + "_pvp", pps=vpp)
        tap(f"{tg}_kact", C["ka"], (128, 8, Q), [C["kan"]])

    def ffn_body_ln4(c):
        tg, C = f"F{c}", fctx(c)
        f_och, f_ochn = C["och"], C["ochn"]
        tap(f"{tg}_preln4", f_och[:], (128, 2, D), [f_ochn])
        for tt in range(2):
            ln_stats_tt(lambda tt, g: f_och[:, tt, g * 512:(g + 1) * 512],
                        [f_ochn], C["GO"], tt, f"{tg}_l4s{tt}")
            ln_fin(C["GO"], tt, f"{tg}_l4f{tt}")

    def ffn_body_fin(c):
        tg, C = f"F{c}", fctx(c)
        f_och, f_ochn = C["och"], C["ochn"]
        _, _, _, _, _, _, rr, rrn, nn_, nnn = C["GO"]
        for tt in range(2):
            s.emit("act", lambda a, tt=tt, f_och=f_och: a.activation(
                f_och[:, tt, :], f_och[:, tt, :], Act.Identity,
                bias=nn_[:, tt:tt + 1], scale=rr[:, tt:tt + 1]),
                reads=[f_ochn, f"{rrn}.{tt}", f"{nnn}.{tt}"], writes=[f_ochn],
                tag=f"{tg}_l4a{tt}")
            s.emit("dve", lambda v, tt=tt, f_och=f_och: v.tensor_tensor(
                f_och[:, tt, :], f_och[:, tt, :], x1ap(c, tt), Alu.add),
                reads=[f_ochn] + x1nm(c), writes=[f_ochn], tag=f"{tg}_l4r{tt}")
        dma("sp", out_d[c * Q:(c + 1) * Q, :].rearrange("(tt p) d -> p tt d", p=128),
            f_och[:], [f_ochn], ["out_d"], f"{tg}_sto")

    def kvctx(c):
        par = c % 2
        d = {}
        d["x"], d["xn"] = xch_b[:, par], f"xch{par}"
        if par == 0:
            d["l1tm"], d["l1tmn"] = ln1x_tm, "ln1x_tm"
            d["l1"], d["l1n"] = ln1x_fm[:], "ln1x_fm"
            d["dm"], d["dmn"] = dfm[:], "dfm"
            d["kf"], d["kfn"] = k_fm[:], "k_fm"
            d["kwf"], d["kwfn"] = kwk_fm[:], "kwk_fm"
            d["kwT"], d["kwTn"] = kwkT[:], "kwkT"
            d["vt"], d["vtn"] = v_tm, "v_tm"
            d["G"] = GF
            d["pps"] = None
        else:
            d["l1tm"], d["l1tmn"] = yn_tm, "yn_tm"
            d["l1"], d["l1n"] = r_fm[:], "r_fm"
            d["dm"], d["dmn"] = g_fm[:], "g_fm"
            d["kf"], d["kfn"] = yg_fm[:], "yg_fm"
            d["kwf"], d["kwfn"] = yn_fm[:], "yn_fm"
            d["kwT"], d["kwTn"] = kwkTB_P1, "r_wb"
            d["vt"], d["vtn"] = v_tmB_P1, "sbar"
            d["G"] = GFb
            d["pps"] = [(sp0[:, 0:512], "sp0"), (sp1[:, 0:512], "sp1")]
        return d

    def kv_ldx(c):
        C = kvctx(c)
        dma("sp", C["x"], xo[c * Q:(c + 1) * Q, :].rearrange("(tt p) d -> p tt d", p=128),
            [], [C["xn"]], f"P{c}_ldx")

    def kv_front(c):
        tg, C = f"P{c}", kvctx(c)
        b6, b6n, a2, a2n, w, wn, rr, rrn, nn_, nnn = C["G"]
        for tt in range(2):
            ln_stats_tt(lambda tt, g: C["x"][:, tt, g * 512:(g + 1) * 512],
                        [C["xn"]], C["G"], tt, tg + "_ln1")
            ln_fin(C["G"], tt, tg + "_ln1")
            s.emit("act", lambda a, tt=tt: a.activation(
                C["l1tm"][:, tt, :], C["x"][:, tt, :], Act.Identity,
                bias=nn_[:, tt:tt + 1], scale=rr[:, tt:tt + 1]),
                reads=[C["xn"], f"{rrn}.{tt}", f"{nnn}.{tt}"],
                writes=[f"{C['l1tmn']}.{tt}"], tag=f"{tg}_l1ap{tt}")
            s.emit("sp", lambda a, tt=tt: a.dma_start(
                out=C["l1"][:, :, tt * 128:(tt + 1) * 128], in_=C["l1tm"][:, tt, :],
                transpose=True),
                reads=[f"{C['l1tmn']}.{tt}"], writes=[C["l1n"]], dma=True,
                tag=f"{tg}_l1t{tt}")
        s.emit("dve", lambda v, c=c: v.tensor_copy(
            bnd_hist[:, :, c:c + 1], C["l1"][:, :, Q - 1:Q]),
            reads=[C["l1n"]], writes=["bnd_hist"], tag=f"{tg}_bnd1")
        dma("pool", l1_dram[c], C["l1"], [C["l1n"]], ["l1_dram"], f"{tg}_spl1")

    KVMIX = {}

    def kv_mix(c):
        tg, C = f"P{c}", kvctx(c)
        bnd_ap = bnd1[:] if c == 0 else bnd_hist[:, :, c - 1:c]
        bnd_nm = "bnd1" if c == 0 else "bnd_hist"
        mk_shift_delta(C["dmn"], C["dm"], C["l1"], C["l1n"], bnd_ap, bnd_nm, tg + "_sh")
        mk = mk_mix(colA[:, :, 0], ["colA"], C["l1"], [C["l1n"]],
                    C["dm"], [C["dmn"]], tg + "_mxk")
        mv = mk_mix(colA[:, :, 1], ["colA"], C["l1"], [C["l1n"]],
                    C["dm"], [C["dmn"]], tg + "_mxv")
        KVMIX[c] = (mk, mv)

    def kv_projk(c):
        tg, C = f"P{c}", kvctx(c)
        (mk, mkn), _ = KVMIX[c]
        kf, kfn, kwf, kwfn = C["kf"], C["kfn"], C["kwf"], C["kwfn"]
        def evac_k(dc, p, pname):
            s.emit("act", lambda a, dc=dc, p=p: a.activation(
                kf[:, dc, :], p[:, 0:Q], Act.Identity),
                reads=[pname, pname + "_acc"], writes=[kfn], tag=f"{tg}_evk{dc}")
            s.emit("dve", lambda v, dc=dc, p=p: v.tensor_tensor(
                kwf[:, dc, :], p[:, 0:Q], mk_fm[:, dc, :], Alu.mult),
                reads=[pname, pname + "_acc", "mk_fm"], writes=[kwfn],
                tag=f"{tg}_evkw{dc}")
        proj_fm(mk, mkn, SLOT["wkT"], WBUF["wkT"], evac_k, tg + "_pk",
                pps=C["pps"])
        s.emit("sp", lambda a: a.dma_start(
            out=C["kwT"], in_=C["kwf"].rearrange("p cc t -> p (cc t)"),
            transpose=True),
            reads=[C["kwfn"]], writes=[C["kwTn"]], dma=True, tag=f"{tg}_kwt")

    def kv_projv(c):
        tg, C = f"P{c}", kvctx(c)
        _, (mv, mvn) = KVMIX.pop(c)
        vt, vtn = C["vt"], C["vtn"]
        vpp = C["pps"] or [(pp0[:, 0:512], "pp0"), (pp1[:, 0:512], "pp1")]
        def evac_v(tt):
            for dh in range(2):
                pap, pname = vpp[dh]
                if dh == 0:
                    s.emit("dve", lambda v, tt=tt, pap=pap: v.tensor_copy(
                        vt[:, tt, 0:512], pap),
                        reads=[pname, pname + "_acc"], writes=[vtn],
                        tag=f"{tg}_evv{tt}.{dh}")
                else:
                    s.emit("act", lambda a, tt=tt, pap=pap: a.activation(
                        vt[:, tt, 512:1024], pap, Act.Identity),
                        reads=[pname, pname + "_acc"], writes=[vtn],
                        tag=f"{tg}_evv{tt}.{dh}")
        proj_tm2(mv, [mvn], SLOT["wvT"], WBUF["wvT"], evac_v, tg + "_pv",
                 pps=vpp)

    def kv_su(c):
        tg, C = f"P{c}", kvctx(c)
        kwT, kwTn, vt, vtn = C["kwT"], C["kwTn"], C["vt"], C["vtn"]
        for hp in range(8):
            suk = SU[hp % 2]
            sname = f"su{hp % 2}"
            for par in range(2):
                h = 2 * hp + par
                for jt in range(2):
                    first = (jt == 0 and par == 0)
                    s.emit("pe", lambda t, hp=hp, jt=jt, par=par, h=h, suk=suk: raw_matmul(
                        t, nc, suk[64 * par:64 * par + 64, 0:HN],
                        kwT[:, hp, jt, 64 * par:64 * par + 64],
                        vt[:, jt, h * HN:(h + 1) * HN],
                        start=(jt == 0), stop=(jt == 1),
                        tile_position=(0, 64 * par)),
                        reads=[kwTn, vtn] + ([sname] if first else []),
                        writes=[sname] if first else [sname + "_acc"],
                        tag=f"{tg}_su{hp}.{jt}.{par}")
            s.emit("dve", lambda v, hp=hp, suk=suk: v.scalar_tensor_tensor(
                state_f[:, hp, :], state_f[:, hp, :], ws_rep[:, hp:hp + 1],
                suk[:, 0:HN], Alu.mult, Alu.add),
                reads=["state_f", "ws_rep", sname, sname + "_acc"],
                writes=["state_f"], tag=f"{tg}_stu{hp}")

    def kv_spill(c):
        tg, C = f"P{c}", kvctx(c)
        dma("pool", k_dram[c], C["kf"], [C["kfn"]], ["k_dram"], f"{tg}_spk")
        dma("pool", kw_dram[c], C["kwT"], [C["kwTn"]], ["kw_dram"], f"{tg}_spkw")
        dma("pool", v_dram[c], C["vt"][:], [C["vtn"]], ["v_dram"], f"{tg}_spv")

    n_p1, n_p2, n_f = LIMITS if LIMITS is not None else (NCHUNK, NCHUNK, NCHUNK)
    if n_p1 > 0:
        kv_ldx(0)
        if n_p1 > 1:
            kv_ldx(1)
        kv_front(0)
        if n_p1 > 1:
            kv_front(1)
    for p in range(0, n_p1, 2):
        a, b = p, p + 1
        hb = b < n_p1
        na, nb = a + 2, b + 2
        if na < n_p1:
            kv_ldx(na)
        if hb and nb < n_p1:
            kv_ldx(nb)
        kv_mix(a)
        if hb:
            kv_mix(b)
        kv_projk(a)
        kv_projv(a)
        if hb:
            kv_projk(b)
            kv_projv(b)
        if na < n_p1:
            kv_front(na)
        if hb and nb < n_p1:
            kv_front(nb)
        kv_su(a)
        kv_spill(a)
        if hb:
            kv_su(b)
            kv_spill(b)
    tap("state_p1", state_f[:], (128, 8, HN), ["state_f"])
    P2_PE_GUARD[0] = s.counts.get("s_pe", 0)
    s.emit("dve", lambda v: v.memset(mixB2[:, 0:1, 0:1], 0.0),
           writes=["mix2"], extra_waits=[("s_pe", P2_PE_GUARD[0])], tag="ulk_m2")
    s.emit("dve", lambda v: v.memset(mixB3[:, 0:1, 0:1], 0.0),
           writes=["mix3"], extra_waits=[("s_pe", P2_PE_GUARD[0])], tag="ulk_m3")
    MIXB[2] = (mixB2, "mix2")
    MIXB[3] = (mixB3, "mix3")
    if n_p1 == NCHUNK:
        dma("pool", my1_dram.ap(),
            state_f.rearrange("p a b -> p (a b)"), ["state_f"], ["my1"], "ex1_st")
        s.emit("pool", lambda g: g.collective_compute(
            "AllGather", Alu.bypass, replica_groups=[[0, 1, 2, 3, 4, 5, 6, 7]],
            ins=[my1_dram.ap()], outs=[g1_dram.ap()]),
            reads=["my1"], writes=["g1"], tag="ex1_ag")
        prev_r = [None]
        def ex1_ld(g):
            pid = (g.partition_id() + 7) % 8
            return g.dma_start(
                out=state_f.rearrange("p a b -> p (a b)"),
                in_=g1_dram.ap().rearrange("r p c -> p r c")[:, ds(pid, 1), 0:512].squeeze(1))
        s.emit("pool", ex1_ld, reads=["g1"], writes=["state_f"], dma=True, tag="ex1_ld")
        s.emit("dve", lambda v: v.tensor_scalar(
            state_f.rearrange("p a b -> p (a b)"), state_f.rearrange("p a b -> p (a b)"),
            hmask[:, 0:1], None, Alu.mult),
            reads=["state_f", "hmask"], writes=["state_f"], tag="ex1_msk")
        for hp in range(8):
            s.emit("dve", lambda v, hp=hp: v.tensor_copy(
                state_b[:, hp, :], state_f[:, hp, :]),
                reads=["state_f"], writes=["state_b"], tag=f"ex1_cb{hp}")
    if n_p2 > 0:
        attn_chunk(xo, 0, mode="full", tag="A", phase="load")
        attn_chunk(xo, 0, mode="full", tag="A", phase="ldx")
        if n_p2 > 1:
            attn_chunk(xo, 1, mode="full", tag="A", phase="ldx")
        attn_chunk(xo, 0, mode="full", tag="A", phase="front")
    for c in range(n_p2):
        if c + 1 < n_p2:
            attn_chunk(xo, c + 1, mode="full", tag="A", phase="load")
        attn_chunk(xo, c, mode="full", tag="A", phase="body")
        if c >= 1:
            attn_tail(c - 1)
            if c + 1 < n_p2:
                attn_chunk(xo, c + 1, mode="full", tag="A", phase="ldx")
        if c + 1 < n_p2:
            attn_chunk(xo, c + 1, mode="full", tag="A", phase="front")
    if n_p2 > 0:
        attn_tail(n_p2 - 1)
    tap("state_p2", state_f[:], (128, 8, HN), ["state_f"])
    if n_p2 == NCHUNK:
        ln_stats(xch_b[:, 1], ["xch1"], D, "l3p")
        ln_apply(ln3x_tm, "ln1x_tm", xch_b[:, 1], ["xch1"], "l3p")
        transpose_tm_to_fm(ln3x_fm, "ln1x_fm", ln3x_tm, ["ln1x_tm"], "l3pt")
        dma("pool", my2_dram.ap(), ln3x_fm[:, :, Q - 1], ["ln1x_fm"], ["my2"], "ex2_bn", slow=True)
        s.emit("pool", lambda g: g.collective_compute(
            "AllGather", Alu.bypass, replica_groups=[[0, 1, 2, 3, 4, 5, 6, 7]],
            ins=[my2_dram.ap()], outs=[g2_dram.ap()]),
            reads=["my2"], writes=["g2"], tag="ex2_ag")
        def ex2_lb(g):
            pid = (g.partition_id() + 7) % 8
            return g.dma_start(
                out=stO6.rearrange("p a b -> p (a b)")[:, 0:8],
                in_=g2_dram.ap().rearrange("r p c -> p r c")[:, ds(pid, 1), :].squeeze(1))
        s.emit("pool", ex2_lb, reads=["g2"], writes=["stO6"], dma=True, tag="ex2_lb")
        s.emit("dve", lambda v: v.tensor_scalar(
            bnd3.rearrange("p a b -> p (a b)"), stO6.rearrange("p a b -> p (a b)")[:, 0:8],
            hmask[:, 0:1], None, Alu.mult),
            reads=["stO6", "hmask"], writes=["bnd3"], tag="ex2_mb")
    if n_f > 0:
        for nm in ["fwrT", "fwkT", "fwvT"]:
            sl = SLOT[nm]
            dma("pool", warena[:, sl, :, :],
                w_dram[nm].rearrange("(cc p) d -> p cc d", p=128),
                [], [WBUF[nm]], f"ldw_{nm}")
    if n_f > 0:
        for c in range(min(4, n_f)):
            ffn_x1_load(c)
        ffn_front(0)
        if n_f > 1:
            ffn_front(1)
    for p in range(0, n_f, 2):
        a, b = p, p + 1
        hb = b < n_f
        na, nb = a + 2, b + 2
        if a + 4 < n_f:
            ffn_x1_load(a + 4)
        if hb and b + 4 < n_f:
            ffn_x1_load(b + 4)
        if na < n_f:
            ffn_front_ln(na)
        if hb and nb < n_f:
            ffn_front_ln(nb)
        ffn_body_v(a)
        if hb:
            ffn_body_v(b)
        if na < n_f:
            ffn_front_shift(na)
        if hb and nb < n_f:
            ffn_front_shift(nb)
        if na < n_f:
            ffn_front_mix(na)
        if hb and nb < n_f:
            ffn_front_mix(nb)
        if na < n_f:
            ffn_front_projr(na)
        if hb and nb < n_f:
            ffn_front_projr(nb)
        if na < n_f:
            ffn_front_projk(na)
        if hb and nb < n_f:
            ffn_front_projk(nb)
        ffn_body_ln4(a)
        if hb:
            ffn_body_ln4(b)
        ffn_body_fin(a)
        if hb:
            ffn_body_fin(b)

    fin = s.barrier()
    s.emit("sp", lambda sp: sp.nop(), extra_waits=fin, tag="final_wait")
    s.build()
    return nc


def _prep_core_inputs(inputs, b, h):
    x = inputs["L0_attn_x"]
    d = {}
    d["xo"] = np.ascontiguousarray(x[b, h * 4096:(h + 1) * 4096, :])
    d["hmask"] = np.full((128, 1), 1.0 if h == 1 else 0.0, np.float32)
    if h == 1:
        row = np.asarray(x[b, 4095, :], np.float32)
    else:
        row = np.zeros(D, np.float32)
    d["xprow"] = np.ascontiguousarray(row[None, :])
    d["xpcol"] = np.ascontiguousarray(row.reshape(8, 128).T)
    for nm, key in [("wrT", "L0_attn_w_r"), ("wkT", "L0_attn_w_k"),
                    ("wvT", "L0_attn_w_v"), ("wgT", "L0_attn_w_g"),
                    ("woT", "L0_attn_w_o"), ("fwrT", "L0_ffn_W_r"),
                    ("fwkT", "L0_ffn_W_k"), ("fwvT", "L0_ffn_W_v")]:
        d[nm] = np.ascontiguousarray(np.asarray(inputs[key]).T)
    def col8(v):
        return np.ascontiguousarray(np.asarray(v).reshape(8, 128).T)
    d["maa"] = np.ascontiguousarray(np.stack(
        [col8(inputs["L0_attn_time_maa_k"]), col8(inputs["L0_attn_time_maa_v"]),
         col8(inputs["L0_attn_time_maa_r"]), col8(inputs["L0_attn_time_maa_g"])],
        axis=2))
    d["lnw"] = np.ascontiguousarray(np.stack(
        [col8(inputs["L0_attn_ln_g"]), col8(inputs["L0_attn_ln_b"])], axis=2))
    d["mus"] = np.ascontiguousarray(np.stack(
        [col8(inputs["L0_ffn_mu_r"]), col8(inputs["L0_ffn_mu_k"])], axis=2))
    dec = np.asarray(inputs["L0_attn_time_decay"])
    u = np.asarray(inputs["L0_attn_time_faaaa"])
    rep = np.zeros((128, 8), np.float32)
    for hp in range(8):
        rep[0:64, hp] = dec[2 * hp]
        rep[64:128, hp] = dec[2 * hp + 1]
    d["dec_rep"] = rep
    d["dec_full"] = np.ascontiguousarray(np.tile(dec[None, :], (128, 1)).astype(np.float32))
    d["u_full"] = np.ascontiguousarray(np.tile(u[None, :], (128, 1)).astype(np.float32))
    return d


def kernel(**inputs):
    nc = build_nc()
    in_maps = []
    for b in range(4):
        for h in range(2):
            in_maps.append(_prep_core_inputs(inputs, b, h))
    res = run_bass_kernel_spmd(nc, in_maps, core_ids=list(range(8)))
    out = np.zeros((4, 8192, D), np.float32)
    i = 0
    for b in range(4):
        for h in range(2):
            out[b, h * 4096:(h + 1) * 4096, :] = res.results[i]["out"]
            i += 1
    return out
